# version 08
# baseline (speedup 1.0000x reference)
import math
import os

import numpy as np

import concourse.bass as bass
import concourse.mybir as mybir
from concourse.bass_utils import run_bass_kernel_spmd

F32 = mybir.dt.float32
F16 = mybir.dt.float16
BF16 = mybir.dt.bfloat16
AF = mybir.ActivationFunctionType
ALU = mybir.AluOpType

D = 2048
INTER = 5632
H = 16
S = 4096
TOK = 2048
TG = 512
NDC = D // 128
NIC = INTER // 128
IBLK = 256
NIB = INTER // IBLK
SCALE = 1.0 / math.sqrt(128.0)
EPS = 1e-5

PHASES = int(os.environ.get("MK_PHASES", "4"))


class Sem:
    __slots__ = ("h", "n")

    def __init__(self, handle):
        self.h = handle
        self.n = 0


class B:
    def __init__(self):
        self.cmds = {e: [] for e in ("t", "v", "s", "g", "y")}
        self.guard = {}

    def op(self, eng, builder, inc=None):
        if inc is None:
            self.cmds[eng].append(lambda e, b=builder: b(e))
            return None
        sem, by = inc
        sem.n += by
        self.cmds[eng].append(lambda e, b=builder, h=sem.h, k=by: b(e).then_inc(h, k))
        return sem.n

    def wait(self, eng, sem, val):
        assert val <= sem.n, f"wait on future value {val} > {sem.n}"
        self.cmds[eng].append(lambda e, h=sem.h, v=val: e.wait_ge(h, v))

    def use(self, eng, key):
        if key in self.guard:
            s, v = self.guard[key]
            self.wait(eng, s, v)

    def rel(self, key, sem, val):
        self.guard[key] = (sem, val)


def build_program():
    nc = bass.Bass("TRN2", target_bir_lowering=False)

    x_in = nc.dram_tensor("x_in", [D, S], F32, kind="ExternalInput").ap()
    gate_t = nc.dram_tensor("gate_t", [D, INTER], F32, kind="ExternalInput").ap()
    up_t = nc.dram_tensor("up_t", [D, INTER], F32, kind="ExternalInput").ap()
    down_t = nc.dram_tensor("down_t", [INTER, D], F32, kind="ExternalInput").ap()
    wqs_t = nc.dram_tensor("wqs_t", [D, 1024], F32, kind="ExternalInput").ap()
    wks_t = nc.dram_tensor("wks_t", [D, 1024], F32, kind="ExternalInput").ap()
    wqg_t = nc.dram_tensor("wqg_t", [D, 1024], F32, kind="ExternalInput").ap()
    wkg_t = nc.dram_tensor("wkg_t", [D, 1024], F32, kind="ExternalInput").ap()
    wv_t = nc.dram_tensor("wv_t", [D, D], F32, kind="ExternalInput").ap()
    wo_t = nc.dram_tensor("wo_t", [D, D], F32, kind="ExternalInput").ap()
    gs_in = nc.dram_tensor("gs_in", [128, NIC], F32, kind="ExternalInput").ap()
    us_in = nc.dram_tensor("us_in", [128, NIC], F32, kind="ExternalInput").ap()
    ds_in = nc.dram_tensor("ds_in", [128, NDC], F32, kind="ExternalInput").ap()
    cos_in = nc.dram_tensor("cos_in", [128, S], F32, kind="ExternalInput").ap()
    sin_in = nc.dram_tensor("sin_in", [128, S], F32, kind="ExternalInput").ap()
    cosl_in = nc.dram_tensor("cosl_in", [128, S], F32, kind="ExternalInput").ap()
    sinl_in = nc.dram_tensor("sinl_in", [128, S], F32, kind="ExternalInput").ap()
    m0_in = nc.dram_tensor("m0_in", [128, 128], F32, kind="ExternalInput").ap()
    m1_in = nc.dram_tensor("m1_in", [128, 128], F32, kind="ExternalInput").ap()

    x_out = nc.dram_tensor("x_out", [D, TOK], F32, kind="ExternalOutput").ap()

    gate_bf = nc.dram_tensor("gate_bf", [NIB, D, IBLK], BF16, kind="Internal").ap()
    up_bf = nc.dram_tensor("up_bf", [NIB, D, IBLK], BF16, kind="Internal").ap()
    down_bf = nc.dram_tensor("down_bf", [NDC, INTER, 128], BF16, kind="Internal").ap()
    wqs_bf = nc.dram_tensor("wqs_bf", [8, D, 128], BF16, kind="Internal").ap()
    wks_bf = nc.dram_tensor("wks_bf", [8, D, 128], BF16, kind="Internal").ap()
    wqg_bf = nc.dram_tensor("wqg_bf", [8, D, 128], BF16, kind="Internal").ap()
    wkg_bf = nc.dram_tensor("wkg_bf", [8, D, 128], BF16, kind="Internal").ap()
    wv_bf = nc.dram_tensor("wv_bf", [4, D, 512], BF16, kind="Internal").ap()
    wo_bf = nc.dram_tensor("wo_bf", [NDC, D, 128], BF16, kind="Internal").ap()
    cos_bf = nc.dram_tensor("cos_bf", [128, S], BF16, kind="Internal").ap()
    sin_bf = nc.dram_tensor("sin_bf", [128, S], BF16, kind="Internal").ap()
    k_loc = nc.dram_tensor("k_loc", [D, S], BF16, kind="Internal").ap()
    v_loc = nc.dram_tensor("v_loc", [S, D], BF16, kind="Internal").ap()
    q_loc = nc.dram_tensor("q_loc", [D, TOK], BF16, kind="Internal").ap()

    k_loc3 = k_loc.rearrange("(h r) t -> h r t", r=128)
    q_loc3 = q_loc.rearrange("(h r) t -> h r t", r=128)
    v_loc3 = v_loc.rearrange("(kt p) dv -> p kt dv", p=128)

    from contextlib import ExitStack
    with ExitStack() as top:
        E = top.enter_context
        p = B()

        sem_names = ("pe", "dve", "act", "w0", "w1", "u0", "u1", "cast", "xin",
                     "con", "rt", "tb0", "ks0", "ks1", "ks2", "ks3", "h0", "h1",
                     "out")
        sm = {n: Sem(E(nc.semaphore(n))) for n in sem_names}
        pe, dve, act = sm["pe"], sm["dve"], sm["act"]

        x_f = E(nc.sbuf_tensor("x_f", [128, NDC * TOK], F16))
        wbig = E(nc.sbuf_tensor("wbig", [128, 16384], BF16))
        gs_sb = E(nc.sbuf_tensor("gs_sb", [128, NIC], F32))
        us_sb = E(nc.sbuf_tensor("us_sb", [128, NIC], F32))
        ds_sb = E(nc.sbuf_tensor("ds_sb", [128, NDC], F32))
        dsh_sb = E(nc.sbuf_tensor("dsh_sb", [128, NDC], F32))
        m0_sb = E(nc.sbuf_tensor("m0_sb", [128, 128], BF16))
        m1_sb = E(nc.sbuf_tensor("m1_sb", [128, 128], BF16))
        ones_bf = E(nc.sbuf_tensor("ones_bf", [128, 1], BF16))
        ones_f16 = E(nc.sbuf_tensor("ones_f16", [128, 1], F16))
        ones_r = E(nc.sbuf_tensor("ones_r", [1, 128], F32))
        ones_f32 = E(nc.sbuf_tensor("ones_f32", [128, 1], F32))
        rw1 = E(nc.sbuf_tensor("rw1", [128, TG], F32))
        rw2 = E(nc.sbuf_tensor("rw2", [128, TG], F32))
        rw3 = E(nc.sbuf_tensor("rw3", [128, TG], F32))
        rw4 = E(nc.sbuf_tensor("rw4", [128, TG], F32))
        rw5 = E(nc.sbuf_tensor("rw5", [128, TG], F32))
        rows = {
            "mean": rw1[0:1, :], "sx": rw1[32:33, :], "sq": rw1[64:65, :],
            "e2": rw2[0:1, :], "var": rw2[32:33, :], "inv": rw2[64:65, :],
            "msq": rw3[0:1, :],
            "rstd": rw4[0:1, :],
            "mr": rw5[0:1, :], "l": rw5[32:33, :],
            "li": rw4[0:1, :],
        }
        t3b = E(nc.sbuf_tensor("t3b", [128, TG], BF16))
        lnb = {}

        wb = [wbig[:, 0:8192], wbig[:, 8192:16384]]
        pslq = [wbig[:, 0:2048], wbig[:, 2048:4096],
                wbig[:, 8192:10240], wbig[:, 10240:12288]]
        wstate = {"mode": "half"}


        pA = E(nc.psum_tensor("pA", [128, 512], F32))
        pB = E(nc.psum_tensor("pB", [128, 512], F32))
        pC = E(nc.psum_tensor("pC", [128, 512], F32))
        pD = E(nc.psum_tensor("pD", [128, 512], F32))
        mga = E(nc.psum_tensor("mga", [128, 1024], F32))
        mgb = E(nc.psum_tensor("mgb", [128, 1024], F32))
        pgu = [pA, pB, pC, pD]
        ppo = [pA, pC]

        con, xin, cast = sm["con"], sm["xin"], sm["cast"]
        for dst, src in ((gs_sb, gs_in), (us_sb, us_in), (ds_sb, ds_in)):
            p.op("y", lambda e, d=dst, s=src: e.dma_start(out=d[:, :], in_=s),
                 inc=(con, 16))
        v_con = con.n
        for dc in range(NDC):
            p.op("g", lambda e, dc=dc: e.dma_start(
                out=x_f[:, dc * TOK:(dc + 1) * TOK],
                in_=x_in[dc * 128:(dc + 1) * 128, 0:TOK]), inc=(xin, 16))
        v_xin = xin.n
        p.wait("g", xin, v_xin)

        def cast_dma(dst, src):
            v = p.op("g", lambda e, d=dst, s=src: e.dma_start(out=d, in_=s),
                     inc=(cast, 16))
            p.wait("g", cast, v)
            return v

        def cast_blocks(dst, src3, nblk):
            out = []
            for ib in range(nblk):
                out.append(cast_dma(dst[ib], src3[ib]))
            return out

        cast_dma(m0_sb[:, :], m0_in)
        cast_dma(m1_sb[:, :], m1_in)
        v_cast = {}
        gate_src = gate_t.rearrange("d (ib i) -> ib d i", i=IBLK)
        up_src = up_t.rearrange("d (ib i) -> ib d i", i=IBLK)
        v_cast["gu_ib"] = []
        for ib in range(NIB):
            cast_dma(gate_bf[ib], gate_src[ib])
            v_cast["gu_ib"].append(cast_dma(up_bf[ib], up_src[ib]))
        v_cast["down_ob"] = cast_blocks(
            down_bf, down_t.rearrange("i (ob o) -> ob i o", o=128), NDC)
        v_cast["gate"] = v_cast["up"] = v_cast["gu_ib"][-1]
        v_cast["down"] = v_cast["down_ob"][-1]
        for nm, dst, src, nblk in (
            ("wks", wks_bf, wks_t.rearrange("d (oc o) -> oc d o", o=128), 8),
            ("wkg", wkg_bf, wkg_t.rearrange("d (oc o) -> oc d o", o=128), 8),
            ("wv", wv_bf, wv_t.rearrange("d (oc o) -> oc d o", o=512), 4),
            ("wqs", wqs_bf, wqs_t.rearrange("d (oc o) -> oc d o", o=128), 8),
            ("wqg", wqg_bf, wqg_t.rearrange("d (oc o) -> oc d o", o=128), 8),
            ("wo", wo_bf, wo_t.rearrange("d (oc o) -> oc d o", o=128), NDC),
        ):
            v_cast[nm] = cast_blocks(dst, src, nblk)[-1]
        with ExitStack() as sc0:
            th = sc0.enter_context(nc.sbuf_tensor("tbl_hi", [128, S], F32))
            tl = sc0.enter_context(nc.sbuf_tensor("tbl_lo", [128, S], F32))
            for dst_bf, hi_src, lo_src in ((cos_bf, cos_in, cosl_in),
                                           (sin_bf, sin_in, sinl_in)):
                v1 = p.op("y", lambda e, s=hi_src: e.dma_start(out=th[:, :], in_=s),
                          inc=(sm["tb0"], 16))
                v2 = p.op("y", lambda e, s=lo_src: e.dma_start(out=tl[:, :], in_=s),
                          inc=(sm["tb0"], 16))
                p.wait("v", sm["tb0"], v2)
                p.use("v", "tbl")
                v3 = p.op("v", lambda e: e.tensor_add(th[:, :], th[:, :], tl[:, :]),
                          inc=(dve, 1))
                p.wait("g", dve, v3)
                v_cast[("tbl", id(dst_bf))] = p.op(
                    "g", lambda e, d=dst_bf: e.dma_start(out=d, in_=th[:, :]),
                    inc=(cast, 16))
                p.wait("g", cast, cast.n)
                p.rel("tbl", cast, cast.n)
                p.wait("y", cast, cast.n)
        v_cast["cos"] = v_cast["sin"] = cast.n
        p.wait("v", cast, cast.n)
        p.wait("s", cast, cast.n)
        p.wait("g", cast, cast.n)

        p.op("v", lambda e: e.memset(mga[:, :], 0.0), inc=(dve, 1))
        p.op("v", lambda e: e.memset(mgb[:, :], 0.0), inc=(dve, 1))
        p.op("v", lambda e: e.memset(ones_bf[:, :], 1.0), inc=(dve, 1))
        p.op("v", lambda e: e.memset(ones_f16[:, :], 1.0), inc=(dve, 1))
        p.op("v", lambda e: e.memset(ones_r[:, :], 1.0), inc=(dve, 1))
        p.op("v", lambda e: e.memset(ones_f32[:, :], 1.0), inc=(dve, 1))
        p.wait("v", con, v_con)
        v_ones = p.op("v", lambda e: e.tensor_scalar_mul(
            dsh_sb[:, :], ds_sb[:, :], 0.5), inc=(dve, 1))

        qsems = [sm["w0"], sm["w1"], sm["u0"], sm["u1"]]

        def wmode(m):
            if wstate["mode"] == m:
                return
            if m == "quarter":
                p.use("y", "wb0")
                p.use("y", "wb1")
            else:
                for j in range(4):
                    p.use("y", f"pw{j}")
            wstate["mode"] = m

        def emit_ln(xc, dst_xn, xn_key="xnrd", x_ready=None, x_chunk_ready=None):
            sqb = lnb["sqb"]
            rstd_bc = lnb["rstd_bc"]
            mr_bc = lnb["mr_bc"]
            xtmp = lnb["xtmp"]
            sxp = mga[0:1, 0:TG]
            sqp = mga[0:1, TG:2 * TG]
            bc1 = mgb[:, 0:TG]
            bc2 = mgb[:, TG:2 * TG]
            p.use("t", "mA")
            p.wait("t", dve, v_ones)
            if x_ready is not None:
                p.wait("t", x_ready[0], x_ready[1])
                p.wait("s", x_ready[0], x_ready[1])
            for dc in range(NDC):
                if x_chunk_ready is not None:
                    p.wait("t", x_chunk_ready[dc][0], x_chunk_ready[dc][1])
                p.op("t", lambda e, dc=dc, sxp=sxp: e.matmul(
                    sxp, ones_f16[:, :], xc(dc),
                    start=(dc == 0), stop=(dc == NDC - 1)))
            p.use("s", "sqb")
            prev_pe = None
            for dc in range(NDC):
                if x_chunk_ready is not None:
                    p.wait("s", x_chunk_ready[dc][0], x_chunk_ready[dc][1])
                if prev_pe is not None:
                    p.wait("s", pe, prev_pe)
                va = p.op("s", lambda e, dc=dc, sqb=sqb: e.activation(
                    sqb[:, :], xc(dc), AF.Square), inc=(act, 1))
                p.wait("t", act, va)
                prev_pe = p.op("t", lambda e, dc=dc, sqb=sqb, sqp=sqp: e.matmul(
                    sqp, ones_f16[:, :], sqb[:, :],
                    start=(dc == 0), stop=(dc == NDC - 1)), inc=(pe, 1))
            p.rel("sqb", pe, prev_pe)
            v_stats = pe.n
            p.use("v", "rowsABC")
            p.wait("v", pe, v_stats)
            p.op("v", lambda e, sxp=sxp: e.tensor_copy(rows["sx"], sxp))
            v_ev = p.op("v", lambda e, sqp=sqp: e.tensor_copy(rows["sq"], sqp),
                        inc=(dve, 1))
            p.rel("mA", dve, v_ev)
            p.wait("s", dve, v_ev)
            p.op("s", lambda e: e.activation(rows["mean"], rows["sx"], AF.Copy,
                                             scale=1.0 / D))
            v_a1 = p.op("s", lambda e: e.activation(rows["e2"], rows["sq"], AF.Copy,
                                                    scale=1.0 / D), inc=(act, 1))
            p.wait("s", act, v_a1)
            v_a2 = p.op("s", lambda e: e.activation(rows["msq"], rows["mean"],
                                                    AF.Square), inc=(act, 1))
            p.wait("v", act, v_a2)
            v_sub = p.op("v", lambda e: e.tensor_sub(
                rows["var"], rows["e2"], rows["msq"]), inc=(dve, 1))
            p.wait("v", dve, v_sub)
            v_d0 = p.op("v", lambda e: e.tensor_scalar_add(
                rows["var"], rows["var"], EPS), inc=(dve, 1))
            p.wait("v", dve, v_d0)
            v_d1 = p.op("v", lambda e: e.reciprocal(rows["inv"], rows["var"]),
                        inc=(dve, 1))
            p.wait("s", dve, v_d1)
            v_a3 = p.op("s", lambda e: e.activation(rows["rstd"], rows["inv"],
                                                    AF.Sqrt), inc=(act, 1))
            p.wait("v", act, v_a3)
            v_d2 = p.op("v", lambda e: e.tensor_mul(
                rows["mr"], rows["mean"], rows["rstd"]), inc=(dve, 1))
            p.use("t", "mB")
            p.wait("t", dve, v_d2)
            p.op("t", lambda e, bc1=bc1: e.matmul(bc1, ones_r[:, :], rows["rstd"],
                                                  start=True, stop=True))
            v_bc = p.op("t", lambda e, bc2=bc2: e.matmul(
                bc2, ones_r[:, :], rows["mr"], start=True, stop=True),
                inc=(pe, 1))
            p.rel("rowsABC", pe, v_bc)
            p.use("s", "bcb")
            p.wait("s", pe, v_bc)
            p.op("s", lambda e, rstd_bc=rstd_bc, bc1=bc1: e.activation(
                rstd_bc[:, :], bc1, AF.Copy))
            v_bce = p.op("s", lambda e, mr_bc=mr_bc, bc2=bc2: e.activation(
                mr_bc[:, :], bc2, AF.Copy), inc=(act, 1))
            p.rel("mB", act, v_bce)
            p.wait("v", act, v_bce)
            p.use("v", xn_key)
            v_last = None
            prev = v_d2
            for dc in range(NDC):
                if prev is not None:
                    p.wait("v", dve, prev)
                v_mul = p.op("v", lambda e, dc=dc, xtmp=xtmp, rstd_bc=rstd_bc:
                     e.tensor_mul(xtmp[:, :], xc(dc), rstd_bc[:, :]), inc=(dve, 1))
                p.wait("v", dve, v_mul)
                v_last = p.op("v", lambda e, dc=dc, xtmp=xtmp, mr_bc=mr_bc:
                              e.tensor_sub(
                    dst_xn[:, dc * TG:(dc + 1) * TG], xtmp[:, :], mr_bc[:, :]),
                    inc=(dve, 1))
                prev = v_last
            p.rel("bcb", dve, v_last)
            return v_last

        def emit_ffn_A(src_xn, v_xn_ready, tiles, first=False, xn_key="xnrd"):
            t1b, t2b, hdn = tiles["t1b"], tiles["t2b"], tiles["hdn"]
            wmode("half")
            last_pe = None
            for ib in range(NIB):
                sl = ib % 2
                wsem = sm["w0"] if sl == 0 else sm["w1"]
                usem = sm["u0"] if sl == 0 else sm["u1"]
                p.use("y", f"wb{sl}")
                if first:
                    p.wait("y", cast, v_cast["gu_ib"][ib])
                vw = p.op("y", lambda e, ib=ib, sl=sl: e.dma_start(
                    out=wb[sl][:, 0:NDC * IBLK],
                    in_=gate_bf[ib].rearrange("(dc p) i -> p dc i", p=128)),
                    inc=(wsem, 16))
                vu = p.op("y", lambda e, ib=ib, sl=sl: e.dma_start(
                    out=wb[sl][:, NDC * IBLK:2 * NDC * IBLK],
                    in_=up_bf[ib].rearrange("(dc p) i -> p dc i", p=128)),
                    inc=(usem, 16))
                p.wait("t", wsem, vw)
                p.wait("t", usem, vu)
                if ib == 0:
                    p.wait("t", dve, v_xn_ready)
                for hh in range(2):
                    ic = ib * 2 + hh
                    bp = ic % 2
                    pg = pgu[2 * bp]
                    pu = pgu[2 * bp + 1]
                    p.use("t", "pA" if bp == 0 else "pC")
                    p.use("t", "pB" if bp == 0 else "pD")
                    for dc in range(NDC):
                        p.op("t", lambda e, sl=sl, dc=dc, hh=hh, pg=pg: e.matmul(
                            pg[:, 0:TG],
                            wb[sl][:, dc * IBLK + hh * 128: dc * IBLK + (hh + 1) * 128],
                            src_xn[:, dc * TG:(dc + 1) * TG],
                            start=(dc == 0), stop=(dc == NDC - 1)))
                    for dc in range(NDC):
                        last_pe = p.op("t", lambda e, sl=sl, dc=dc, hh=hh, pu=pu: e.matmul(
                            pu[:, 0:TG],
                            wb[sl][:, NDC * IBLK + dc * IBLK + hh * 128:
                                    NDC * IBLK + dc * IBLK + (hh + 1) * 128],
                            src_xn[:, dc * TG:(dc + 1) * TG],
                            start=(dc == 0), stop=(dc == NDC - 1)),
                            inc=(pe, 1) if dc == NDC - 1 else None)
                    tp = ic % 2
                    p.use("s", f"t12_{tp}")
                    p.wait("s", pe, last_pe)
                    p.op("s", lambda e, ic=ic, tp=tp, pg=pg: e.activation(
                        t1b[:, tp * TG:(tp + 1) * TG], pg[:, 0:TG], AF.Silu,
                        scale=gs_sb[:, ic:ic + 1]))
                    va = p.op("s", lambda e, ic=ic, tp=tp, pu=pu: e.activation(
                        t2b[:, tp * TG:(tp + 1) * TG], pu[:, 0:TG], AF.Copy,
                        scale=us_sb[:, ic:ic + 1]), inc=(act, 1))
                    p.rel("pA" if bp == 0 else "pC", act, va)
                    p.rel("pB" if bp == 0 else "pD", act, va)
                    p.wait("v", act, va)
                    if ic < 4:
                        p.use("v", f"stg{ic}")
                    elif ic < 8:
                        p.use("v", f"gr{ic - 4}")
                    vh_ = p.op("v", lambda e, ic=ic, tp=tp: e.tensor_mul(
                        hdn[:, ic * TG:(ic + 1) * TG],
                        t1b[:, tp * TG:(tp + 1) * TG],
                        t2b[:, tp * TG:(tp + 1) * TG]), inc=(dve, 1))
                    p.rel(f"t12_{tp}", dve, vh_)
                p.rel(f"wb{sl}", pe, last_pe)
            p.rel(xn_key, pe, last_pe)
            return dve.n

        def emit_ffn_B(v_hdn_done, xdst, tiles, first=False, after_ob=None):
            hdn = tiles["hdn"]
            wmode("half")
            v_res = None
            rlist = []
            for ob in range(NDC):
                sl = ob % 2
                wsem = sm["w0"] if sl == 0 else sm["w1"]
                p.use("y", f"wb{sl}")
                if first:
                    p.wait("y", cast, v_cast["down_ob"][ob])
                vw = p.op("y", lambda e, ob=ob, sl=sl: e.dma_start(
                    out=wb[sl][:, 0:NIC * 128],
                    in_=down_bf[ob].rearrange("(ic p) o -> p ic o", p=128)),
                    inc=(wsem, 16))
                p.wait("t", wsem, vw)
                if ob == 0:
                    p.wait("t", dve, v_hdn_done)
                bp = ob % 2
                po = ppo[bp]
                p.use("t", "pA" if bp == 0 else "pC")
                vmm = None
                for ic in range(NIC):
                    vmm = p.op("t", lambda e, sl=sl, ic=ic, po=po: e.matmul(
                        po[:, 0:TG], wb[sl][:, ic * 128:(ic + 1) * 128],
                        hdn[:, ic * TG:(ic + 1) * TG],
                        start=(ic == 0), stop=(ic == NIC - 1)),
                        inc=(pe, 1) if ic == NIC - 1 else None)
                p.rel(f"wb{sl}", pe, vmm)
                p.use("s", "t3")
                p.wait("s", pe, vmm)
                va = p.op("s", lambda e, ob=ob, po=po: e.activation(
                    t3b[:, :], po[:, 0:TG], AF.Copy,
                    scale=dsh_sb[:, ob:ob + 1]), inc=(act, 1))
                p.rel("pA" if bp == 0 else "pC", act, va)
                p.wait("v", act, va)
                if v_res is not None:
                    p.wait("v", dve, v_res)
                v_res = p.op("v", lambda e, ob=ob: e.tensor_add(
                    xdst(ob), xdst(ob), t3b[:, :]), inc=(dve, 1))
                p.rel("t3", dve, v_res)
                rlist.append(v_res)
                if after_ob is not None:
                    after_ob(ob, v_res)
            return rlist

        def emit_ffn(src_xn, v_xn_ready, xdst, tiles, first=False, xn_key="xnrd"):
            vh = emit_ffn_A(src_xn, v_xn_ready, tiles, first=first, xn_key=xn_key)
            return emit_ffn_B(vh, xdst, tiles, first=first)[-1]

        def load_w(sl, src_ap, ncols, pre_wait=None):
            wsem = sm["w0"] if sl == 0 else sm["w1"]
            p.use("y", f"wb{sl}")
            if pre_wait is not None:
                p.wait("y", cast, pre_wait)
            vw = p.op("y", lambda e, s=src_ap, sl=sl, n=ncols: e.dma_start(
                out=wb[sl][:, 0:n], in_=s), inc=(wsem, 16))
            return wsem, vw

        def proj_mm4(sl, src_xn, v_xn_ready, first_mm, bp):
            pk = ppo[bp]
            p.use("t", "pA" if bp == 0 else "pC")
            if first_mm:
                p.wait("t", dve, v_xn_ready)
            vmm = None
            for dc in range(NDC):
                vmm = p.op("t", lambda e, sl=sl, dc=dc, pk=pk: e.matmul(
                    pk[:, 0:TG], pslq[sl][:, dc * 128:(dc + 1) * 128],
                    src_xn[:, dc * TG:(dc + 1) * TG],
                    start=(dc == 0), stop=(dc == NDC - 1)),
                    inc=(pe, 1) if dc == NDC - 1 else None)
            return pk, vmm

        def proj_mm(sl, src_xn, v_xn_ready, first_mm, bp):
            pk = ppo[bp]
            p.use("t", "pA" if bp == 0 else "pC")
            if first_mm:
                p.wait("t", dve, v_xn_ready)
            vmm = None
            for dc in range(NDC):
                vmm = p.op("t", lambda e, sl=sl, dc=dc, pk=pk: e.matmul(
                    pk[:, 0:TG], wb[sl][:, dc * 128:(dc + 1) * 128],
                    src_xn[:, dc * TG:(dc + 1) * TG],
                    start=(dc == 0), stop=(dc == NDC - 1)),
                    inc=(pe, 1) if dc == NDC - 1 else None)
            return pk, vmm

        def emit_sem_proj(tiles, src_xn, v_xn_ready, w_bf, dst3, tbase,
                          first_w=None):
            stg = tiles["stg"]
            wmode("quarter")
            for oc in range(8):
                sl = oc % 4
                wsem = qsems[sl]
                p.use("y", f"pw{sl}")
                if first_w is not None and oc < 4:
                    p.wait("y", cast, first_w)
                vw = p.op("y", lambda e, oc=oc, sl=sl, wb_=w_bf: e.dma_start(
                    out=pslq[sl][:, :],
                    in_=wb_[oc].rearrange("(dc p) o -> p dc o", p=128)),
                    inc=(wsem, 16))
                p.wait("t", wsem, vw)
                pk, vmm = proj_mm4(sl, src_xn, v_xn_ready, oc == 0, oc % 2)
                p.rel(f"pw{sl}", pe, vmm)
                sp = oc % 4
                ksem = sm[f"ks{sp}"]
                p.use("v", f"stg{sp}")
                p.wait("v", pe, vmm)
                vd = p.op("v", lambda e, sp=sp, pk=pk, stg=stg: e.tensor_copy(
                    stg[:, sp * TG:(sp + 1) * TG], pk[:, 0:TG]), inc=(dve, 1))
                p.rel("pA" if oc % 2 == 0 else "pC", dve, vd)
                p.wait("g", dve, vd)
                vst = p.op("g", lambda e, oc=oc, sp=sp, d3=dst3, tb=tbase, stg=stg:
                           e.dma_start(
                    out=d3[2 * oc:2 * oc + 2, 0:64, tb:tb + TG],
                    in_=stg[:, sp * TG:(sp + 1) * TG]), inc=(ksem, 16))
                p.rel(f"stg{sp}", ksem, vst)
            p.rel("xnrd", pe, pe.n)

        def emit_geo_proj(tiles, src_xn, v_xn_ready, w_bf, dst3, tbase,
                          first_w=None):
            stg, gr = tiles["stg"], tiles["gr"]
            ra = tiles["t1b"][:, 0:TG]
            rb = tiles["t1b"][:, TG:2 * TG]
            csb, snb = lnb["csb"], lnb["snb"]
            wmode("quarter")
            p.use("g", "ropetbl")
            p.use("g", "t12_0")
            p.use("g", "t12_1")
            p.op("g", lambda e, tb=tbase, csb=csb: e.dma_start(
                out=csb[:, :], in_=cos_bf[:, tb:tb + TG]), inc=(sm["rt"], 16))
            v_rt = p.op("g", lambda e, tb=tbase, snb=snb: e.dma_start(
                out=snb[:, :], in_=sin_bf[:, tb:tb + TG]), inc=(sm["rt"], 16))
            order = [0, 4, 1, 5, 2, 6, 3, 7]
            for idx, oc in enumerate(order):
                sl = idx % 4
                wsem = qsems[sl]
                p.use("y", f"pw{sl}")
                if first_w is not None and idx < 4:
                    p.wait("y", cast, first_w)
                vw = p.op("y", lambda e, oc=oc, sl=sl, wb_=w_bf: e.dma_start(
                    out=pslq[sl][:, :],
                    in_=wb_[oc].rearrange("(dc p) o -> p dc o", p=128)),
                    inc=(wsem, 16))
                p.wait("t", wsem, vw)
                pk, vmm = proj_mm4(sl, src_xn, v_xn_ready, idx == 0, idx % 2)
                p.rel(f"pw{sl}", pe, vmm)
                gslot = (idx // 2) % 2 + (2 if oc >= 4 else 0)
                p.use("v", f"gr{gslot}")
                p.wait("v", pe, vmm)
                vd = p.op("v", lambda e, g=gslot, pk=pk, gr=gr: e.tensor_copy(
                    gr[:, g * TG:(g + 1) * TG], pk[:, 0:TG]), inc=(dve, 1))
                p.rel("pA" if idx % 2 == 0 else "pC", dve, vd)
                if oc >= 4:
                    c = oc - 4
                    s1 = (idx // 2) % 2
                    s2 = s1 + 2
                    g1 = gr[:, s1 * TG:(s1 + 1) * TG]
                    g2 = gr[:, s2 * TG:(s2 + 1) * TG]
                    sp = c % 2
                    ksemA = sm[f"ks{2 * sp}"]
                    ksemB = sm[f"ks{2 * sp + 1}"]
                    p.use("v", f"stg{2 * sp}")
                    p.use("v", f"stg{2 * sp + 1}")
                    p.use("v", "t12_0")
                    p.use("v", "t12_1")
                    p.wait("v", dve, vd)
                    p.wait("v", sm["rt"], v_rt)
                    p.op("v", lambda e, g1=g1, csb=csb: e.tensor_mul(ra, g1, csb[:, :]))
                    v1 = p.op("v", lambda e, g2=g2, snb=snb: e.tensor_mul(rb, g2, snb[:, :]),
                              inc=(dve, 1))
                    p.wait("v", dve, v1)
                    v1b = p.op("v", lambda e, sp=sp, stg=stg: e.tensor_sub(
                        stg[:, (2 * sp) * TG:(2 * sp + 1) * TG], ra, rb),
                        inc=(dve, 1))
                    p.wait("v", dve, v1b)
                    p.op("v", lambda e, g1=g1, snb=snb: e.tensor_mul(ra, g1, snb[:, :]))
                    v2 = p.op("v", lambda e, g2=g2, csb=csb: e.tensor_mul(rb, g2, csb[:, :]),
                              inc=(dve, 1))
                    p.wait("v", dve, v2)
                    vro = p.op("v", lambda e, sp=sp, stg=stg: e.tensor_add(
                        stg[:, (2 * sp + 1) * TG:(2 * sp + 2) * TG], ra, rb),
                        inc=(dve, 1))
                    p.rel(f"gr{s1}", dve, vro)
                    p.rel(f"gr{s2}", dve, vro)
                    p.rel("t12_0", dve, vro)
                    p.rel("t12_1", dve, vro)
                    p.rel("ropetbl", dve, vro)
                    p.wait("g", dve, vro)
                    vs1 = p.op("g", lambda e, c=c, sp=sp, d3=dst3, tb=tbase, stg=stg:
                               e.dma_start(
                        out=d3[4 * c:4 * c + 4, 64:96, tb:tb + TG],
                        in_=stg[:, (2 * sp) * TG:(2 * sp + 1) * TG]), inc=(ksemA, 16))
                    vs2 = p.op("g", lambda e, c=c, sp=sp, d3=dst3, tb=tbase, stg=stg:
                               e.dma_start(
                        out=d3[4 * c:4 * c + 4, 96:128, tb:tb + TG],
                        in_=stg[:, (2 * sp + 1) * TG:(2 * sp + 2) * TG]),
                        inc=(ksemB, 16))
                    p.rel(f"stg{2 * sp}", ksemA, vs1)
                    p.rel(f"stg{2 * sp + 1}", ksemB, vs2)
            p.rel("xnrd", pe, pe.n)

        def emit_v_proj(tiles, src_xn, v_xn_ready, tbase, first_w=None):
            stg = tiles["stg"]
            wmode("half")
            for dvc in range(4):
                sl = dvc % 2
                wsem, vw = load_w(sl, wv_bf[dvc].rearrange("(dc p) o -> p dc o", p=128),
                                  NDC * 512, first_w if dvc < 2 else None)
                p.wait("t", wsem, vw)
                if dvc == 0:
                    p.wait("t", dve, v_xn_ready)
                vmm = None
                for tt in range(4):
                    bp = tt % 2
                    pv = ppo[bp]
                    p.use("t", "pA" if bp == 0 else "pC")
                    for dc in range(NDC):
                        vmm = p.op("t", lambda e, sl=sl, dc=dc, tt=tt, pv=pv: e.matmul(
                            pv[:, 0:TG],
                            src_xn[:, dc * TG + tt * 128: dc * TG + (tt + 1) * 128],
                            wb[sl][:, dc * 512:(dc + 1) * 512],
                            start=(dc == 0), stop=(dc == NDC - 1)),
                            inc=(pe, 1) if dc == NDC - 1 else None)
                    sp = tt
                    ksem = sm[f"ks{sp}"]
                    p.use("v", f"stg{sp}")
                    p.wait("v", pe, vmm)
                    vd = p.op("v", lambda e, sp=sp, pv=pv, stg=stg: e.tensor_copy(
                        stg[:, sp * TG:(sp + 1) * TG], pv[:, 0:TG]), inc=(dve, 1))
                    p.rel("pA" if bp == 0 else "pC", dve, vd)
                    p.wait("g", dve, vd)
                    vst = p.op("g", lambda e, dvc=dvc, tt=tt, sp=sp, tb=tbase, stg=stg:
                               e.dma_start(
                        out=v_loc[tb + tt * 128: tb + (tt + 1) * 128,
                                  dvc * 512:(dvc + 1) * 512],
                        in_=stg[:, sp * TG:(sp + 1) * TG]), inc=(ksem, 16))
                    p.rel(f"stg{sp}", ksem, vst)
                p.rel(f"wb{sl}", pe, vmm)
            p.rel("xnrd", pe, pe.n)

        def xf_chunk(gidx):
            def xc(dc):
                return x_f[:, dc * TOK + gidx * TG: dc * TOK + gidx * TG + TG]
            return xc

        v_wo_done = None
        with ExitStack() as scA:
            EA = scA.enter_context
            tiles_sA = {
                "xn": EA(nc.sbuf_tensor("xn", [128, NDC * TG], BF16)),
                "xn2": EA(nc.sbuf_tensor("xn2", [128, NDC * TG], BF16)),
                "t1b": EA(nc.sbuf_tensor("t1b", [128, 2 * TG], BF16)),
                "t2b": EA(nc.sbuf_tensor("t2b", [128, 2 * TG], BF16)),
                "hdn": EA(nc.sbuf_tensor("hdn", [128, NIC * TG], BF16)),
                "xp": EA(nc.sbuf_tensor("xp", [128, NDC * TG], F16)),
            }
            tiles_sA["stg"] = tiles_sA["hdn"][:, 0:4 * TG]
            tiles_sA["gr"] = tiles_sA["hdn"][:, 4 * TG:8 * TG]
            lnb.clear()
            lnb["sqb"] = EA(nc.sbuf_tensor("sqbA", [128, TG], F16))
            lnb["rstd_bc"] = EA(nc.sbuf_tensor("rstdbcA", [128, TG], F16))
            lnb["mr_bc"] = EA(nc.sbuf_tensor("mrbcA", [128, TG], F16))
            lnb["xtmp"] = rw3[:, :]
            lnb["csb"] = tiles_sA["t2b"][:, 0:TG]
            lnb["snb"] = tiles_sA["t2b"][:, TG:2 * TG]
            xn = tiles_sA["xn"]
            xp = tiles_sA["xp"]

            v_ffn1 = {}
            xn2 = tiles_sA["xn2"]
            xbufs = [(xn, "xnA"), (xn2, "xnB")]
            v_ln = {0: emit_ln(xf_chunk(0), xn, xn_key="xnA",
                               x_ready=(xin, v_xin))}
            for gidx in range(4):
                xc = xf_chunk(gidx)
                buf, key = xbufs[gidx % 2]
                vh = emit_ffn_A(buf, v_ln[gidx], tiles_sA,
                                first=(gidx == 0), xn_key=key)
                if gidx + 1 < 4:
                    nbuf, nkey = xbufs[(gidx + 1) % 2]
                    v_ln[gidx + 1] = emit_ln(xf_chunk(gidx + 1), nbuf, xn_key=nkey)
                v_ffn1[gidx] = emit_ffn_B(vh, lambda oc, xc=xc: xc(oc),
                                          tiles_sA, first=(gidx == 0))[-1]

            if PHASES >= 2:
                def xp_chunk(dc):
                    return xp[:, dc * TG:(dc + 1) * TG]

                xv = x_in.rearrange("(dc p) t -> p dc t", p=128)
                for gidx in range(4):
                    p.use("g", "xp")
                    vxp = p.op("g", lambda e, gidx=gidx, xv=xv: e.dma_start(
                        out=xp[:, :],
                        in_=xv[:, :, TOK + gidx * TG: TOK + (gidx + 1) * TG]),
                        inc=(xin, 16))
                    p.wait("g", xin, vxp)
                    vxn_p1 = emit_ln(xp_chunk, xn, xn_key="xnA",
                                     x_ready=(xin, vxp))
                    vh = emit_ffn_A(xn, vxn_p1, tiles_sA, xn_key="xnA")
                    vxn_o = emit_ln(xf_chunk(gidx), xn2, xn_key="xnB",
                                    x_ready=(dve, v_ffn1[gidx]) if gidx < 4 else None)
                    v_fp_l = emit_ffn_B(vh, xp_chunk, tiles_sA)
                    vxn_p = emit_ln(xp_chunk, xn, xn_key="xnA",
                                    x_chunk_ready=[(dve, v) for v in v_fp_l])
                    p.rel("xp", dve, vxn_p)
                    tb_par = TOK + gidx * TG
                    tb_own = gidx * TG
                    emit_sem_proj(tiles_sA, xn, vxn_p, wks_bf, k_loc3, tb_par,
                                  first_w=v_cast["wks"] if gidx == 0 else None)
                    emit_geo_proj(tiles_sA, xn, vxn_p, wkg_bf, k_loc3, tb_par,
                                  first_w=v_cast["wkg"] if gidx == 0 else None)
                    emit_v_proj(tiles_sA, xn, vxn_p, tb_par,
                                first_w=v_cast["wv"] if gidx == 0 else None)
                    emit_sem_proj(tiles_sA, xn2, vxn_o, wks_bf, k_loc3, tb_own)
                    emit_geo_proj(tiles_sA, xn2, vxn_o, wkg_bf, k_loc3, tb_own)
                    emit_v_proj(tiles_sA, xn2, vxn_o, tb_own)
                    emit_sem_proj(tiles_sA, xn2, vxn_o, wqs_bf, q_loc3, tb_own,
                                  first_w=v_cast["wqs"] if gidx == 0 else None)
                    emit_geo_proj(tiles_sA, xn2, vxn_o, wqg_bf, q_loc3, tb_own,
                                  first_w=v_cast["wqg"] if gidx == 0 else None)

        if PHASES >= 3:
            with ExitStack() as scB:
                EB = scB.enter_context
                kh = EB(nc.sbuf_tensor("kh", [128, S], BF16))
                vhb = EB(nc.sbuf_tensor("vhb", [128, S], BF16))
                qh = EB(nc.sbuf_tensor("qh", [128, TOK], BF16))
                ptb = EB(nc.sbuf_tensor("ptb", [128, 4 * TG], BF16))
                lacc = EB(nc.sbuf_tensor("lacc", [128, TG], F32))
                linv_bc = EB(nc.sbuf_tensor("linv_bc", [128, TG], F32))
                avs = EB(nc.sbuf_tensor("avs", [128, TG], F32))
                ao = EB(nc.sbuf_tensor("ao", [128, H * TOK], BF16))

                p.use("v", "mA")
                p.use("v", "mB")
                vz1 = p.op("v", lambda e: e.memset(mga[:, :], 0.0), inc=(dve, 1))
                vz2 = p.op("v", lambda e: e.memset(mgb[:, :], 0.0), inc=(dve, 1))
                p.rel("mA", dve, vz1)
                p.rel("mB", dve, vz2)
                v_ks_all = [sm[f"ks{i}"].n for i in range(4)]
                for h in range(H):
                    hs = h % 2
                    hsem = sm["h0"] if hs == 0 else sm["h1"]
                    if h < 2:
                        for i in range(4):
                            p.wait("y", sm[f"ks{i}"], v_ks_all[i])
                    p.use("y", "kh")
                    p.op("y", lambda e, h=h: e.dma_start(
                        out=kh[:, :], in_=k_loc[h * 128:(h + 1) * 128, :]),
                        inc=(hsem, 16))
                    p.use("y", "vh")
                    p.op("y", lambda e, h=h: e.dma_start(
                        out=vhb[:, :],
                        in_=v_loc3[:, :, h * 128:(h + 1) * 128]), inc=(hsem, 16))
                    p.use("y", "qh")
                    vldh = p.op("y", lambda e, h=h: e.dma_start(
                        out=qh[:, :], in_=q_loc[h * 128:(h + 1) * 128, :]),
                        inc=(hsem, 16))
                    p.wait("t", hsem, vldh)
                    vhv = vhb[:, :]
                    v_last_pe_h = None
                    for c in range(4):
                        ab = c % 2
                        pav = ppo[ab]
                        p.use("t", "pA" if ab == 0 else "pC")
                        p.use("t", "pD")
                        strips = ([(0, l) for l in range(4 * c + 4)]
                                  + [(1, l) for l in range(1, 4 * c + 4)]
                                  + [(1, 0)])
                        nst = len(strips)
                        nb = (nst + 1) // 2
                        binfo = []
                        v_lacc = None
                        for k in range(nb + 1):
                            if k < nb:
                                lo = 2 * k
                                hi = min(2 * k + 1, nst - 1)
                                mg = mga if k % 2 == 0 else mgb
                                mkey = "mA" if k % 2 == 0 else "mB"
                                p.use("t", mkey)
                                vs = None
                                for j in range(lo, hi + 1):
                                    sec, l = strips[j]
                                    off = max(l - 4 * c, 0) * 128
                                    w = TG - off
                                    jj = j - lo
                                    vs = p.op("t", lambda e, sec=sec, l=l, off=off,
                                              w=w, jj=jj, mg=mg, c=c: e.matmul(
                                        mg[:, jj * TG + off:jj * TG + off + w],
                                        kh[:, sec * TOK + l * 128:
                                           sec * TOK + (l + 1) * 128],
                                        qh[:, c * TG + off:c * TG + off + w],
                                        start=True, stop=True), inc=(pe, 1))
                                cnt = hi - lo + 1
                                p.use("s", f"ptb{k % 2}")
                                p.use("s", f"ptbV{k % 2}")
                                p.wait("s", pe, vs)
                                va = p.op("s", lambda e, k=k, cnt=cnt, mg=mg: e.activation(
                                    ptb[:, (k % 2) * 2 * TG:
                                        (k % 2) * 2 * TG + cnt * TG],
                                    mg[:, 0:cnt * TG], AF.Exp, scale=SCALE),
                                    inc=(act, 1))
                                p.rel(mkey, act, va)
                                vmask = None
                                for j in range(lo, hi + 1):
                                    sec, l = strips[j]
                                    if l >= 4 * c:
                                        joff = (l - 4 * c) * 128
                                        msk = m0_sb if sec == 0 else m1_sb
                                        base = (k % 2) * 2 * TG + (j - lo) * TG
                                        p.wait("v", act, va)
                                        vmask = p.op("v", lambda e, base=base,
                                                     joff=joff, msk=msk: e.tensor_mul(
                                            ptb[:, base + joff:base + joff + 128],
                                            ptb[:, base + joff:base + joff + 128],
                                            msk[:, :]), inc=(dve, 1))
                                for j in range(lo, hi + 1):
                                    sec, l = strips[j]
                                    joff = max(l - 4 * c, 0) * 128
                                    jw = TG - joff
                                    base = (k % 2) * 2 * TG + (j - lo) * TG
                                    if vmask is not None:
                                        p.wait("v", dve, vmask)
                                    else:
                                        p.wait("v", act, va)
                                    if v_lacc is not None:
                                        p.wait("v", dve, v_lacc)
                                    if j == 0:
                                        v_lacc = p.op("v", lambda e, base=base:
                                                      e.tensor_copy(
                                            lacc[:, :], ptb[:, base:base + TG]),
                                            inc=(dve, 1))
                                    else:
                                        v_lacc = p.op("v", lambda e, base=base,
                                                      joff=joff, jw=jw: e.tensor_add(
                                            lacc[:, joff:joff + jw],
                                            lacc[:, joff:joff + jw],
                                            ptb[:, base + joff:base + joff + jw]),
                                            inc=(dve, 1))
                                p.rel(f"ptbV{k % 2}", dve, v_lacc)
                                binfo.append((k, lo, hi, va, vmask))
                            if k >= 1:
                                kk, lo, hi, va, vmask = binfo[k - 1]
                                if vmask is not None:
                                    p.wait("t", dve, vmask)
                                else:
                                    p.wait("t", act, va)
                                for j in range(lo, hi + 1):
                                    sec, l = strips[j]
                                    joff = max(l - 4 * c, 0) * 128
                                    jw = TG - joff
                                    base = (kk % 2) * 2 * TG + (j - lo) * TG
                                    vav = p.op("t", lambda e, sec=sec, l=l, joff=joff,
                                               jw=jw, base=base, vhv=vhv, pav=pav,
                                               stf=(j == 0), spf=(j == nst - 1):
                                               e.matmul(
                                        pav[:, joff:joff + jw],
                                        vhv[:, (sec * 16 + l) * 128:
                                            (sec * 16 + l + 1) * 128],
                                        ptb[:, base + joff:base + joff + jw],
                                        start=stf, stop=spf),
                                        inc=(pe, 1))
                                    v_last_pe_h = vav
                                p.rel(f"ptb{kk % 2}", pe, v_last_pe_h)
                        p.wait("t", dve, v_lacc)
                        vlp = p.op("t", lambda e: e.matmul(
                            pD[0:1, 0:TG], ones_f32[:, :], lacc[:, :],
                            start=True, stop=True), inc=(pe, 1))
                        p.use("v", "rowsD")
                        p.wait("v", pe, vlp)
                        vlc = p.op("v", lambda e: e.tensor_copy(
                            rows["l"], pD[0:1, 0:TG]), inc=(dve, 1))
                        p.wait("v", dve, vlc)
                        vli = p.op("v", lambda e: e.reciprocal(rows["li"], rows["l"]),
                                   inc=(dve, 1))
                        p.rel("pD", dve, vli)
                        p.use("t", "pB")
                        p.wait("t", dve, vli)
                        vbc = p.op("t", lambda e: e.matmul(
                            pB[:, 0:TG], ones_r[:, :], rows["li"],
                            start=True, stop=True), inc=(pe, 1))
                        p.rel("rowsD", pe, vbc)
                        p.use("s", "linvbc")
                        p.wait("s", pe, vbc)
                        va1 = p.op("s", lambda e: e.activation(
                            linv_bc[:, :], pB[:, 0:TG], AF.Copy), inc=(act, 1))
                        p.rel("pB", act, va1)
                        p.use("s", "avs")
                        p.wait("s", pe, v_last_pe_h)
                        va2 = p.op("s", lambda e, pav=pav: e.activation(
                            avs[:, :], pav[:, 0:TG], AF.Copy), inc=(act, 1))
                        p.rel("pA" if ab == 0 else "pC", act, va2)
                        p.wait("v", act, va2)
                        vao = p.op("v", lambda e, h=h, c=c: e.tensor_mul(
                            ao[:, h * TOK + c * TG: h * TOK + (c + 1) * TG],
                            avs[:, :], linv_bc[:, :]), inc=(dve, 1))
                        p.rel("linvbc", dve, vao)
                        p.rel("avs", dve, vao)
                    p.rel("kh", pe, v_last_pe_h)
                    p.rel("vh", pe, v_last_pe_h)
                    p.rel("qh", pe, v_last_pe_h)
                v_ao_done = dve.n

                v_res = None
                for ob in range(NDC):
                    sl = ob % 2
                    wsem, vw = load_w(sl,
                                      wo_bf[ob].rearrange("(ac p) o -> p ac o", p=128),
                                      NDC * 128,
                                      v_cast["wo"] if ob < 2 else None)
                    p.wait("t", wsem, vw)
                    if ob == 0:
                        p.wait("t", dve, v_ao_done)
                    vmm = None
                    for qc in range(4):
                        bp = qc % 2
                        po = ppo[bp]
                        p.use("t", "pA" if bp == 0 else "pC")
                        for adc in range(NDC):
                            vmm = p.op("t", lambda e, sl=sl, adc=adc, qc=qc, po=po:
                                       e.matmul(
                                po[:, 0:TG],
                                wb[sl][:, adc * 128:(adc + 1) * 128],
                                ao[:, adc * TOK + qc * TG: adc * TOK + (qc + 1) * TG],
                                start=(adc == 0), stop=(adc == NDC - 1)),
                                inc=(pe, 1) if adc == NDC - 1 else None)
                        p.use("s", "t3")
                        p.wait("s", pe, vmm)
                        va = p.op("s", lambda e, po=po: e.activation(
                            t3b[:, :], po[:, 0:TG], AF.Copy), inc=(act, 1))
                        p.rel("pA" if bp == 0 else "pC", act, va)
                        p.wait("v", act, va)
                        if v_res is not None:
                            p.wait("v", dve, v_res)
                        v_res = p.op("v", lambda e, ob=ob, qc=qc: e.tensor_add(
                            x_f[:, ob * TOK + qc * TG: ob * TOK + (qc + 1) * TG],
                            x_f[:, ob * TOK + qc * TG: ob * TOK + (qc + 1) * TG],
                            t3b[:, :]), inc=(dve, 1))
                        p.rel("t3", dve, v_res)
                    p.rel(f"wb{sl}", pe, vmm)
                v_wo_done = v_res

        if PHASES >= 4:
            with ExitStack() as scC:
                EC = scC.enter_context
                tiles_sC = {
                    "t1b": EC(nc.sbuf_tensor("t1c", [128, 2 * TG], BF16)),
                    "t2b": EC(nc.sbuf_tensor("t2c", [128, 2 * TG], BF16)),
                    "hdn": EC(nc.sbuf_tensor("hdnc", [128, NIC * TG], BF16)),
                }
                xn3 = EC(nc.sbuf_tensor("xn3", [128, NDC * TG], BF16))
                xn4 = EC(nc.sbuf_tensor("xn4", [128, NDC * TG], BF16))
                lnb.clear()
                lnb["sqb"] = EC(nc.sbuf_tensor("sqbC", [128, TG], F16))
                lnb["rstd_bc"] = EC(nc.sbuf_tensor("rstdbcC", [128, TG], F16))
                lnb["mr_bc"] = EC(nc.sbuf_tensor("mrbcC", [128, TG], F16))
                lnb["xtmp"] = rw3[:, :]
                xbufs4 = [(xn3, "xnA"), (xn4, "xnB")]
                v_ln4 = {0: emit_ln(xf_chunk(0), xn3, xn_key="xnA",
                                    x_ready=(dve, v_wo_done)
                                    if v_wo_done is not None else None)}
                def store_ob(ob, v_res):
                    p.wait("g", dve, v_res)
                    p.op("g", lambda e, ob=ob: e.dma_start(
                        out=x_out[ob * 128:(ob + 1) * 128, :],
                        in_=x_f[:, ob * TOK:(ob + 1) * TOK]), inc=(sm["out"], 16))

                for gidx in range(4):
                    xc = xf_chunk(gidx)
                    buf, key = xbufs4[gidx % 2]
                    vh = emit_ffn_A(buf, v_ln4[gidx], tiles_sC, xn_key=key)
                    if gidx + 1 < 4:
                        nbuf, nkey = xbufs4[(gidx + 1) % 2]
                        v_ln4[gidx + 1] = emit_ln(xf_chunk(gidx + 1), nbuf,
                                                  xn_key=nkey)
                    emit_ffn_B(vh, lambda oc, xc=xc: xc(oc), tiles_sC,
                               after_ob=store_ob if gidx == 3 else None)

        if PHASES < 4:
            v_final = dve.n
            p.wait("g", dve, v_final)
            for dc in range(NDC):
                p.op("g", lambda e, dc=dc: e.dma_start(
                    out=x_out[dc * 128:(dc + 1) * 128, :],
                    in_=x_f[:, dc * TOK:(dc + 1) * TOK]), inc=(sm["out"], 16))
        p.wait("g", sm["out"], sm["out"].n)

        with nc.Block() as block:
            @block.tensor
            def _(e):
                for fn in p.cmds["t"]:
                    fn(e)

            @block.vector
            def _(e):
                for fn in p.cmds["v"]:
                    fn(e)

            @block.scalar
            def _(e):
                for fn in p.cmds["s"]:
                    fn(e)

            @block.gpsimd
            def _(e):
                for fn in p.cmds["g"]:
                    fn(e)

            @block.sync
            def _(e):
                for fn in p.cmds["y"]:
                    fn(e)

    return nc


def _geo_perm():
    perm = np.zeros(1024, np.int64)
    for pos in range(1024):
        if pos < 512:
            h, j = pos // 32, pos % 32
            perm[pos] = h * 64 + j
        else:
            h, j = (pos - 512) // 32, (pos - 512) % 32
            perm[pos] = h * 64 + 32 + j
    return perm


def make_in_maps(L0_attn_x, Wq_sem, Wk_sem, Wq_geo, Wk_geo, Wv, Wo,
                 gate_w, up_w, down_w, gate_s, up_s, down_s):
    f32 = np.float32
    x = np.asarray(L0_attn_x, dtype=f32)
    perm = _geo_perm()
    shared = {
        "wqs_t": np.ascontiguousarray(np.asarray(Wq_sem, f32).T),
        "wks_t": np.ascontiguousarray(np.asarray(Wk_sem, f32).T),
        "wqg_t": np.ascontiguousarray(np.asarray(Wq_geo, f32).T[:, perm]),
        "wkg_t": np.ascontiguousarray(np.asarray(Wk_geo, f32).T[:, perm]),
        "wv_t": np.ascontiguousarray(np.asarray(Wv, f32).T),
        "wo_t": np.ascontiguousarray(np.asarray(Wo, f32).T),
        "gate_t": np.ascontiguousarray(np.asarray(gate_w, f32).T),
        "up_t": np.ascontiguousarray(np.asarray(up_w, f32).T),
        "down_t": np.ascontiguousarray(np.asarray(down_w, f32).T),
        "gs_in": np.ascontiguousarray(np.asarray(gate_s, f32).reshape(NIC, 128).T),
        "us_in": np.ascontiguousarray(np.asarray(up_s, f32).reshape(NIC, 128).T),
        "ds_in": np.ascontiguousarray(np.asarray(down_s, f32).reshape(NDC, 128).T),
    }
    tri = (np.arange(128)[:, None] <= np.arange(128)[None, :]).astype(f32)
    in_maps = []
    for r in range(8):
        b, par = r // 2, r % 2
        own = np.arange(par, 32, 2)
        partn = np.arange(1 - par, 32, 2)
        xs = x[b].reshape(32, 128, D)
        x_core = np.concatenate([xs[own], xs[partn]], axis=0).reshape(S, D)
        x_core = np.ascontiguousarray(x_core.T)
        tiles = np.concatenate([own, partn])
        pos = (tiles[:, None] * 128 + np.arange(128)[None, :]).reshape(-1)
        inv = 1.0 / (10000.0 ** (np.arange(0, 64, 2, dtype=np.float64) / 64.0))
        fr = pos[None, :].astype(np.float64) * inv[:, None]
        cos64 = np.tile(np.cos(fr), (4, 1))
        sin64 = np.tile(np.sin(fr), (4, 1))
        cos128 = cos64.astype(f32)
        sin128 = sin64.astype(f32)
        cosl = (cos64 - cos128).astype(f32)
        sinl = (sin64 - sin128).astype(f32)
        m1 = np.zeros((128, 128), f32) if par == 0 else np.ones((128, 128), f32)
        d = dict(shared)
        d.update({
            "x_in": x_core,
            "cos_in": np.ascontiguousarray(cos128),
            "sin_in": np.ascontiguousarray(sin128),
            "cosl_in": np.ascontiguousarray(cosl),
            "sinl_in": np.ascontiguousarray(sinl),
            "m0_in": tri, "m1_in": m1,
        })
        in_maps.append(d)
    return in_maps


def assemble_output(results):
    out = np.zeros((4, S, D), np.float32)
    for r in range(8):
        b, par = r // 2, r % 2
        own = np.arange(par, 32, 2)
        xo = np.asarray(results[r]["x_out"], dtype=np.float32)
        out[b].reshape(32, 128, D)[own] = xo.T.reshape(16, 128, D)
    return out


def kernel(L0_attn_x, L0_attn_Wq_sem, L0_attn_Wk_sem, L0_attn_Wq_geo,
           L0_attn_Wk_geo, L0_attn_Wv, L0_attn_Wo, L0_ffn_gate_w,
           L0_ffn_up_w, L0_ffn_down_w, L0_ffn_gate_s, L0_ffn_up_s,
           L0_ffn_down_s):
    in_maps = make_in_maps(L0_attn_x, L0_attn_Wq_sem, L0_attn_Wk_sem,
                           L0_attn_Wq_geo, L0_attn_Wk_geo, L0_attn_Wv,
                           L0_attn_Wo, L0_ffn_gate_w, L0_ffn_up_w,
                           L0_ffn_down_w, L0_ffn_gate_s, L0_ffn_up_s,
                           L0_ffn_down_s)
    nc = build_program()
    res = run_bass_kernel_spmd(nc, in_maps, core_ids=list(range(8)))
    return assemble_output(res.results)


# version 09
# speedup vs baseline: 1.0074x; 1.0074x over previous
import math
import os

import numpy as np

import concourse.bass as bass
import concourse.mybir as mybir
from concourse.bass_utils import run_bass_kernel_spmd

F32 = mybir.dt.float32
F16 = mybir.dt.float16
BF16 = mybir.dt.bfloat16
AF = mybir.ActivationFunctionType
ALU = mybir.AluOpType

D = 2048
INTER = 5632
H = 16
S = 4096
TOK = 2048
TG = 512
NDC = D // 128
NIC = INTER // 128
IBLK = 256
NIB = INTER // IBLK
SCALE = 1.0 / math.sqrt(128.0)
EPS = 1e-5

PHASES = int(os.environ.get("MK_PHASES", "4"))


class Sem:
    __slots__ = ("h", "n")

    def __init__(self, handle):
        self.h = handle
        self.n = 0


class B:
    def __init__(self):
        self.cmds = {e: [] for e in ("t", "v", "s", "g", "y")}
        self.guard = {}

    def op(self, eng, builder, inc=None):
        if inc is None:
            self.cmds[eng].append(lambda e, b=builder: b(e))
            return None
        sem, by = inc
        sem.n += by
        self.cmds[eng].append(lambda e, b=builder, h=sem.h, k=by: b(e).then_inc(h, k))
        return sem.n

    def wait(self, eng, sem, val):
        assert val <= sem.n, f"wait on future value {val} > {sem.n}"
        self.cmds[eng].append(lambda e, h=sem.h, v=val: e.wait_ge(h, v))

    def use(self, eng, key):
        if key in self.guard:
            s, v = self.guard[key]
            self.wait(eng, s, v)

    def rel(self, key, sem, val):
        self.guard[key] = (sem, val)


def build_program():
    nc = bass.Bass("TRN2", target_bir_lowering=False)

    x_in = nc.dram_tensor("x_in", [D, S], F32, kind="ExternalInput").ap()
    gate_t = nc.dram_tensor("gate_t", [D, INTER], F32, kind="ExternalInput").ap()
    up_t = nc.dram_tensor("up_t", [D, INTER], F32, kind="ExternalInput").ap()
    down_t = nc.dram_tensor("down_t", [INTER, D], F32, kind="ExternalInput").ap()
    wqs_t = nc.dram_tensor("wqs_t", [D, 1024], F32, kind="ExternalInput").ap()
    wks_t = nc.dram_tensor("wks_t", [D, 1024], F32, kind="ExternalInput").ap()
    wqg_t = nc.dram_tensor("wqg_t", [D, 1024], F32, kind="ExternalInput").ap()
    wkg_t = nc.dram_tensor("wkg_t", [D, 1024], F32, kind="ExternalInput").ap()
    wv_t = nc.dram_tensor("wv_t", [D, D], F32, kind="ExternalInput").ap()
    wo_t = nc.dram_tensor("wo_t", [D, D], F32, kind="ExternalInput").ap()
    gs_in = nc.dram_tensor("gs_in", [128, NIC], F32, kind="ExternalInput").ap()
    us_in = nc.dram_tensor("us_in", [128, NIC], F32, kind="ExternalInput").ap()
    ds_in = nc.dram_tensor("ds_in", [128, NDC], F32, kind="ExternalInput").ap()
    cos_in = nc.dram_tensor("cos_in", [128, S], F32, kind="ExternalInput").ap()
    sin_in = nc.dram_tensor("sin_in", [128, S], F32, kind="ExternalInput").ap()
    cosl_in = nc.dram_tensor("cosl_in", [128, S], F32, kind="ExternalInput").ap()
    sinl_in = nc.dram_tensor("sinl_in", [128, S], F32, kind="ExternalInput").ap()
    m0_in = nc.dram_tensor("m0_in", [128, 128], F32, kind="ExternalInput").ap()
    m1_in = nc.dram_tensor("m1_in", [128, 128], F32, kind="ExternalInput").ap()

    x_out = nc.dram_tensor("x_out", [D, TOK], F32, kind="ExternalOutput").ap()

    gate_bf = nc.dram_tensor("gate_bf", [NIB, D, IBLK], BF16, kind="Internal").ap()
    up_bf = nc.dram_tensor("up_bf", [NIB, D, IBLK], BF16, kind="Internal").ap()
    down_bf = nc.dram_tensor("down_bf", [NDC, INTER, 128], BF16, kind="Internal").ap()
    wqs_bf = nc.dram_tensor("wqs_bf", [8, D, 128], BF16, kind="Internal").ap()
    wks_bf = nc.dram_tensor("wks_bf", [8, D, 128], BF16, kind="Internal").ap()
    wqg_bf = nc.dram_tensor("wqg_bf", [8, D, 128], BF16, kind="Internal").ap()
    wkg_bf = nc.dram_tensor("wkg_bf", [8, D, 128], BF16, kind="Internal").ap()
    wv_bf = nc.dram_tensor("wv_bf", [4, D, 512], BF16, kind="Internal").ap()
    wo_bf = nc.dram_tensor("wo_bf", [NDC, D, 128], BF16, kind="Internal").ap()
    cos_bf = nc.dram_tensor("cos_bf", [128, S], BF16, kind="Internal").ap()
    sin_bf = nc.dram_tensor("sin_bf", [128, S], BF16, kind="Internal").ap()
    k_loc = nc.dram_tensor("k_loc", [D, S], BF16, kind="Internal").ap()
    v_loc = nc.dram_tensor("v_loc", [S, D], BF16, kind="Internal").ap()
    q_loc = nc.dram_tensor("q_loc", [D, TOK], BF16, kind="Internal").ap()

    k_loc3 = k_loc.rearrange("(h r) t -> h r t", r=128)
    q_loc3 = q_loc.rearrange("(h r) t -> h r t", r=128)
    v_loc3 = v_loc.rearrange("(kt p) dv -> p kt dv", p=128)

    from contextlib import ExitStack
    with ExitStack() as top:
        E = top.enter_context
        p = B()

        sem_names = ("pe", "dve", "act", "w0", "w1", "u0", "u1", "cast", "xin",
                     "con", "rt", "tb0", "ks0", "ks1", "ks2", "ks3", "h0", "h1",
                     "out")
        sm = {n: Sem(E(nc.semaphore(n))) for n in sem_names}
        pe, dve, act = sm["pe"], sm["dve"], sm["act"]

        x_f = E(nc.sbuf_tensor("x_f", [128, NDC * TOK], F16))
        wbig = E(nc.sbuf_tensor("wbig", [128, 16384], BF16))
        gs_sb = E(nc.sbuf_tensor("gs_sb", [128, NIC], F32))
        us_sb = E(nc.sbuf_tensor("us_sb", [128, NIC], F32))
        ds_sb = E(nc.sbuf_tensor("ds_sb", [128, NDC], F32))
        dsh_sb = E(nc.sbuf_tensor("dsh_sb", [128, NDC], F32))
        m0_sb = E(nc.sbuf_tensor("m0_sb", [128, 128], BF16))
        m1_sb = E(nc.sbuf_tensor("m1_sb", [128, 128], BF16))
        ones_bf = E(nc.sbuf_tensor("ones_bf", [128, 1], BF16))
        ones_f16 = E(nc.sbuf_tensor("ones_f16", [128, 1], F16))
        ones_r = E(nc.sbuf_tensor("ones_r", [1, 128], F32))
        ones_f32 = E(nc.sbuf_tensor("ones_f32", [128, 1], F32))
        rw1 = E(nc.sbuf_tensor("rw1", [128, TG], F32))
        rw2 = E(nc.sbuf_tensor("rw2", [128, TG], F32))
        rw3 = E(nc.sbuf_tensor("rw3", [128, TG], F32))
        rw4 = E(nc.sbuf_tensor("rw4", [128, TG], F32))
        rw5 = E(nc.sbuf_tensor("rw5", [128, TG], F32))
        rows = {
            "mean": rw1[0:1, :], "sx": rw1[32:33, :], "sq": rw1[64:65, :],
            "e2": rw2[0:1, :], "var": rw2[32:33, :], "inv": rw2[64:65, :],
            "msq": rw3[0:1, :],
            "rstd": rw4[0:1, :],
            "mr": rw5[0:1, :], "l": rw5[32:33, :],
            "li": rw4[0:1, :],
        }
        t3b = E(nc.sbuf_tensor("t3b", [128, TG], BF16))
        lnb = {}

        wb = [wbig[:, 0:8192], wbig[:, 8192:16384]]
        pslq = [wbig[:, 0:2048], wbig[:, 2048:4096],
                wbig[:, 8192:10240], wbig[:, 10240:12288]]
        wstate = {"mode": "half"}


        pA = E(nc.psum_tensor("pA", [128, 512], F32))
        pB = E(nc.psum_tensor("pB", [128, 512], F32))
        pC = E(nc.psum_tensor("pC", [128, 512], F32))
        pD = E(nc.psum_tensor("pD", [128, 512], F32))
        mga = E(nc.psum_tensor("mga", [128, 1024], F32))
        mgb = E(nc.psum_tensor("mgb", [128, 1024], F32))
        pgu = [pA, pB, pC, pD]
        ppo = [pA, pC]

        con, xin, cast = sm["con"], sm["xin"], sm["cast"]
        for dst, src in ((gs_sb, gs_in), (us_sb, us_in), (ds_sb, ds_in)):
            p.op("y", lambda e, d=dst, s=src: e.dma_start(out=d[:, :], in_=s),
                 inc=(con, 16))
        v_con = con.n
        for dc in range(NDC):
            p.op("g", lambda e, dc=dc: e.dma_start(
                out=x_f[:, dc * TOK:(dc + 1) * TOK],
                in_=x_in[dc * 128:(dc + 1) * 128, 0:TOK]), inc=(xin, 16))
        v_xin = xin.n
        p.wait("g", xin, v_xin)

        def cast_dma(dst, src):
            v = p.op("g", lambda e, d=dst, s=src: e.dma_start(out=d, in_=s),
                     inc=(cast, 16))
            p.wait("g", cast, v)
            return v

        def cast_blocks(dst, src3, nblk):
            out = []
            for ib in range(nblk):
                out.append(cast_dma(dst[ib], src3[ib]))
            return out

        cast_dma(m0_sb[:, :], m0_in)
        cast_dma(m1_sb[:, :], m1_in)
        v_cast = {}
        gate_src = gate_t.rearrange("d (ib i) -> ib d i", i=IBLK)
        up_src = up_t.rearrange("d (ib i) -> ib d i", i=IBLK)
        v_cast["gu_ib"] = []
        for ib in range(NIB):
            cast_dma(gate_bf[ib], gate_src[ib])
            v_cast["gu_ib"].append(cast_dma(up_bf[ib], up_src[ib]))
        v_cast["down_ob"] = cast_blocks(
            down_bf, down_t.rearrange("i (ob o) -> ob i o", o=128), NDC)
        v_cast["gate"] = v_cast["up"] = v_cast["gu_ib"][-1]
        v_cast["down"] = v_cast["down_ob"][-1]
        for nm, dst, src, nblk in (
            ("wks", wks_bf, wks_t.rearrange("d (oc o) -> oc d o", o=128), 8),
            ("wkg", wkg_bf, wkg_t.rearrange("d (oc o) -> oc d o", o=128), 8),
            ("wv", wv_bf, wv_t.rearrange("d (oc o) -> oc d o", o=512), 4),
            ("wqs", wqs_bf, wqs_t.rearrange("d (oc o) -> oc d o", o=128), 8),
            ("wqg", wqg_bf, wqg_t.rearrange("d (oc o) -> oc d o", o=128), 8),
            ("wo", wo_bf, wo_t.rearrange("d (oc o) -> oc d o", o=128), NDC),
        ):
            v_cast[nm] = cast_blocks(dst, src, nblk)[-1]
        with ExitStack() as sc0:
            th = sc0.enter_context(nc.sbuf_tensor("tbl_hi", [128, S], F32))
            tl = sc0.enter_context(nc.sbuf_tensor("tbl_lo", [128, S], F32))
            for dst_bf, hi_src, lo_src in ((cos_bf, cos_in, cosl_in),
                                           (sin_bf, sin_in, sinl_in)):
                v1 = p.op("y", lambda e, s=hi_src: e.dma_start(out=th[:, :], in_=s),
                          inc=(sm["tb0"], 16))
                v2 = p.op("y", lambda e, s=lo_src: e.dma_start(out=tl[:, :], in_=s),
                          inc=(sm["tb0"], 16))
                p.wait("v", sm["tb0"], v2)
                p.use("v", "tbl")
                v3 = p.op("v", lambda e: e.tensor_add(th[:, :], th[:, :], tl[:, :]),
                          inc=(dve, 1))
                p.wait("g", dve, v3)
                v_cast[("tbl", id(dst_bf))] = p.op(
                    "g", lambda e, d=dst_bf: e.dma_start(out=d, in_=th[:, :]),
                    inc=(cast, 16))
                p.wait("g", cast, cast.n)
                p.rel("tbl", cast, cast.n)
                p.wait("y", cast, cast.n)
        v_cast["cos"] = v_cast["sin"] = cast.n
        p.wait("v", cast, cast.n)
        p.wait("s", cast, cast.n)
        p.wait("g", cast, cast.n)

        p.op("v", lambda e: e.memset(mga[:, :], 0.0), inc=(dve, 1))
        p.op("v", lambda e: e.memset(mgb[:, :], 0.0), inc=(dve, 1))
        p.op("v", lambda e: e.memset(ones_bf[:, :], 1.0), inc=(dve, 1))
        p.op("v", lambda e: e.memset(ones_f16[:, :], 1.0), inc=(dve, 1))
        p.op("v", lambda e: e.memset(ones_r[:, :], 1.0), inc=(dve, 1))
        p.op("v", lambda e: e.memset(ones_f32[:, :], 1.0), inc=(dve, 1))
        p.wait("v", con, v_con)
        v_ones = p.op("v", lambda e: e.tensor_scalar_mul(
            dsh_sb[:, :], ds_sb[:, :], 0.5), inc=(dve, 1))

        qsems = [sm["w0"], sm["w1"], sm["u0"], sm["u1"]]

        def wmode(m):
            if wstate["mode"] == m:
                return
            if m == "quarter":
                p.use("y", "wb0")
                p.use("y", "wb1")
            else:
                for j in range(4):
                    p.use("y", f"pw{j}")
            wstate["mode"] = m

        def emit_ln(xc, dst_xn, xn_key="xnrd", x_ready=None, x_chunk_ready=None):
            sqb = lnb["sqb"]
            rstd_bc = lnb["rstd_bc"]
            mr_bc = lnb["mr_bc"]
            xtmp = lnb["xtmp"]
            sxp = mga[0:1, 0:TG]
            sqp = mga[0:1, TG:2 * TG]
            bc1 = mgb[:, 0:TG]
            bc2 = mgb[:, TG:2 * TG]
            p.use("t", "mA")
            p.wait("t", dve, v_ones)
            if x_ready is not None:
                p.wait("t", x_ready[0], x_ready[1])
                p.wait("s", x_ready[0], x_ready[1])
            for dc in range(NDC):
                if x_chunk_ready is not None:
                    p.wait("t", x_chunk_ready[dc][0], x_chunk_ready[dc][1])
                p.op("t", lambda e, dc=dc, sxp=sxp: e.matmul(
                    sxp, ones_f16[:, :], xc(dc),
                    start=(dc == 0), stop=(dc == NDC - 1)))
            p.use("s", "sqb")
            prev_pe = None
            for dc in range(NDC):
                if x_chunk_ready is not None:
                    p.wait("s", x_chunk_ready[dc][0], x_chunk_ready[dc][1])
                if prev_pe is not None:
                    p.wait("s", pe, prev_pe)
                va = p.op("s", lambda e, dc=dc, sqb=sqb: e.activation(
                    sqb[:, :], xc(dc), AF.Square), inc=(act, 1))
                p.wait("t", act, va)
                prev_pe = p.op("t", lambda e, dc=dc, sqb=sqb, sqp=sqp: e.matmul(
                    sqp, ones_f16[:, :], sqb[:, :],
                    start=(dc == 0), stop=(dc == NDC - 1)), inc=(pe, 1))
            p.rel("sqb", pe, prev_pe)
            v_stats = pe.n
            p.use("v", "rowsABC")
            p.wait("v", pe, v_stats)
            p.op("v", lambda e, sxp=sxp: e.tensor_copy(rows["sx"], sxp))
            v_ev = p.op("v", lambda e, sqp=sqp: e.tensor_copy(rows["sq"], sqp),
                        inc=(dve, 1))
            p.rel("mA", dve, v_ev)
            p.wait("s", dve, v_ev)
            p.op("s", lambda e: e.activation(rows["mean"], rows["sx"], AF.Copy,
                                             scale=1.0 / D))
            v_a1 = p.op("s", lambda e: e.activation(rows["e2"], rows["sq"], AF.Copy,
                                                    scale=1.0 / D), inc=(act, 1))
            p.wait("s", act, v_a1)
            v_a2 = p.op("s", lambda e: e.activation(rows["msq"], rows["mean"],
                                                    AF.Square), inc=(act, 1))
            p.wait("v", act, v_a2)
            v_sub = p.op("v", lambda e: e.tensor_sub(
                rows["var"], rows["e2"], rows["msq"]), inc=(dve, 1))
            p.wait("v", dve, v_sub)
            v_d0 = p.op("v", lambda e: e.tensor_scalar_add(
                rows["var"], rows["var"], EPS), inc=(dve, 1))
            p.wait("v", dve, v_d0)
            v_d1 = p.op("v", lambda e: e.reciprocal(rows["inv"], rows["var"]),
                        inc=(dve, 1))
            p.wait("s", dve, v_d1)
            v_a3 = p.op("s", lambda e: e.activation(rows["rstd"], rows["inv"],
                                                    AF.Sqrt), inc=(act, 1))
            p.wait("v", act, v_a3)
            v_d2 = p.op("v", lambda e: e.tensor_mul(
                rows["mr"], rows["mean"], rows["rstd"]), inc=(dve, 1))
            p.use("t", "mB")
            p.wait("t", dve, v_d2)
            p.op("t", lambda e, bc1=bc1: e.matmul(bc1, ones_r[:, :], rows["rstd"],
                                                  start=True, stop=True))
            v_bc = p.op("t", lambda e, bc2=bc2: e.matmul(
                bc2, ones_r[:, :], rows["mr"], start=True, stop=True),
                inc=(pe, 1))
            p.rel("rowsABC", pe, v_bc)
            p.use("s", "bcb")
            p.wait("s", pe, v_bc)
            p.op("s", lambda e, rstd_bc=rstd_bc, bc1=bc1: e.activation(
                rstd_bc[:, :], bc1, AF.Copy))
            v_bce = p.op("s", lambda e, mr_bc=mr_bc, bc2=bc2: e.activation(
                mr_bc[:, :], bc2, AF.Copy), inc=(act, 1))
            p.rel("mB", act, v_bce)
            p.wait("v", act, v_bce)
            p.use("v", xn_key)
            v_last = None
            prev = v_d2
            for dc in range(NDC):
                if prev is not None:
                    p.wait("v", dve, prev)
                v_mul = p.op("v", lambda e, dc=dc, xtmp=xtmp, rstd_bc=rstd_bc:
                     e.tensor_mul(xtmp[:, :], xc(dc), rstd_bc[:, :]), inc=(dve, 1))
                p.wait("v", dve, v_mul)
                v_last = p.op("v", lambda e, dc=dc, xtmp=xtmp, mr_bc=mr_bc:
                              e.tensor_sub(
                    dst_xn[:, dc * TG:(dc + 1) * TG], xtmp[:, :], mr_bc[:, :]),
                    inc=(dve, 1))
                prev = v_last
            p.rel("bcb", dve, v_last)
            return v_last

        def emit_ffn_A(src_xn, v_xn_ready, tiles, first=False, xn_key="xnrd"):
            t1b, t2b, hdn = tiles["t1b"], tiles["t2b"], tiles["hdn"]
            wmode("half")
            last_pe = None
            for ib in range(NIB):
                sl = ib % 2
                wsem = sm["w0"] if sl == 0 else sm["w1"]
                usem = sm["u0"] if sl == 0 else sm["u1"]
                p.use("y", f"wb{sl}")
                if first:
                    p.wait("y", cast, v_cast["gu_ib"][ib])
                vw = p.op("y", lambda e, ib=ib, sl=sl: e.dma_start(
                    out=wb[sl][:, 0:NDC * IBLK],
                    in_=gate_bf[ib].rearrange("(dc p) i -> p dc i", p=128)),
                    inc=(wsem, 16))
                vu = p.op("y", lambda e, ib=ib, sl=sl: e.dma_start(
                    out=wb[sl][:, NDC * IBLK:2 * NDC * IBLK],
                    in_=up_bf[ib].rearrange("(dc p) i -> p dc i", p=128)),
                    inc=(usem, 16))
                p.wait("t", wsem, vw)
                p.wait("t", usem, vu)
                if ib == 0:
                    p.wait("t", dve, v_xn_ready)
                for hh in range(2):
                    ic = ib * 2 + hh
                    bp = ic % 2
                    pg = pgu[2 * bp]
                    pu = pgu[2 * bp + 1]
                    p.use("t", "pA" if bp == 0 else "pC")
                    p.use("t", "pB" if bp == 0 else "pD")
                    for dc in range(NDC):
                        p.op("t", lambda e, sl=sl, dc=dc, hh=hh, pg=pg: e.matmul(
                            pg[:, 0:TG],
                            wb[sl][:, dc * IBLK + hh * 128: dc * IBLK + (hh + 1) * 128],
                            src_xn[:, dc * TG:(dc + 1) * TG],
                            start=(dc == 0), stop=(dc == NDC - 1)))
                    for dc in range(NDC):
                        last_pe = p.op("t", lambda e, sl=sl, dc=dc, hh=hh, pu=pu: e.matmul(
                            pu[:, 0:TG],
                            wb[sl][:, NDC * IBLK + dc * IBLK + hh * 128:
                                    NDC * IBLK + dc * IBLK + (hh + 1) * 128],
                            src_xn[:, dc * TG:(dc + 1) * TG],
                            start=(dc == 0), stop=(dc == NDC - 1)),
                            inc=(pe, 1) if dc == NDC - 1 else None)
                    tp = ic % 2
                    p.use("s", f"t12_{tp}")
                    p.wait("s", pe, last_pe)
                    p.op("s", lambda e, ic=ic, tp=tp, pg=pg: e.activation(
                        t1b[:, tp * TG:(tp + 1) * TG], pg[:, 0:TG], AF.Silu,
                        scale=gs_sb[:, ic:ic + 1]))
                    va = p.op("s", lambda e, ic=ic, tp=tp, pu=pu: e.activation(
                        t2b[:, tp * TG:(tp + 1) * TG], pu[:, 0:TG], AF.Copy,
                        scale=us_sb[:, ic:ic + 1]), inc=(act, 1))
                    p.rel("pA" if bp == 0 else "pC", act, va)
                    p.rel("pB" if bp == 0 else "pD", act, va)
                    p.wait("v", act, va)
                    if ic < 4:
                        p.use("v", f"stg{ic}")
                    elif ic < 8:
                        p.use("v", f"gr{ic - 4}")
                    vh_ = p.op("v", lambda e, ic=ic, tp=tp: e.tensor_mul(
                        hdn[:, ic * TG:(ic + 1) * TG],
                        t1b[:, tp * TG:(tp + 1) * TG],
                        t2b[:, tp * TG:(tp + 1) * TG]), inc=(dve, 1))
                    p.rel(f"t12_{tp}", dve, vh_)
                p.rel(f"wb{sl}", pe, last_pe)
            p.rel(xn_key, pe, last_pe)
            return dve.n

        def emit_ffn_B(v_hdn_done, xdst, tiles, first=False, after_ob=None):
            hdn = tiles["hdn"]
            wmode("half")
            v_res = None
            rlist = []
            for ob in range(NDC):
                sl = ob % 2
                wsem = sm["w0"] if sl == 0 else sm["w1"]
                p.use("y", f"wb{sl}")
                if first:
                    p.wait("y", cast, v_cast["down_ob"][ob])
                vw = p.op("y", lambda e, ob=ob, sl=sl: e.dma_start(
                    out=wb[sl][:, 0:NIC * 128],
                    in_=down_bf[ob].rearrange("(ic p) o -> p ic o", p=128)),
                    inc=(wsem, 16))
                p.wait("t", wsem, vw)
                if ob == 0:
                    p.wait("t", dve, v_hdn_done)
                bp = ob % 2
                po = ppo[bp]
                p.use("t", "pA" if bp == 0 else "pC")
                vmm = None
                for ic in range(NIC):
                    vmm = p.op("t", lambda e, sl=sl, ic=ic, po=po: e.matmul(
                        po[:, 0:TG], wb[sl][:, ic * 128:(ic + 1) * 128],
                        hdn[:, ic * TG:(ic + 1) * TG],
                        start=(ic == 0), stop=(ic == NIC - 1)),
                        inc=(pe, 1) if ic == NIC - 1 else None)
                p.rel(f"wb{sl}", pe, vmm)
                p.use("s", "t3")
                p.wait("s", pe, vmm)
                va = p.op("s", lambda e, ob=ob, po=po: e.activation(
                    t3b[:, :], po[:, 0:TG], AF.Copy,
                    scale=dsh_sb[:, ob:ob + 1]), inc=(act, 1))
                p.rel("pA" if bp == 0 else "pC", act, va)
                p.wait("v", act, va)
                if v_res is not None:
                    p.wait("v", dve, v_res)
                v_res = p.op("v", lambda e, ob=ob: e.tensor_add(
                    xdst(ob), xdst(ob), t3b[:, :]), inc=(dve, 1))
                p.rel("t3", dve, v_res)
                rlist.append(v_res)
                if after_ob is not None:
                    after_ob(ob, v_res)
            return rlist

        def emit_ffn(src_xn, v_xn_ready, xdst, tiles, first=False, xn_key="xnrd"):
            vh = emit_ffn_A(src_xn, v_xn_ready, tiles, first=first, xn_key=xn_key)
            return emit_ffn_B(vh, xdst, tiles, first=first)[-1]

        def load_w(sl, src_ap, ncols, pre_wait=None):
            wsem = sm["w0"] if sl == 0 else sm["w1"]
            p.use("y", f"wb{sl}")
            if pre_wait is not None:
                p.wait("y", cast, pre_wait)
            vw = p.op("y", lambda e, s=src_ap, sl=sl, n=ncols: e.dma_start(
                out=wb[sl][:, 0:n], in_=s), inc=(wsem, 16))
            return wsem, vw

        def proj_mm4(sl, src_xn, v_xn_ready, first_mm, bp):
            pk = ppo[bp]
            p.use("t", "pA" if bp == 0 else "pC")
            if first_mm:
                p.wait("t", dve, v_xn_ready)
            vmm = None
            for dc in range(NDC):
                vmm = p.op("t", lambda e, sl=sl, dc=dc, pk=pk: e.matmul(
                    pk[:, 0:TG], pslq[sl][:, dc * 128:(dc + 1) * 128],
                    src_xn[:, dc * TG:(dc + 1) * TG],
                    start=(dc == 0), stop=(dc == NDC - 1)),
                    inc=(pe, 1) if dc == NDC - 1 else None)
            return pk, vmm

        def proj_mm(sl, src_xn, v_xn_ready, first_mm, bp):
            pk = ppo[bp]
            p.use("t", "pA" if bp == 0 else "pC")
            if first_mm:
                p.wait("t", dve, v_xn_ready)
            vmm = None
            for dc in range(NDC):
                vmm = p.op("t", lambda e, sl=sl, dc=dc, pk=pk: e.matmul(
                    pk[:, 0:TG], wb[sl][:, dc * 128:(dc + 1) * 128],
                    src_xn[:, dc * TG:(dc + 1) * TG],
                    start=(dc == 0), stop=(dc == NDC - 1)),
                    inc=(pe, 1) if dc == NDC - 1 else None)
            return pk, vmm

        def emit_sem_proj(tiles, src_xn, v_xn_ready, w_bf, dst3, tbase,
                          first_w=None):
            stg = tiles["stg"]
            wmode("quarter")
            for oc in range(8):
                sl = oc % 4
                wsem = qsems[sl]
                p.use("y", f"pw{sl}")
                if first_w is not None and oc < 4:
                    p.wait("y", cast, first_w)
                vw = p.op("y", lambda e, oc=oc, sl=sl, wb_=w_bf: e.dma_start(
                    out=pslq[sl][:, :],
                    in_=wb_[oc].rearrange("(dc p) o -> p dc o", p=128)),
                    inc=(wsem, 16))
                p.wait("t", wsem, vw)
                pk, vmm = proj_mm4(sl, src_xn, v_xn_ready, oc == 0, oc % 2)
                p.rel(f"pw{sl}", pe, vmm)
                sp = oc % 4
                ksem = sm[f"ks{sp}"]
                p.use("v", f"stg{sp}")
                p.wait("v", pe, vmm)
                vd = p.op("v", lambda e, sp=sp, pk=pk, stg=stg: e.tensor_copy(
                    stg[:, sp * TG:(sp + 1) * TG], pk[:, 0:TG]), inc=(dve, 1))
                p.rel("pA" if oc % 2 == 0 else "pC", dve, vd)
                p.wait("g", dve, vd)
                vst = p.op("g", lambda e, oc=oc, sp=sp, d3=dst3, tb=tbase, stg=stg:
                           e.dma_start(
                    out=d3[2 * oc:2 * oc + 2, 0:64, tb:tb + TG],
                    in_=stg[:, sp * TG:(sp + 1) * TG]), inc=(ksem, 16))
                p.rel(f"stg{sp}", ksem, vst)
            p.rel("xnrd", pe, pe.n)

        def emit_geo_proj(tiles, src_xn, v_xn_ready, w_bf, dst3, tbase,
                          first_w=None):
            stg, gr = tiles["stg"], tiles["gr"]
            ra = tiles["t1b"][:, 0:TG]
            rb = tiles["t1b"][:, TG:2 * TG]
            csb, snb = lnb["csb"], lnb["snb"]
            wmode("quarter")
            p.use("g", "ropetbl")
            p.use("g", "t12_0")
            p.use("g", "t12_1")
            p.op("g", lambda e, tb=tbase, csb=csb: e.dma_start(
                out=csb[:, :], in_=cos_bf[:, tb:tb + TG]), inc=(sm["rt"], 16))
            v_rt = p.op("g", lambda e, tb=tbase, snb=snb: e.dma_start(
                out=snb[:, :], in_=sin_bf[:, tb:tb + TG]), inc=(sm["rt"], 16))
            order = [0, 4, 1, 5, 2, 6, 3, 7]
            for idx, oc in enumerate(order):
                sl = idx % 4
                wsem = qsems[sl]
                p.use("y", f"pw{sl}")
                if first_w is not None and idx < 4:
                    p.wait("y", cast, first_w)
                vw = p.op("y", lambda e, oc=oc, sl=sl, wb_=w_bf: e.dma_start(
                    out=pslq[sl][:, :],
                    in_=wb_[oc].rearrange("(dc p) o -> p dc o", p=128)),
                    inc=(wsem, 16))
                p.wait("t", wsem, vw)
                pk, vmm = proj_mm4(sl, src_xn, v_xn_ready, idx == 0, idx % 2)
                p.rel(f"pw{sl}", pe, vmm)
                gslot = (idx // 2) % 2 + (2 if oc >= 4 else 0)
                p.use("v", f"gr{gslot}")
                p.wait("v", pe, vmm)
                vd = p.op("v", lambda e, g=gslot, pk=pk, gr=gr: e.tensor_copy(
                    gr[:, g * TG:(g + 1) * TG], pk[:, 0:TG]), inc=(dve, 1))
                p.rel("pA" if idx % 2 == 0 else "pC", dve, vd)
                if oc >= 4:
                    c = oc - 4
                    s1 = (idx // 2) % 2
                    s2 = s1 + 2
                    g1 = gr[:, s1 * TG:(s1 + 1) * TG]
                    g2 = gr[:, s2 * TG:(s2 + 1) * TG]
                    sp = c % 2
                    ksemA = sm[f"ks{2 * sp}"]
                    ksemB = sm[f"ks{2 * sp + 1}"]
                    p.use("v", f"stg{2 * sp}")
                    p.use("v", f"stg{2 * sp + 1}")
                    p.use("v", "t12_0")
                    p.use("v", "t12_1")
                    p.wait("v", dve, vd)
                    p.wait("v", sm["rt"], v_rt)
                    p.op("v", lambda e, g1=g1, csb=csb: e.tensor_mul(ra, g1, csb[:, :]))
                    v1 = p.op("v", lambda e, g2=g2, snb=snb: e.tensor_mul(rb, g2, snb[:, :]),
                              inc=(dve, 1))
                    p.wait("v", dve, v1)
                    v1b = p.op("v", lambda e, sp=sp, stg=stg: e.tensor_sub(
                        stg[:, (2 * sp) * TG:(2 * sp + 1) * TG], ra, rb),
                        inc=(dve, 1))
                    p.wait("v", dve, v1b)
                    p.op("v", lambda e, g1=g1, snb=snb: e.tensor_mul(ra, g1, snb[:, :]))
                    v2 = p.op("v", lambda e, g2=g2, csb=csb: e.tensor_mul(rb, g2, csb[:, :]),
                              inc=(dve, 1))
                    p.wait("v", dve, v2)
                    vro = p.op("v", lambda e, sp=sp, stg=stg: e.tensor_add(
                        stg[:, (2 * sp + 1) * TG:(2 * sp + 2) * TG], ra, rb),
                        inc=(dve, 1))
                    p.rel(f"gr{s1}", dve, vro)
                    p.rel(f"gr{s2}", dve, vro)
                    p.rel("t12_0", dve, vro)
                    p.rel("t12_1", dve, vro)
                    p.rel("ropetbl", dve, vro)
                    p.wait("g", dve, vro)
                    vs1 = p.op("g", lambda e, c=c, sp=sp, d3=dst3, tb=tbase, stg=stg:
                               e.dma_start(
                        out=d3[4 * c:4 * c + 4, 64:96, tb:tb + TG],
                        in_=stg[:, (2 * sp) * TG:(2 * sp + 1) * TG]), inc=(ksemA, 16))
                    vs2 = p.op("g", lambda e, c=c, sp=sp, d3=dst3, tb=tbase, stg=stg:
                               e.dma_start(
                        out=d3[4 * c:4 * c + 4, 96:128, tb:tb + TG],
                        in_=stg[:, (2 * sp + 1) * TG:(2 * sp + 2) * TG]),
                        inc=(ksemB, 16))
                    p.rel(f"stg{2 * sp}", ksemA, vs1)
                    p.rel(f"stg{2 * sp + 1}", ksemB, vs2)
            p.rel("xnrd", pe, pe.n)

        def emit_v_proj(tiles, src_xn, v_xn_ready, tbase, first_w=None):
            stg = tiles["stg"]
            wmode("half")
            for dvc in range(4):
                sl = dvc % 2
                wsem, vw = load_w(sl, wv_bf[dvc].rearrange("(dc p) o -> p dc o", p=128),
                                  NDC * 512, first_w if dvc < 2 else None)
                p.wait("t", wsem, vw)
                if dvc == 0:
                    p.wait("t", dve, v_xn_ready)
                vmm = None
                for tt in range(4):
                    bp = tt % 2
                    pv = ppo[bp]
                    p.use("t", "pA" if bp == 0 else "pC")
                    for dc in range(NDC):
                        vmm = p.op("t", lambda e, sl=sl, dc=dc, tt=tt, pv=pv: e.matmul(
                            pv[:, 0:TG],
                            src_xn[:, dc * TG + tt * 128: dc * TG + (tt + 1) * 128],
                            wb[sl][:, dc * 512:(dc + 1) * 512],
                            start=(dc == 0), stop=(dc == NDC - 1)),
                            inc=(pe, 1) if dc == NDC - 1 else None)
                    sp = tt
                    ksem = sm[f"ks{sp}"]
                    p.use("v", f"stg{sp}")
                    p.wait("v", pe, vmm)
                    vd = p.op("v", lambda e, sp=sp, pv=pv, stg=stg: e.tensor_copy(
                        stg[:, sp * TG:(sp + 1) * TG], pv[:, 0:TG]), inc=(dve, 1))
                    p.rel("pA" if bp == 0 else "pC", dve, vd)
                    p.wait("g", dve, vd)
                    vst = p.op("g", lambda e, dvc=dvc, tt=tt, sp=sp, tb=tbase, stg=stg:
                               e.dma_start(
                        out=v_loc[tb + tt * 128: tb + (tt + 1) * 128,
                                  dvc * 512:(dvc + 1) * 512],
                        in_=stg[:, sp * TG:(sp + 1) * TG]), inc=(ksem, 16))
                    p.rel(f"stg{sp}", ksem, vst)
                p.rel(f"wb{sl}", pe, vmm)
            p.rel("xnrd", pe, pe.n)

        def xf_chunk(gidx):
            def xc(dc):
                return x_f[:, dc * TOK + gidx * TG: dc * TOK + gidx * TG + TG]
            return xc

        v_wo_done = None
        with ExitStack() as scA:
            EA = scA.enter_context
            tiles_sA = {
                "xn": EA(nc.sbuf_tensor("xn", [128, NDC * TG], BF16)),
                "xn2": EA(nc.sbuf_tensor("xn2", [128, NDC * TG], BF16)),
                "t1b": EA(nc.sbuf_tensor("t1b", [128, 2 * TG], BF16)),
                "t2b": EA(nc.sbuf_tensor("t2b", [128, 2 * TG], BF16)),
                "hdn": EA(nc.sbuf_tensor("hdn", [128, NIC * TG], BF16)),
                "xp": EA(nc.sbuf_tensor("xp", [128, NDC * TG], F16)),
            }
            tiles_sA["stg"] = tiles_sA["hdn"][:, 0:4 * TG]
            tiles_sA["gr"] = tiles_sA["hdn"][:, 4 * TG:8 * TG]
            lnb.clear()
            lnb["sqb"] = EA(nc.sbuf_tensor("sqbA", [128, TG], F16))
            lnb["rstd_bc"] = EA(nc.sbuf_tensor("rstdbcA", [128, TG], F16))
            lnb["mr_bc"] = EA(nc.sbuf_tensor("mrbcA", [128, TG], F16))
            lnb["xtmp"] = rw3[:, :]
            lnb["csb"] = tiles_sA["t2b"][:, 0:TG]
            lnb["snb"] = tiles_sA["t2b"][:, TG:2 * TG]
            xn = tiles_sA["xn"]
            xp = tiles_sA["xp"]

            v_ffn1 = {}
            xn2 = tiles_sA["xn2"]
            xbufs = [(xn, "xnA"), (xn2, "xnB")]
            v_ln = {0: emit_ln(xf_chunk(0), xn, xn_key="xnA",
                               x_ready=(xin, v_xin))}
            for gidx in range(4):
                xc = xf_chunk(gidx)
                buf, key = xbufs[gidx % 2]
                vh = emit_ffn_A(buf, v_ln[gidx], tiles_sA,
                                first=(gidx == 0), xn_key=key)
                if gidx + 1 < 4:
                    nbuf, nkey = xbufs[(gidx + 1) % 2]
                    v_ln[gidx + 1] = emit_ln(xf_chunk(gidx + 1), nbuf, xn_key=nkey)
                v_ffn1[gidx] = emit_ffn_B(vh, lambda oc, xc=xc: xc(oc),
                                          tiles_sA, first=(gidx == 0))[-1]

            if PHASES >= 2:
                def xp_chunk(dc):
                    return xp[:, dc * TG:(dc + 1) * TG]

                xv = x_in.rearrange("(dc p) t -> p dc t", p=128)
                for gidx in range(4):
                    p.use("g", "xp")
                    vxp = p.op("g", lambda e, gidx=gidx, xv=xv: e.dma_start(
                        out=xp[:, :],
                        in_=xv[:, :, TOK + gidx * TG: TOK + (gidx + 1) * TG]),
                        inc=(xin, 16))
                    p.wait("g", xin, vxp)
                    vxn_p1 = emit_ln(xp_chunk, xn, xn_key="xnA",
                                     x_ready=(xin, vxp))
                    vh = emit_ffn_A(xn, vxn_p1, tiles_sA, xn_key="xnA")
                    vxn_o = emit_ln(xf_chunk(gidx), xn2, xn_key="xnB",
                                    x_ready=(dve, v_ffn1[gidx]) if gidx < 4 else None)
                    v_fp_l = emit_ffn_B(vh, xp_chunk, tiles_sA)
                    vxn_p = emit_ln(xp_chunk, xn, xn_key="xnA",
                                    x_chunk_ready=[(dve, v) for v in v_fp_l])
                    p.rel("xp", dve, vxn_p)
                    tb_par = TOK + gidx * TG
                    tb_own = gidx * TG
                    emit_sem_proj(tiles_sA, xn, vxn_p, wks_bf, k_loc3, tb_par,
                                  first_w=v_cast["wks"] if gidx == 0 else None)
                    emit_geo_proj(tiles_sA, xn, vxn_p, wkg_bf, k_loc3, tb_par,
                                  first_w=v_cast["wkg"] if gidx == 0 else None)
                    emit_v_proj(tiles_sA, xn, vxn_p, tb_par,
                                first_w=v_cast["wv"] if gidx == 0 else None)
                    emit_sem_proj(tiles_sA, xn2, vxn_o, wks_bf, k_loc3, tb_own)
                    emit_geo_proj(tiles_sA, xn2, vxn_o, wkg_bf, k_loc3, tb_own)
                    emit_v_proj(tiles_sA, xn2, vxn_o, tb_own)
                    emit_sem_proj(tiles_sA, xn2, vxn_o, wqs_bf, q_loc3, tb_own,
                                  first_w=v_cast["wqs"] if gidx == 0 else None)
                    emit_geo_proj(tiles_sA, xn2, vxn_o, wqg_bf, q_loc3, tb_own,
                                  first_w=v_cast["wqg"] if gidx == 0 else None)

        if PHASES >= 3:
            with ExitStack() as scB:
                EB = scB.enter_context
                kh = EB(nc.sbuf_tensor("kh", [128, S], BF16))
                vhb = EB(nc.sbuf_tensor("vhb", [128, S], BF16))
                qh = EB(nc.sbuf_tensor("qh", [128, TOK], BF16))
                ptb = EB(nc.sbuf_tensor("ptb", [128, 4 * TG], BF16))
                lacc = EB(nc.sbuf_tensor("lacc", [128, TG], F32))
                linv_bc = EB(nc.sbuf_tensor("linv_bc", [128, TG], F32))
                avs = EB(nc.sbuf_tensor("avs", [128, TG], F32))
                ao = EB(nc.sbuf_tensor("ao", [128, H * TOK], BF16))

                p.use("v", "mA")
                p.use("v", "mB")
                vz1 = p.op("v", lambda e: e.memset(mga[:, :], 0.0), inc=(dve, 1))
                vz2 = p.op("v", lambda e: e.memset(mgb[:, :], 0.0), inc=(dve, 1))
                p.rel("mA", dve, vz1)
                p.rel("mB", dve, vz2)
                v_ks_all = [sm[f"ks{i}"].n for i in range(4)]
                pending = [None]

                def flush_pending():
                    if pending[0] is not None:
                        pending[0]()
                        pending[0] = None

                for h in range(H):
                    hs = h % 2
                    hsem = sm["h0"] if hs == 0 else sm["h1"]
                    if h < 2:
                        for i in range(4):
                            p.wait("y", sm[f"ks{i}"], v_ks_all[i])
                    p.use("y", "kh")
                    p.op("y", lambda e, h=h: e.dma_start(
                        out=kh[:, :], in_=k_loc[h * 128:(h + 1) * 128, :]),
                        inc=(hsem, 16))
                    p.use("y", "vh")
                    p.op("y", lambda e, h=h: e.dma_start(
                        out=vhb[:, :],
                        in_=v_loc3[:, :, h * 128:(h + 1) * 128]), inc=(hsem, 16))
                    p.use("y", "qh")
                    vldh = p.op("y", lambda e, h=h: e.dma_start(
                        out=qh[:, :], in_=q_loc[h * 128:(h + 1) * 128, :]),
                        inc=(hsem, 16))
                    p.wait("t", hsem, vldh)
                    vhv = vhb[:, :]
                    v_last_pe_h = None
                    for c in range(4):
                        ab = c % 2
                        pav = ppo[ab]
                        p.use("t", "pA" if ab == 0 else "pC")
                        p.use("t", "pD")
                        strips = ([(0, l) for l in range(4 * c + 4)]
                                  + [(1, l) for l in range(1, 4 * c + 4)]
                                  + [(1, 0)])
                        nst = len(strips)
                        nb = (nst + 1) // 2
                        binfo = []
                        v_lacc = None
                        for k in range(nb + 1):
                            if k < nb:
                                lo = 2 * k
                                hi = min(2 * k + 1, nst - 1)
                                mg = mga if k % 2 == 0 else mgb
                                mkey = "mA" if k % 2 == 0 else "mB"
                                p.use("t", mkey)
                                vs = None
                                for j in range(lo, hi + 1):
                                    sec, l = strips[j]
                                    off = max(l - 4 * c, 0) * 128
                                    w = TG - off
                                    jj = j - lo
                                    vs = p.op("t", lambda e, sec=sec, l=l, off=off,
                                              w=w, jj=jj, mg=mg, c=c: e.matmul(
                                        mg[:, jj * TG + off:jj * TG + off + w],
                                        kh[:, sec * TOK + l * 128:
                                           sec * TOK + (l + 1) * 128],
                                        qh[:, c * TG + off:c * TG + off + w],
                                        start=True, stop=True), inc=(pe, 1))
                                cnt = hi - lo + 1
                                p.use("s", f"ptb{k % 2}")
                                p.use("s", f"ptbV{k % 2}")
                                p.wait("s", pe, vs)
                                va = p.op("s", lambda e, k=k, cnt=cnt, mg=mg: e.activation(
                                    ptb[:, (k % 2) * 2 * TG:
                                        (k % 2) * 2 * TG + cnt * TG],
                                    mg[:, 0:cnt * TG], AF.Exp, scale=SCALE),
                                    inc=(act, 1))
                                p.rel(mkey, act, va)
                                vmask = None
                                for j in range(lo, hi + 1):
                                    sec, l = strips[j]
                                    if l >= 4 * c:
                                        joff = (l - 4 * c) * 128
                                        msk = m0_sb if sec == 0 else m1_sb
                                        base = (k % 2) * 2 * TG + (j - lo) * TG
                                        p.wait("v", act, va)
                                        vmask = p.op("v", lambda e, base=base,
                                                     joff=joff, msk=msk: e.tensor_mul(
                                            ptb[:, base + joff:base + joff + 128],
                                            ptb[:, base + joff:base + joff + 128],
                                            msk[:, :]), inc=(dve, 1))
                                for j in range(lo, hi + 1):
                                    sec, l = strips[j]
                                    joff = max(l - 4 * c, 0) * 128
                                    jw = TG - joff
                                    base = (k % 2) * 2 * TG + (j - lo) * TG
                                    if vmask is not None:
                                        p.wait("v", dve, vmask)
                                    else:
                                        p.wait("v", act, va)
                                    if v_lacc is not None:
                                        p.wait("v", dve, v_lacc)
                                    if j == 0:
                                        v_lacc = p.op("v", lambda e, base=base:
                                                      e.tensor_copy(
                                            lacc[:, :], ptb[:, base:base + TG]),
                                            inc=(dve, 1))
                                    else:
                                        v_lacc = p.op("v", lambda e, base=base,
                                                      joff=joff, jw=jw: e.tensor_add(
                                            lacc[:, joff:joff + jw],
                                            lacc[:, joff:joff + jw],
                                            ptb[:, base + joff:base + joff + jw]),
                                            inc=(dve, 1))
                                p.rel(f"ptbV{k % 2}", dve, v_lacc)
                                binfo.append((k, lo, hi, va, vmask))
                            if k >= 1:
                                if k == 1:
                                    flush_pending()
                                kk, lo, hi, va, vmask = binfo[k - 1]
                                if vmask is not None:
                                    p.wait("t", dve, vmask)
                                else:
                                    p.wait("t", act, va)
                                for j in range(lo, hi + 1):
                                    sec, l = strips[j]
                                    joff = max(l - 4 * c, 0) * 128
                                    jw = TG - joff
                                    base = (kk % 2) * 2 * TG + (j - lo) * TG
                                    vav = p.op("t", lambda e, sec=sec, l=l, joff=joff,
                                               jw=jw, base=base, vhv=vhv, pav=pav,
                                               stf=(j == 0), spf=(j == nst - 1):
                                               e.matmul(
                                        pav[:, joff:joff + jw],
                                        vhv[:, (sec * 16 + l) * 128:
                                            (sec * 16 + l + 1) * 128],
                                        ptb[:, base + joff:base + joff + jw],
                                        start=stf, stop=spf),
                                        inc=(pe, 1))
                                    v_last_pe_h = vav
                                p.rel(f"ptb{kk % 2}", pe, v_last_pe_h)
                        p.wait("t", dve, v_lacc)
                        vlp = p.op("t", lambda e: e.matmul(
                            pD[0:1, 0:TG], ones_f32[:, :], lacc[:, :],
                            start=True, stop=True), inc=(pe, 1))
                        p.use("v", "rowsD")
                        p.wait("v", pe, vlp)
                        vlc = p.op("v", lambda e: e.tensor_copy(
                            rows["l"], pD[0:1, 0:TG]), inc=(dve, 1))
                        p.wait("v", dve, vlc)
                        vli = p.op("v", lambda e: e.reciprocal(rows["li"], rows["l"]),
                                   inc=(dve, 1))
                        p.rel("pD", dve, vli)

                        def mk_pending(vli=vli, v_last=v_last_pe_h, pav=pav,
                                       ab=ab, h=h, c=c):
                            def emit():
                                p.use("t", "pB")
                                p.wait("t", dve, vli)
                                vbc = p.op("t", lambda e: e.matmul(
                                    pB[:, 0:TG], ones_r[:, :], rows["li"],
                                    start=True, stop=True), inc=(pe, 1))
                                p.rel("rowsD", pe, vbc)
                                p.use("s", "linvbc")
                                p.wait("s", pe, vbc)
                                va1 = p.op("s", lambda e: e.activation(
                                    linv_bc[:, :], pB[:, 0:TG], AF.Copy),
                                    inc=(act, 1))
                                p.rel("pB", act, va1)
                                p.use("s", "avs")
                                p.wait("s", pe, v_last)
                                va2 = p.op("s", lambda e, pav=pav: e.activation(
                                    avs[:, :], pav[:, 0:TG], AF.Copy), inc=(act, 1))
                                p.rel("pA" if ab == 0 else "pC", act, va2)
                                p.wait("v", act, va2)
                                vao = p.op("v", lambda e, h=h, c=c: e.tensor_mul(
                                    ao[:, h * TOK + c * TG: h * TOK + (c + 1) * TG],
                                    avs[:, :], linv_bc[:, :]), inc=(dve, 1))
                                p.rel("linvbc", dve, vao)
                                p.rel("avs", dve, vao)
                            return emit
                        pending[0] = mk_pending()
                    p.rel("kh", pe, v_last_pe_h)
                    p.rel("vh", pe, v_last_pe_h)
                    p.rel("qh", pe, v_last_pe_h)
                flush_pending()
                v_ao_done = dve.n

                v_res = None
                for ob in range(NDC):
                    sl = ob % 2
                    wsem, vw = load_w(sl,
                                      wo_bf[ob].rearrange("(ac p) o -> p ac o", p=128),
                                      NDC * 128,
                                      v_cast["wo"] if ob < 2 else None)
                    p.wait("t", wsem, vw)
                    if ob == 0:
                        p.wait("t", dve, v_ao_done)
                    vmm = None
                    for qc in range(4):
                        bp = qc % 2
                        po = ppo[bp]
                        p.use("t", "pA" if bp == 0 else "pC")
                        for adc in range(NDC):
                            vmm = p.op("t", lambda e, sl=sl, adc=adc, qc=qc, po=po:
                                       e.matmul(
                                po[:, 0:TG],
                                wb[sl][:, adc * 128:(adc + 1) * 128],
                                ao[:, adc * TOK + qc * TG: adc * TOK + (qc + 1) * TG],
                                start=(adc == 0), stop=(adc == NDC - 1)),
                                inc=(pe, 1) if adc == NDC - 1 else None)
                        p.use("s", "t3")
                        p.wait("s", pe, vmm)
                        va = p.op("s", lambda e, po=po: e.activation(
                            t3b[:, :], po[:, 0:TG], AF.Copy), inc=(act, 1))
                        p.rel("pA" if bp == 0 else "pC", act, va)
                        p.wait("v", act, va)
                        if v_res is not None:
                            p.wait("v", dve, v_res)
                        v_res = p.op("v", lambda e, ob=ob, qc=qc: e.tensor_add(
                            x_f[:, ob * TOK + qc * TG: ob * TOK + (qc + 1) * TG],
                            x_f[:, ob * TOK + qc * TG: ob * TOK + (qc + 1) * TG],
                            t3b[:, :]), inc=(dve, 1))
                        p.rel("t3", dve, v_res)
                    p.rel(f"wb{sl}", pe, vmm)
                v_wo_done = v_res

        if PHASES >= 4:
            with ExitStack() as scC:
                EC = scC.enter_context
                tiles_sC = {
                    "t1b": EC(nc.sbuf_tensor("t1c", [128, 2 * TG], BF16)),
                    "t2b": EC(nc.sbuf_tensor("t2c", [128, 2 * TG], BF16)),
                    "hdn": EC(nc.sbuf_tensor("hdnc", [128, NIC * TG], BF16)),
                }
                xn3 = EC(nc.sbuf_tensor("xn3", [128, NDC * TG], BF16))
                xn4 = EC(nc.sbuf_tensor("xn4", [128, NDC * TG], BF16))
                lnb.clear()
                lnb["sqb"] = EC(nc.sbuf_tensor("sqbC", [128, TG], F16))
                lnb["rstd_bc"] = EC(nc.sbuf_tensor("rstdbcC", [128, TG], F16))
                lnb["mr_bc"] = EC(nc.sbuf_tensor("mrbcC", [128, TG], F16))
                lnb["xtmp"] = rw3[:, :]
                xbufs4 = [(xn3, "xnA"), (xn4, "xnB")]
                v_ln4 = {0: emit_ln(xf_chunk(0), xn3, xn_key="xnA",
                                    x_ready=(dve, v_wo_done)
                                    if v_wo_done is not None else None)}
                def store_ob(ob, v_res):
                    p.wait("g", dve, v_res)
                    p.op("g", lambda e, ob=ob: e.dma_start(
                        out=x_out[ob * 128:(ob + 1) * 128, :],
                        in_=x_f[:, ob * TOK:(ob + 1) * TOK]), inc=(sm["out"], 16))

                for gidx in range(4):
                    xc = xf_chunk(gidx)
                    buf, key = xbufs4[gidx % 2]
                    vh = emit_ffn_A(buf, v_ln4[gidx], tiles_sC, xn_key=key)
                    if gidx + 1 < 4:
                        nbuf, nkey = xbufs4[(gidx + 1) % 2]
                        v_ln4[gidx + 1] = emit_ln(xf_chunk(gidx + 1), nbuf,
                                                  xn_key=nkey)
                    emit_ffn_B(vh, lambda oc, xc=xc: xc(oc), tiles_sC,
                               after_ob=store_ob if gidx == 3 else None)

        if PHASES < 4:
            v_final = dve.n
            p.wait("g", dve, v_final)
            for dc in range(NDC):
                p.op("g", lambda e, dc=dc: e.dma_start(
                    out=x_out[dc * 128:(dc + 1) * 128, :],
                    in_=x_f[:, dc * TOK:(dc + 1) * TOK]), inc=(sm["out"], 16))
        p.wait("g", sm["out"], sm["out"].n)

        with nc.Block() as block:
            @block.tensor
            def _(e):
                for fn in p.cmds["t"]:
                    fn(e)

            @block.vector
            def _(e):
                for fn in p.cmds["v"]:
                    fn(e)

            @block.scalar
            def _(e):
                for fn in p.cmds["s"]:
                    fn(e)

            @block.gpsimd
            def _(e):
                for fn in p.cmds["g"]:
                    fn(e)

            @block.sync
            def _(e):
                for fn in p.cmds["y"]:
                    fn(e)

    return nc


def _geo_perm():
    perm = np.zeros(1024, np.int64)
    for pos in range(1024):
        if pos < 512:
            h, j = pos // 32, pos % 32
            perm[pos] = h * 64 + j
        else:
            h, j = (pos - 512) // 32, (pos - 512) % 32
            perm[pos] = h * 64 + 32 + j
    return perm


def make_in_maps(L0_attn_x, Wq_sem, Wk_sem, Wq_geo, Wk_geo, Wv, Wo,
                 gate_w, up_w, down_w, gate_s, up_s, down_s):
    f32 = np.float32
    x = np.asarray(L0_attn_x, dtype=f32)
    perm = _geo_perm()
    shared = {
        "wqs_t": np.ascontiguousarray(np.asarray(Wq_sem, f32).T),
        "wks_t": np.ascontiguousarray(np.asarray(Wk_sem, f32).T),
        "wqg_t": np.ascontiguousarray(np.asarray(Wq_geo, f32).T[:, perm]),
        "wkg_t": np.ascontiguousarray(np.asarray(Wk_geo, f32).T[:, perm]),
        "wv_t": np.ascontiguousarray(np.asarray(Wv, f32).T),
        "wo_t": np.ascontiguousarray(np.asarray(Wo, f32).T),
        "gate_t": np.ascontiguousarray(np.asarray(gate_w, f32).T),
        "up_t": np.ascontiguousarray(np.asarray(up_w, f32).T),
        "down_t": np.ascontiguousarray(np.asarray(down_w, f32).T),
        "gs_in": np.ascontiguousarray(np.asarray(gate_s, f32).reshape(NIC, 128).T),
        "us_in": np.ascontiguousarray(np.asarray(up_s, f32).reshape(NIC, 128).T),
        "ds_in": np.ascontiguousarray(np.asarray(down_s, f32).reshape(NDC, 128).T),
    }
    tri = (np.arange(128)[:, None] <= np.arange(128)[None, :]).astype(f32)
    in_maps = []
    for r in range(8):
        b, par = r // 2, r % 2
        own = np.arange(par, 32, 2)
        partn = np.arange(1 - par, 32, 2)
        xs = x[b].reshape(32, 128, D)
        x_core = np.concatenate([xs[own], xs[partn]], axis=0).reshape(S, D)
        x_core = np.ascontiguousarray(x_core.T)
        tiles = np.concatenate([own, partn])
        pos = (tiles[:, None] * 128 + np.arange(128)[None, :]).reshape(-1)
        inv = 1.0 / (10000.0 ** (np.arange(0, 64, 2, dtype=np.float64) / 64.0))
        fr = pos[None, :].astype(np.float64) * inv[:, None]
        cos64 = np.tile(np.cos(fr), (4, 1))
        sin64 = np.tile(np.sin(fr), (4, 1))
        cos128 = cos64.astype(f32)
        sin128 = sin64.astype(f32)
        cosl = (cos64 - cos128).astype(f32)
        sinl = (sin64 - sin128).astype(f32)
        m1 = np.zeros((128, 128), f32) if par == 0 else np.ones((128, 128), f32)
        d = dict(shared)
        d.update({
            "x_in": x_core,
            "cos_in": np.ascontiguousarray(cos128),
            "sin_in": np.ascontiguousarray(sin128),
            "cosl_in": np.ascontiguousarray(cosl),
            "sinl_in": np.ascontiguousarray(sinl),
            "m0_in": tri, "m1_in": m1,
        })
        in_maps.append(d)
    return in_maps


def assemble_output(results):
    out = np.zeros((4, S, D), np.float32)
    for r in range(8):
        b, par = r // 2, r % 2
        own = np.arange(par, 32, 2)
        xo = np.asarray(results[r]["x_out"], dtype=np.float32)
        out[b].reshape(32, 128, D)[own] = xo.T.reshape(16, 128, D)
    return out


def kernel(L0_attn_x, L0_attn_Wq_sem, L0_attn_Wk_sem, L0_attn_Wq_geo,
           L0_attn_Wk_geo, L0_attn_Wv, L0_attn_Wo, L0_ffn_gate_w,
           L0_ffn_up_w, L0_ffn_down_w, L0_ffn_gate_s, L0_ffn_up_s,
           L0_ffn_down_s):
    in_maps = make_in_maps(L0_attn_x, L0_attn_Wq_sem, L0_attn_Wk_sem,
                           L0_attn_Wq_geo, L0_attn_Wk_geo, L0_attn_Wv,
                           L0_attn_Wo, L0_ffn_gate_w, L0_ffn_up_w,
                           L0_ffn_down_w, L0_ffn_gate_s, L0_ffn_up_s,
                           L0_ffn_down_s)
    nc = build_program()
    res = run_bass_kernel_spmd(nc, in_maps, core_ids=list(range(8)))
    return assemble_output(res.results)


# version 10
# speedup vs baseline: 1.0539x; 1.0462x over previous
import math
import os

import numpy as np

import concourse.bass as bass
import concourse.mybir as mybir
from concourse.bass_utils import run_bass_kernel_spmd

F32 = mybir.dt.float32
F16 = mybir.dt.float16
BF16 = mybir.dt.bfloat16
AF = mybir.ActivationFunctionType
ALU = mybir.AluOpType

D = 2048
INTER = 5632
H = 16
S = 4096
TOK = 2048
TG = 512
NDC = D // 128
NIC = INTER // 128
IBLK = 256
NIB = INTER // IBLK
SCALE = 1.0 / math.sqrt(128.0)
EPS = 1e-5

PHASES = int(os.environ.get("MK_PHASES", "4"))


class Sem:
    __slots__ = ("h", "n")

    def __init__(self, handle):
        self.h = handle
        self.n = 0


class B:
    def __init__(self):
        self.cmds = {e: [] for e in ("t", "v", "s", "g", "y")}
        self.guard = {}

    def op(self, eng, builder, inc=None):
        if inc is None:
            self.cmds[eng].append(lambda e, b=builder: b(e))
            return None
        sem, by = inc
        sem.n += by
        self.cmds[eng].append(lambda e, b=builder, h=sem.h, k=by: b(e).then_inc(h, k))
        return sem.n

    def wait(self, eng, sem, val):
        assert val <= sem.n, f"wait on future value {val} > {sem.n}"
        self.cmds[eng].append(lambda e, h=sem.h, v=val: e.wait_ge(h, v))

    def use(self, eng, key):
        if key in self.guard:
            s, v = self.guard[key]
            self.wait(eng, s, v)

    def rel(self, key, sem, val):
        self.guard[key] = (sem, val)


def build_program():
    nc = bass.Bass("TRN2", target_bir_lowering=False)

    x_in = nc.dram_tensor("x_in", [D, S], F32, kind="ExternalInput").ap()
    gate_t = nc.dram_tensor("gate_t", [D, INTER], F32, kind="ExternalInput").ap()
    up_t = nc.dram_tensor("up_t", [D, INTER], F32, kind="ExternalInput").ap()
    down_t = nc.dram_tensor("down_t", [INTER, D], F32, kind="ExternalInput").ap()
    wqs_t = nc.dram_tensor("wqs_t", [D, 1024], F32, kind="ExternalInput").ap()
    wks_t = nc.dram_tensor("wks_t", [D, 1024], F32, kind="ExternalInput").ap()
    wqg_t = nc.dram_tensor("wqg_t", [D, 1024], F32, kind="ExternalInput").ap()
    wkg_t = nc.dram_tensor("wkg_t", [D, 1024], F32, kind="ExternalInput").ap()
    wv_t = nc.dram_tensor("wv_t", [D, D], F32, kind="ExternalInput").ap()
    wo_t = nc.dram_tensor("wo_t", [D, D], F32, kind="ExternalInput").ap()
    gs_in = nc.dram_tensor("gs_in", [128, NIC], F32, kind="ExternalInput").ap()
    us_in = nc.dram_tensor("us_in", [128, NIC], F32, kind="ExternalInput").ap()
    ds_in = nc.dram_tensor("ds_in", [128, NDC], F32, kind="ExternalInput").ap()
    cos_in = nc.dram_tensor("cos_in", [128, S], F32, kind="ExternalInput").ap()
    sin_in = nc.dram_tensor("sin_in", [128, S], F32, kind="ExternalInput").ap()
    cosl_in = nc.dram_tensor("cosl_in", [128, S], F32, kind="ExternalInput").ap()
    sinl_in = nc.dram_tensor("sinl_in", [128, S], F32, kind="ExternalInput").ap()
    m0_in = nc.dram_tensor("m0_in", [128, 128], F32, kind="ExternalInput").ap()
    m1_in = nc.dram_tensor("m1_in", [128, 128], F32, kind="ExternalInput").ap()

    x_out = nc.dram_tensor("x_out", [D, TOK], F32, kind="ExternalOutput").ap()

    gate_bf = nc.dram_tensor("gate_bf", [NIB, D, IBLK], BF16, kind="Internal").ap()
    up_bf = nc.dram_tensor("up_bf", [NIB, D, IBLK], BF16, kind="Internal").ap()
    down_bf = nc.dram_tensor("down_bf", [NDC, INTER, 128], BF16, kind="Internal").ap()
    wqs_bf = nc.dram_tensor("wqs_bf", [8, D, 128], BF16, kind="Internal").ap()
    wks_bf = nc.dram_tensor("wks_bf", [8, D, 128], BF16, kind="Internal").ap()
    wqg_bf = nc.dram_tensor("wqg_bf", [8, D, 128], BF16, kind="Internal").ap()
    wkg_bf = nc.dram_tensor("wkg_bf", [8, D, 128], BF16, kind="Internal").ap()
    wv_bf = nc.dram_tensor("wv_bf", [4, D, 512], BF16, kind="Internal").ap()
    wo_bf = nc.dram_tensor("wo_bf", [NDC, D, 128], BF16, kind="Internal").ap()
    cos_bf = nc.dram_tensor("cos_bf", [128, S], BF16, kind="Internal").ap()
    sin_bf = nc.dram_tensor("sin_bf", [128, S], BF16, kind="Internal").ap()
    k_loc = nc.dram_tensor("k_loc", [D, S], BF16, kind="Internal").ap()
    v_loc = nc.dram_tensor("v_loc", [S, D], BF16, kind="Internal").ap()
    q_loc = nc.dram_tensor("q_loc", [D, TOK], BF16, kind="Internal").ap()

    k_loc3 = k_loc.rearrange("(h r) t -> h r t", r=128)
    q_loc3 = q_loc.rearrange("(h r) t -> h r t", r=128)
    v_loc3 = v_loc.rearrange("(kt p) dv -> p kt dv", p=128)

    from contextlib import ExitStack
    with ExitStack() as top:
        E = top.enter_context
        p = B()

        sem_names = ("pe", "dve", "act", "w0", "w1", "u0", "u1", "cast", "xin",
                     "con", "rt", "tb0", "ks0", "ks1", "ks2", "ks3", "h0", "h1",
                     "out", "kq0", "kq1")
        sm = {n: Sem(E(nc.semaphore(n))) for n in sem_names}
        pe, dve, act = sm["pe"], sm["dve"], sm["act"]

        x_f = E(nc.sbuf_tensor("x_f", [128, NDC * TOK], F16))
        wbig = E(nc.sbuf_tensor("wbig", [128, 16384], BF16))
        gs_sb = E(nc.sbuf_tensor("gs_sb", [128, NIC], F32))
        us_sb = E(nc.sbuf_tensor("us_sb", [128, NIC], F32))
        ds_sb = E(nc.sbuf_tensor("ds_sb", [128, NDC], F32))
        dsh_sb = E(nc.sbuf_tensor("dsh_sb", [128, NDC], F32))
        m0_sb = E(nc.sbuf_tensor("m0_sb", [128, 128], BF16))
        m1_sb = E(nc.sbuf_tensor("m1_sb", [128, 128], BF16))
        ones_bf = E(nc.sbuf_tensor("ones_bf", [128, 1], BF16))
        ones_f16 = E(nc.sbuf_tensor("ones_f16", [128, 1], F16))
        ones_r = E(nc.sbuf_tensor("ones_r", [1, 128], F32))
        ones_f32 = E(nc.sbuf_tensor("ones_f32", [128, 1], F32))
        rw1 = E(nc.sbuf_tensor("rw1", [128, TG], F32))
        rw2 = E(nc.sbuf_tensor("rw2", [128, TG], F32))
        rw3 = E(nc.sbuf_tensor("rw3", [128, TG], F32))
        rw4 = E(nc.sbuf_tensor("rw4", [128, TG], F32))
        rw5 = E(nc.sbuf_tensor("rw5", [128, TG], F32))
        rows = {
            "mean": rw1[0:1, :], "sx": rw1[32:33, :], "sq": rw1[64:65, :],
            "e2": rw2[0:1, :], "var": rw2[32:33, :], "inv": rw2[64:65, :],
            "msq": rw3[0:1, :],
            "rstd": rw4[0:1, :],
            "mr": rw5[0:1, :], "l": rw5[32:33, :],
            "li": rw4[0:1, :],
        }
        t3b = E(nc.sbuf_tensor("t3b", [128, TG], BF16))
        lnb = {}

        wb = [wbig[:, 0:8192], wbig[:, 8192:16384]]
        pslq = [wbig[:, 0:2048], wbig[:, 2048:4096],
                wbig[:, 8192:10240], wbig[:, 10240:12288]]
        wstate = {"mode": "half"}


        pA = E(nc.psum_tensor("pA", [128, 512], F32))
        pB = E(nc.psum_tensor("pB", [128, 512], F32))
        pC = E(nc.psum_tensor("pC", [128, 512], F32))
        pD = E(nc.psum_tensor("pD", [128, 512], F32))
        mga = E(nc.psum_tensor("mga", [128, 1024], F32))
        mgb = E(nc.psum_tensor("mgb", [128, 1024], F32))
        pgu = [pA, pB, pC, pD]
        ppo = [pA, pC]

        con, xin, cast = sm["con"], sm["xin"], sm["cast"]
        for dst, src in ((gs_sb, gs_in), (us_sb, us_in), (ds_sb, ds_in)):
            p.op("y", lambda e, d=dst, s=src: e.dma_start(out=d[:, :], in_=s),
                 inc=(con, 16))
        v_con = con.n
        for dc in range(NDC):
            p.op("g", lambda e, dc=dc: e.dma_start(
                out=x_f[:, dc * TOK:(dc + 1) * TOK],
                in_=x_in[dc * 128:(dc + 1) * 128, 0:TOK]), inc=(xin, 16))
        v_xin = xin.n
        p.wait("g", xin, v_xin)

        def cast_dma(dst, src):
            v = p.op("g", lambda e, d=dst, s=src: e.dma_start(out=d, in_=s),
                     inc=(cast, 16))
            p.wait("g", cast, v)
            return v

        def cast_blocks(dst, src3, nblk):
            out = []
            for ib in range(nblk):
                out.append(cast_dma(dst[ib], src3[ib]))
            return out

        cast_dma(m0_sb[:, :], m0_in)
        cast_dma(m1_sb[:, :], m1_in)
        v_cast = {}
        gate_src = gate_t.rearrange("d (ib i) -> ib d i", i=IBLK)
        up_src = up_t.rearrange("d (ib i) -> ib d i", i=IBLK)
        v_cast["gu_ib"] = []
        for ib in range(NIB):
            cast_dma(gate_bf[ib], gate_src[ib])
            v_cast["gu_ib"].append(cast_dma(up_bf[ib], up_src[ib]))
        v_cast["down_ob"] = cast_blocks(
            down_bf, down_t.rearrange("i (ob o) -> ob i o", o=128), NDC)
        v_cast["gate"] = v_cast["up"] = v_cast["gu_ib"][-1]
        v_cast["down"] = v_cast["down_ob"][-1]
        for nm, dst, src, nblk in (
            ("wks", wks_bf, wks_t.rearrange("d (oc o) -> oc d o", o=128), 8),
            ("wkg", wkg_bf, wkg_t.rearrange("d (oc o) -> oc d o", o=128), 8),
            ("wv", wv_bf, wv_t.rearrange("d (oc o) -> oc d o", o=512), 4),
            ("wqs", wqs_bf, wqs_t.rearrange("d (oc o) -> oc d o", o=128), 8),
            ("wqg", wqg_bf, wqg_t.rearrange("d (oc o) -> oc d o", o=128), 8),
            ("wo", wo_bf, wo_t.rearrange("d (oc o) -> oc d o", o=128), NDC),
        ):
            v_cast[nm] = cast_blocks(dst, src, nblk)[-1]
        with ExitStack() as sc0:
            th = sc0.enter_context(nc.sbuf_tensor("tbl_hi", [128, S], F32))
            tl = sc0.enter_context(nc.sbuf_tensor("tbl_lo", [128, S], F32))
            for dst_bf, hi_src, lo_src in ((cos_bf, cos_in, cosl_in),
                                           (sin_bf, sin_in, sinl_in)):
                v1 = p.op("y", lambda e, s=hi_src: e.dma_start(out=th[:, :], in_=s),
                          inc=(sm["tb0"], 16))
                v2 = p.op("y", lambda e, s=lo_src: e.dma_start(out=tl[:, :], in_=s),
                          inc=(sm["tb0"], 16))
                p.wait("v", sm["tb0"], v2)
                p.use("v", "tbl")
                v3 = p.op("v", lambda e: e.tensor_add(th[:, :], th[:, :], tl[:, :]),
                          inc=(dve, 1))
                p.wait("g", dve, v3)
                v_cast[("tbl", id(dst_bf))] = p.op(
                    "g", lambda e, d=dst_bf: e.dma_start(out=d, in_=th[:, :]),
                    inc=(cast, 16))
                p.wait("g", cast, cast.n)
                p.rel("tbl", cast, cast.n)
                p.wait("y", cast, cast.n)
        v_cast["cos"] = v_cast["sin"] = cast.n
        p.wait("v", cast, cast.n)
        p.wait("s", cast, cast.n)
        p.wait("g", cast, cast.n)

        p.op("v", lambda e: e.memset(mga[:, :], 0.0), inc=(dve, 1))
        p.op("v", lambda e: e.memset(mgb[:, :], 0.0), inc=(dve, 1))
        p.op("v", lambda e: e.memset(ones_bf[:, :], 1.0), inc=(dve, 1))
        p.op("v", lambda e: e.memset(ones_f16[:, :], 1.0), inc=(dve, 1))
        p.op("v", lambda e: e.memset(ones_r[:, :], 1.0), inc=(dve, 1))
        p.op("v", lambda e: e.memset(ones_f32[:, :], 1.0), inc=(dve, 1))
        p.wait("v", con, v_con)
        v_ones = p.op("v", lambda e: e.tensor_scalar_mul(
            dsh_sb[:, :], ds_sb[:, :], 0.5), inc=(dve, 1))

        qsems = [sm["w0"], sm["w1"], sm["u0"], sm["u1"]]

        def wmode(m):
            if wstate["mode"] == m:
                return
            if m == "quarter":
                p.use("y", "wb0")
                p.use("y", "wb1")
            else:
                for j in range(4):
                    p.use("y", f"pw{j}")
            wstate["mode"] = m

        def emit_ln(xc, dst_xn, xn_key="xnrd", x_ready=None, x_chunk_ready=None):
            sqb = lnb["sqb"]
            rstd_bc = lnb["rstd_bc"]
            mr_bc = lnb["mr_bc"]
            xtmp = lnb["xtmp"]
            sxp = mga[0:1, 0:TG]
            sqp = mga[0:1, TG:2 * TG]
            bc1 = mgb[:, 0:TG]
            bc2 = mgb[:, TG:2 * TG]
            p.use("t", "mA")
            p.wait("t", dve, v_ones)
            if x_ready is not None:
                p.wait("t", x_ready[0], x_ready[1])
                p.wait("s", x_ready[0], x_ready[1])
            for dc in range(NDC):
                if x_chunk_ready is not None:
                    p.wait("t", x_chunk_ready[dc][0], x_chunk_ready[dc][1])
                p.op("t", lambda e, dc=dc, sxp=sxp: e.matmul(
                    sxp, ones_f16[:, :], xc(dc),
                    start=(dc == 0), stop=(dc == NDC - 1)))
            p.use("s", "sqb")
            prev_pe = None
            for dc in range(NDC):
                if x_chunk_ready is not None:
                    p.wait("s", x_chunk_ready[dc][0], x_chunk_ready[dc][1])
                if prev_pe is not None:
                    p.wait("s", pe, prev_pe)
                va = p.op("s", lambda e, dc=dc, sqb=sqb: e.activation(
                    sqb[:, :], xc(dc), AF.Square), inc=(act, 1))
                p.wait("t", act, va)
                prev_pe = p.op("t", lambda e, dc=dc, sqb=sqb, sqp=sqp: e.matmul(
                    sqp, ones_f16[:, :], sqb[:, :],
                    start=(dc == 0), stop=(dc == NDC - 1)), inc=(pe, 1))
            p.rel("sqb", pe, prev_pe)
            v_stats = pe.n
            p.use("v", "rowsABC")
            p.wait("v", pe, v_stats)
            p.op("v", lambda e, sxp=sxp: e.tensor_copy(rows["sx"], sxp))
            v_ev = p.op("v", lambda e, sqp=sqp: e.tensor_copy(rows["sq"], sqp),
                        inc=(dve, 1))
            p.rel("mA", dve, v_ev)
            p.wait("s", dve, v_ev)
            p.op("s", lambda e: e.activation(rows["mean"], rows["sx"], AF.Copy,
                                             scale=1.0 / D))
            v_a1 = p.op("s", lambda e: e.activation(rows["e2"], rows["sq"], AF.Copy,
                                                    scale=1.0 / D), inc=(act, 1))
            p.wait("s", act, v_a1)
            v_a2 = p.op("s", lambda e: e.activation(rows["msq"], rows["mean"],
                                                    AF.Square), inc=(act, 1))
            p.wait("v", act, v_a2)
            v_sub = p.op("v", lambda e: e.tensor_sub(
                rows["var"], rows["e2"], rows["msq"]), inc=(dve, 1))
            p.wait("v", dve, v_sub)
            v_d0 = p.op("v", lambda e: e.tensor_scalar_add(
                rows["var"], rows["var"], EPS), inc=(dve, 1))
            p.wait("v", dve, v_d0)
            v_d1 = p.op("v", lambda e: e.reciprocal(rows["inv"], rows["var"]),
                        inc=(dve, 1))
            p.wait("s", dve, v_d1)
            v_a3 = p.op("s", lambda e: e.activation(rows["rstd"], rows["inv"],
                                                    AF.Sqrt), inc=(act, 1))
            p.wait("v", act, v_a3)
            v_d2 = p.op("v", lambda e: e.tensor_mul(
                rows["mr"], rows["mean"], rows["rstd"]), inc=(dve, 1))
            p.use("t", "mB")
            p.wait("t", dve, v_d2)
            p.op("t", lambda e, bc1=bc1: e.matmul(bc1, ones_r[:, :], rows["rstd"],
                                                  start=True, stop=True))
            v_bc = p.op("t", lambda e, bc2=bc2: e.matmul(
                bc2, ones_r[:, :], rows["mr"], start=True, stop=True),
                inc=(pe, 1))
            p.rel("rowsABC", pe, v_bc)
            p.use("s", "bcb")
            p.wait("s", pe, v_bc)
            p.op("s", lambda e, rstd_bc=rstd_bc, bc1=bc1: e.activation(
                rstd_bc[:, :], bc1, AF.Copy))
            v_bce = p.op("s", lambda e, mr_bc=mr_bc, bc2=bc2: e.activation(
                mr_bc[:, :], bc2, AF.Copy), inc=(act, 1))
            p.rel("mB", act, v_bce)
            p.wait("v", act, v_bce)
            p.use("v", xn_key)
            v_last = None
            prev = v_d2
            for dc in range(NDC):
                if prev is not None:
                    p.wait("v", dve, prev)
                v_mul = p.op("v", lambda e, dc=dc, xtmp=xtmp, rstd_bc=rstd_bc:
                     e.tensor_mul(xtmp[:, :], xc(dc), rstd_bc[:, :]), inc=(dve, 1))
                p.wait("v", dve, v_mul)
                v_last = p.op("v", lambda e, dc=dc, xtmp=xtmp, mr_bc=mr_bc:
                              e.tensor_sub(
                    dst_xn[:, dc * TG:(dc + 1) * TG], xtmp[:, :], mr_bc[:, :]),
                    inc=(dve, 1))
                prev = v_last
            p.rel("bcb", dve, v_last)
            return v_last

        def emit_ffn_A(src_xn, v_xn_ready, tiles, first=False, xn_key="xnrd"):
            t1b, t2b, hdn = tiles["t1b"], tiles["t2b"], tiles["hdn"]
            wmode("half")
            last_pe = None
            for ib in range(NIB):
                sl = ib % 2
                wsem = sm["w0"] if sl == 0 else sm["w1"]
                usem = sm["u0"] if sl == 0 else sm["u1"]
                p.use("y", f"wb{sl}")
                if first:
                    p.wait("y", cast, v_cast["gu_ib"][ib])
                vw = p.op("y", lambda e, ib=ib, sl=sl: e.dma_start(
                    out=wb[sl][:, 0:NDC * IBLK],
                    in_=gate_bf[ib].rearrange("(dc p) i -> p dc i", p=128)),
                    inc=(wsem, 16))
                vu = p.op("y", lambda e, ib=ib, sl=sl: e.dma_start(
                    out=wb[sl][:, NDC * IBLK:2 * NDC * IBLK],
                    in_=up_bf[ib].rearrange("(dc p) i -> p dc i", p=128)),
                    inc=(usem, 16))
                p.wait("t", wsem, vw)
                p.wait("t", usem, vu)
                if ib == 0:
                    p.wait("t", dve, v_xn_ready)
                for hh in range(2):
                    ic = ib * 2 + hh
                    bp = ic % 2
                    pg = pgu[2 * bp]
                    pu = pgu[2 * bp + 1]
                    p.use("t", "pA" if bp == 0 else "pC")
                    p.use("t", "pB" if bp == 0 else "pD")
                    for dc in range(NDC):
                        p.op("t", lambda e, sl=sl, dc=dc, hh=hh, pg=pg: e.matmul(
                            pg[:, 0:TG],
                            wb[sl][:, dc * IBLK + hh * 128: dc * IBLK + (hh + 1) * 128],
                            src_xn[:, dc * TG:(dc + 1) * TG],
                            start=(dc == 0), stop=(dc == NDC - 1)))
                    for dc in range(NDC):
                        last_pe = p.op("t", lambda e, sl=sl, dc=dc, hh=hh, pu=pu: e.matmul(
                            pu[:, 0:TG],
                            wb[sl][:, NDC * IBLK + dc * IBLK + hh * 128:
                                    NDC * IBLK + dc * IBLK + (hh + 1) * 128],
                            src_xn[:, dc * TG:(dc + 1) * TG],
                            start=(dc == 0), stop=(dc == NDC - 1)),
                            inc=(pe, 1) if dc == NDC - 1 else None)
                    tp = ic % 2
                    p.use("s", f"t12_{tp}")
                    p.wait("s", pe, last_pe)
                    p.op("s", lambda e, ic=ic, tp=tp, pg=pg: e.activation(
                        t1b[:, tp * TG:(tp + 1) * TG], pg[:, 0:TG], AF.Silu,
                        scale=gs_sb[:, ic:ic + 1]))
                    va = p.op("s", lambda e, ic=ic, tp=tp, pu=pu: e.activation(
                        t2b[:, tp * TG:(tp + 1) * TG], pu[:, 0:TG], AF.Copy,
                        scale=us_sb[:, ic:ic + 1]), inc=(act, 1))
                    p.rel("pA" if bp == 0 else "pC", act, va)
                    p.rel("pB" if bp == 0 else "pD", act, va)
                    p.wait("v", act, va)
                    if ic < 4:
                        p.use("v", f"stg{ic}")
                    elif ic < 8:
                        p.use("v", f"gr{ic - 4}")
                    vh_ = p.op("v", lambda e, ic=ic, tp=tp: e.tensor_mul(
                        hdn[:, ic * TG:(ic + 1) * TG],
                        t1b[:, tp * TG:(tp + 1) * TG],
                        t2b[:, tp * TG:(tp + 1) * TG]), inc=(dve, 1))
                    p.rel(f"t12_{tp}", dve, vh_)
                p.rel(f"wb{sl}", pe, last_pe)
            p.rel(xn_key, pe, last_pe)
            return dve.n

        def emit_ffn_B(v_hdn_done, xdst, tiles, first=False, after_ob=None):
            hdn = tiles["hdn"]
            wmode("half")
            v_res = None
            rlist = []
            for ob in range(NDC):
                sl = ob % 2
                wsem = sm["w0"] if sl == 0 else sm["w1"]
                p.use("y", f"wb{sl}")
                if first:
                    p.wait("y", cast, v_cast["down_ob"][ob])
                vw = p.op("y", lambda e, ob=ob, sl=sl: e.dma_start(
                    out=wb[sl][:, 0:NIC * 128],
                    in_=down_bf[ob].rearrange("(ic p) o -> p ic o", p=128)),
                    inc=(wsem, 16))
                p.wait("t", wsem, vw)
                if ob == 0:
                    p.wait("t", dve, v_hdn_done)
                bp = ob % 2
                po = ppo[bp]
                p.use("t", "pA" if bp == 0 else "pC")
                vmm = None
                for ic in range(NIC):
                    vmm = p.op("t", lambda e, sl=sl, ic=ic, po=po: e.matmul(
                        po[:, 0:TG], wb[sl][:, ic * 128:(ic + 1) * 128],
                        hdn[:, ic * TG:(ic + 1) * TG],
                        start=(ic == 0), stop=(ic == NIC - 1)),
                        inc=(pe, 1) if ic == NIC - 1 else None)
                p.rel(f"wb{sl}", pe, vmm)
                p.use("s", "t3")
                p.wait("s", pe, vmm)
                va = p.op("s", lambda e, ob=ob, po=po: e.activation(
                    t3b[:, :], po[:, 0:TG], AF.Copy,
                    scale=dsh_sb[:, ob:ob + 1]), inc=(act, 1))
                p.rel("pA" if bp == 0 else "pC", act, va)
                p.wait("v", act, va)
                if v_res is not None:
                    p.wait("v", dve, v_res)
                v_res = p.op("v", lambda e, ob=ob: e.tensor_add(
                    xdst(ob), xdst(ob), t3b[:, :]), inc=(dve, 1))
                p.rel("t3", dve, v_res)
                rlist.append(v_res)
                if after_ob is not None:
                    after_ob(ob, v_res)
            return rlist

        def emit_ffn(src_xn, v_xn_ready, xdst, tiles, first=False, xn_key="xnrd"):
            vh = emit_ffn_A(src_xn, v_xn_ready, tiles, first=first, xn_key=xn_key)
            return emit_ffn_B(vh, xdst, tiles, first=first)[-1]

        def load_w(sl, src_ap, ncols, pre_wait=None):
            wsem = sm["w0"] if sl == 0 else sm["w1"]
            p.use("y", f"wb{sl}")
            if pre_wait is not None:
                p.wait("y", cast, pre_wait)
            vw = p.op("y", lambda e, s=src_ap, sl=sl, n=ncols: e.dma_start(
                out=wb[sl][:, 0:n], in_=s), inc=(wsem, 16))
            return wsem, vw

        def proj_mm4(sl, src_xn, v_xn_ready, first_mm, bp):
            pk = ppo[bp]
            p.use("t", "pA" if bp == 0 else "pC")
            if first_mm:
                p.wait("t", dve, v_xn_ready)
            vmm = None
            for dc in range(NDC):
                vmm = p.op("t", lambda e, sl=sl, dc=dc, pk=pk: e.matmul(
                    pk[:, 0:TG], pslq[sl][:, dc * 128:(dc + 1) * 128],
                    src_xn[:, dc * TG:(dc + 1) * TG],
                    start=(dc == 0), stop=(dc == NDC - 1)),
                    inc=(pe, 1) if dc == NDC - 1 else None)
            return pk, vmm

        def proj_mm(sl, src_xn, v_xn_ready, first_mm, bp):
            pk = ppo[bp]
            p.use("t", "pA" if bp == 0 else "pC")
            if first_mm:
                p.wait("t", dve, v_xn_ready)
            vmm = None
            for dc in range(NDC):
                vmm = p.op("t", lambda e, sl=sl, dc=dc, pk=pk: e.matmul(
                    pk[:, 0:TG], wb[sl][:, dc * 128:(dc + 1) * 128],
                    src_xn[:, dc * TG:(dc + 1) * TG],
                    start=(dc == 0), stop=(dc == NDC - 1)),
                    inc=(pe, 1) if dc == NDC - 1 else None)
            return pk, vmm

        def emit_sem_proj(tiles, src_xn, v_xn_ready, w_bf, dst3, tbase,
                          first_w=None):
            stg = tiles["stg"]
            wmode("quarter")
            for oc in range(8):
                sl = oc % 4
                wsem = qsems[sl]
                p.use("y", f"pw{sl}")
                if first_w is not None and oc < 4:
                    p.wait("y", cast, first_w)
                vw = p.op("y", lambda e, oc=oc, sl=sl, wb_=w_bf: e.dma_start(
                    out=pslq[sl][:, :],
                    in_=wb_[oc].rearrange("(dc p) o -> p dc o", p=128)),
                    inc=(wsem, 16))
                p.wait("t", wsem, vw)
                pk, vmm = proj_mm4(sl, src_xn, v_xn_ready, oc == 0, oc % 2)
                p.rel(f"pw{sl}", pe, vmm)
                sp = oc % 4
                ksem = sm[f"ks{sp}"]
                p.use("v", f"stg{sp}")
                p.wait("v", pe, vmm)
                vd = p.op("v", lambda e, sp=sp, pk=pk, stg=stg: e.tensor_copy(
                    stg[:, sp * TG:(sp + 1) * TG], pk[:, 0:TG]), inc=(dve, 1))
                p.rel("pA" if oc % 2 == 0 else "pC", dve, vd)
                p.wait("g", dve, vd)
                vst = p.op("g", lambda e, oc=oc, sp=sp, d3=dst3, tb=tbase, stg=stg:
                           e.dma_start(
                    out=d3[2 * oc:2 * oc + 2, 0:64, tb:tb + TG],
                    in_=stg[:, sp * TG:(sp + 1) * TG]), inc=(ksem, 16))
                p.rel(f"stg{sp}", ksem, vst)
            p.rel("xnrd", pe, pe.n)

        def emit_geo_proj(tiles, src_xn, v_xn_ready, w_bf, dst3, tbase,
                          first_w=None):
            stg, gr = tiles["stg"], tiles["gr"]
            ra = tiles["t1b"][:, 0:TG]
            rb = tiles["t1b"][:, TG:2 * TG]
            csb, snb = lnb["csb"], lnb["snb"]
            wmode("quarter")
            p.use("g", "ropetbl")
            p.use("g", "t12_0")
            p.use("g", "t12_1")
            p.op("g", lambda e, tb=tbase, csb=csb: e.dma_start(
                out=csb[:, :], in_=cos_bf[:, tb:tb + TG]), inc=(sm["rt"], 16))
            v_rt = p.op("g", lambda e, tb=tbase, snb=snb: e.dma_start(
                out=snb[:, :], in_=sin_bf[:, tb:tb + TG]), inc=(sm["rt"], 16))
            order = [0, 4, 1, 5, 2, 6, 3, 7]
            for idx, oc in enumerate(order):
                sl = idx % 4
                wsem = qsems[sl]
                p.use("y", f"pw{sl}")
                if first_w is not None and idx < 4:
                    p.wait("y", cast, first_w)
                vw = p.op("y", lambda e, oc=oc, sl=sl, wb_=w_bf: e.dma_start(
                    out=pslq[sl][:, :],
                    in_=wb_[oc].rearrange("(dc p) o -> p dc o", p=128)),
                    inc=(wsem, 16))
                p.wait("t", wsem, vw)
                pk, vmm = proj_mm4(sl, src_xn, v_xn_ready, idx == 0, idx % 2)
                p.rel(f"pw{sl}", pe, vmm)
                gslot = (idx // 2) % 2 + (2 if oc >= 4 else 0)
                p.use("v", f"gr{gslot}")
                p.wait("v", pe, vmm)
                vd = p.op("v", lambda e, g=gslot, pk=pk, gr=gr: e.tensor_copy(
                    gr[:, g * TG:(g + 1) * TG], pk[:, 0:TG]), inc=(dve, 1))
                p.rel("pA" if idx % 2 == 0 else "pC", dve, vd)
                if oc >= 4:
                    c = oc - 4
                    s1 = (idx // 2) % 2
                    s2 = s1 + 2
                    g1 = gr[:, s1 * TG:(s1 + 1) * TG]
                    g2 = gr[:, s2 * TG:(s2 + 1) * TG]
                    sp = c % 2
                    ksemA = sm[f"ks{2 * sp}"]
                    ksemB = sm[f"ks{2 * sp + 1}"]
                    p.use("v", f"stg{2 * sp}")
                    p.use("v", f"stg{2 * sp + 1}")
                    p.use("v", "t12_0")
                    p.use("v", "t12_1")
                    p.wait("v", dve, vd)
                    p.wait("v", sm["rt"], v_rt)
                    p.op("v", lambda e, g1=g1, csb=csb: e.tensor_mul(ra, g1, csb[:, :]))
                    v1 = p.op("v", lambda e, g2=g2, snb=snb: e.tensor_mul(rb, g2, snb[:, :]),
                              inc=(dve, 1))
                    p.wait("v", dve, v1)
                    v1b = p.op("v", lambda e, sp=sp, stg=stg: e.tensor_sub(
                        stg[:, (2 * sp) * TG:(2 * sp + 1) * TG], ra, rb),
                        inc=(dve, 1))
                    p.wait("v", dve, v1b)
                    p.op("v", lambda e, g1=g1, snb=snb: e.tensor_mul(ra, g1, snb[:, :]))
                    v2 = p.op("v", lambda e, g2=g2, csb=csb: e.tensor_mul(rb, g2, csb[:, :]),
                              inc=(dve, 1))
                    p.wait("v", dve, v2)
                    vro = p.op("v", lambda e, sp=sp, stg=stg: e.tensor_add(
                        stg[:, (2 * sp + 1) * TG:(2 * sp + 2) * TG], ra, rb),
                        inc=(dve, 1))
                    p.rel(f"gr{s1}", dve, vro)
                    p.rel(f"gr{s2}", dve, vro)
                    p.rel("t12_0", dve, vro)
                    p.rel("t12_1", dve, vro)
                    p.rel("ropetbl", dve, vro)
                    p.wait("g", dve, vro)
                    vs1 = p.op("g", lambda e, c=c, sp=sp, d3=dst3, tb=tbase, stg=stg:
                               e.dma_start(
                        out=d3[4 * c:4 * c + 4, 64:96, tb:tb + TG],
                        in_=stg[:, (2 * sp) * TG:(2 * sp + 1) * TG]), inc=(ksemA, 16))
                    vs2 = p.op("g", lambda e, c=c, sp=sp, d3=dst3, tb=tbase, stg=stg:
                               e.dma_start(
                        out=d3[4 * c:4 * c + 4, 96:128, tb:tb + TG],
                        in_=stg[:, (2 * sp + 1) * TG:(2 * sp + 2) * TG]),
                        inc=(ksemB, 16))
                    p.rel(f"stg{2 * sp}", ksemA, vs1)
                    p.rel(f"stg{2 * sp + 1}", ksemB, vs2)
            p.rel("xnrd", pe, pe.n)

        def emit_v_proj(tiles, src_xn, v_xn_ready, tbase, first_w=None):
            stg = tiles["stg"]
            wmode("half")
            for dvc in range(4):
                sl = dvc % 2
                wsem, vw = load_w(sl, wv_bf[dvc].rearrange("(dc p) o -> p dc o", p=128),
                                  NDC * 512, first_w if dvc < 2 else None)
                p.wait("t", wsem, vw)
                if dvc == 0:
                    p.wait("t", dve, v_xn_ready)
                vmm = None
                for tt in range(4):
                    bp = tt % 2
                    pv = ppo[bp]
                    p.use("t", "pA" if bp == 0 else "pC")
                    for dc in range(NDC):
                        vmm = p.op("t", lambda e, sl=sl, dc=dc, tt=tt, pv=pv: e.matmul(
                            pv[:, 0:TG],
                            src_xn[:, dc * TG + tt * 128: dc * TG + (tt + 1) * 128],
                            wb[sl][:, dc * 512:(dc + 1) * 512],
                            start=(dc == 0), stop=(dc == NDC - 1)),
                            inc=(pe, 1) if dc == NDC - 1 else None)
                    sp = tt
                    ksem = sm[f"ks{sp}"]
                    p.use("v", f"stg{sp}")
                    p.wait("v", pe, vmm)
                    vd = p.op("v", lambda e, sp=sp, pv=pv, stg=stg: e.tensor_copy(
                        stg[:, sp * TG:(sp + 1) * TG], pv[:, 0:TG]), inc=(dve, 1))
                    p.rel("pA" if bp == 0 else "pC", dve, vd)
                    p.wait("g", dve, vd)
                    vst = p.op("g", lambda e, dvc=dvc, tt=tt, sp=sp, tb=tbase, stg=stg:
                               e.dma_start(
                        out=v_loc[tb + tt * 128: tb + (tt + 1) * 128,
                                  dvc * 512:(dvc + 1) * 512],
                        in_=stg[:, sp * TG:(sp + 1) * TG]), inc=(ksem, 16))
                    p.rel(f"stg{sp}", ksem, vst)
                p.rel(f"wb{sl}", pe, vmm)
            p.rel("xnrd", pe, pe.n)

        def xf_chunk(gidx):
            def xc(dc):
                return x_f[:, dc * TOK + gidx * TG: dc * TOK + gidx * TG + TG]
            return xc

        v_wo_done = None
        with ExitStack() as scA:
            EA = scA.enter_context
            tiles_sA = {
                "xn": EA(nc.sbuf_tensor("xn", [128, NDC * TG], BF16)),
                "xn2": EA(nc.sbuf_tensor("xn2", [128, NDC * TG], BF16)),
                "t1b": EA(nc.sbuf_tensor("t1b", [128, 2 * TG], BF16)),
                "t2b": EA(nc.sbuf_tensor("t2b", [128, 2 * TG], BF16)),
                "hdn": EA(nc.sbuf_tensor("hdn", [128, NIC * TG], BF16)),
                "xp": EA(nc.sbuf_tensor("xp", [128, NDC * TG], F16)),
            }
            tiles_sA["stg"] = tiles_sA["hdn"][:, 0:4 * TG]
            tiles_sA["gr"] = tiles_sA["hdn"][:, 4 * TG:8 * TG]
            lnb.clear()
            lnb["sqb"] = EA(nc.sbuf_tensor("sqbA", [128, TG], F16))
            lnb["rstd_bc"] = EA(nc.sbuf_tensor("rstdbcA", [128, TG], F16))
            lnb["mr_bc"] = EA(nc.sbuf_tensor("mrbcA", [128, TG], F16))
            lnb["xtmp"] = rw3[:, :]
            lnb["csb"] = tiles_sA["t2b"][:, 0:TG]
            lnb["snb"] = tiles_sA["t2b"][:, TG:2 * TG]
            xn = tiles_sA["xn"]
            xp = tiles_sA["xp"]

            v_ffn1 = {}
            xn2 = tiles_sA["xn2"]
            xbufs = [(xn, "xnA"), (xn2, "xnB")]
            v_ln = {0: emit_ln(xf_chunk(0), xn, xn_key="xnA",
                               x_ready=(xin, v_xin))}
            for gidx in range(4):
                xc = xf_chunk(gidx)
                buf, key = xbufs[gidx % 2]
                vh = emit_ffn_A(buf, v_ln[gidx], tiles_sA,
                                first=(gidx == 0), xn_key=key)
                if gidx + 1 < 4:
                    nbuf, nkey = xbufs[(gidx + 1) % 2]
                    v_ln[gidx + 1] = emit_ln(xf_chunk(gidx + 1), nbuf, xn_key=nkey)
                v_ffn1[gidx] = emit_ffn_B(vh, lambda oc, xc=xc: xc(oc),
                                          tiles_sA, first=(gidx == 0))[-1]

            if PHASES >= 2:
                def xp_chunk(dc):
                    return xp[:, dc * TG:(dc + 1) * TG]

                xv = x_in.rearrange("(dc p) t -> p dc t", p=128)
                for gidx in range(4):
                    p.use("g", "xp")
                    vxp = p.op("g", lambda e, gidx=gidx, xv=xv: e.dma_start(
                        out=xp[:, :],
                        in_=xv[:, :, TOK + gidx * TG: TOK + (gidx + 1) * TG]),
                        inc=(xin, 16))
                    p.wait("g", xin, vxp)
                    vxn_p1 = emit_ln(xp_chunk, xn, xn_key="xnA",
                                     x_ready=(xin, vxp))
                    vh = emit_ffn_A(xn, vxn_p1, tiles_sA, xn_key="xnA")
                    vxn_o = emit_ln(xf_chunk(gidx), xn2, xn_key="xnB",
                                    x_ready=(dve, v_ffn1[gidx]) if gidx < 4 else None)
                    v_fp_l = emit_ffn_B(vh, xp_chunk, tiles_sA)
                    vxn_p = emit_ln(xp_chunk, xn, xn_key="xnA",
                                    x_chunk_ready=[(dve, v) for v in v_fp_l])
                    p.rel("xp", dve, vxn_p)
                    tb_par = TOK + gidx * TG
                    tb_own = gidx * TG
                    emit_sem_proj(tiles_sA, xn, vxn_p, wks_bf, k_loc3, tb_par,
                                  first_w=v_cast["wks"] if gidx == 0 else None)
                    emit_geo_proj(tiles_sA, xn, vxn_p, wkg_bf, k_loc3, tb_par,
                                  first_w=v_cast["wkg"] if gidx == 0 else None)
                    emit_v_proj(tiles_sA, xn, vxn_p, tb_par,
                                first_w=v_cast["wv"] if gidx == 0 else None)
                    emit_sem_proj(tiles_sA, xn2, vxn_o, wks_bf, k_loc3, tb_own)
                    emit_geo_proj(tiles_sA, xn2, vxn_o, wkg_bf, k_loc3, tb_own)
                    emit_v_proj(tiles_sA, xn2, vxn_o, tb_own)
                    emit_sem_proj(tiles_sA, xn2, vxn_o, wqs_bf, q_loc3, tb_own,
                                  first_w=v_cast["wqs"] if gidx == 0 else None)
                    emit_geo_proj(tiles_sA, xn2, vxn_o, wqg_bf, q_loc3, tb_own,
                                  first_w=v_cast["wqg"] if gidx == 0 else None)

        if PHASES >= 3:
            with ExitStack() as scB:
                EB = scB.enter_context
                kh = EB(nc.sbuf_tensor("kh", [128, S], BF16))
                vhb = EB(nc.sbuf_tensor("vhb", [128, 2 * S], BF16))
                qh = EB(nc.sbuf_tensor("qh", [128, TOK], BF16))
                ptb = EB(nc.sbuf_tensor("ptb", [128, 4 * TG], BF16))
                lacc = rw2[:, :]
                linv_bc = EB(nc.sbuf_tensor("linv_bc", [128, TG], F16))
                avs = EB(nc.sbuf_tensor("avs", [128, TG], F16))
                ao = EB(nc.sbuf_tensor("ao", [128, H * TOK], BF16))

                p.use("v", "mA")
                p.use("v", "mB")
                vz1 = p.op("v", lambda e: e.memset(mga[:, :], 0.0), inc=(dve, 1))
                vz2 = p.op("v", lambda e: e.memset(mgb[:, :], 0.0), inc=(dve, 1))
                p.rel("mA", dve, vz1)
                p.rel("mB", dve, vz2)
                v_ks_all = [sm[f"ks{i}"].n for i in range(4)]
                pending = [None]

                def flush_pending():
                    if pending[0] is not None:
                        pending[0]()
                        pending[0] = None

                for h in range(H):
                    hs = h % 2
                    hsem = sm["h0"] if hs == 0 else sm["h1"]
                    kqsem = sm["kq0"] if hs == 0 else sm["kq1"]
                    if h < 2:
                        for i in range(4):
                            p.wait("y", sm[f"ks{i}"], v_ks_all[i])
                    p.use("g", "kh")
                    p.op("g", lambda e, h=h: e.dma_start(
                        out=kh[:, :], in_=k_loc[h * 128:(h + 1) * 128, :]),
                        inc=(kqsem, 16))
                    p.use("y", f"vh{hs}")
                    p.op("y", lambda e, h=h, hs=hs: e.dma_start(
                        out=vhb[:, hs * S:(hs + 1) * S],
                        in_=v_loc3[:, :, h * 128:(h + 1) * 128]), inc=(hsem, 16))
                    p.use("g", "qh")
                    vldq = p.op("g", lambda e, h=h: e.dma_start(
                        out=qh[:, :], in_=q_loc[h * 128:(h + 1) * 128, :]),
                        inc=(kqsem, 16))
                    p.wait("t", kqsem, vldq)
                    p.wait("t", hsem, hsem.n)
                    vhv = vhb[:, hs * S:(hs + 1) * S]
                    v_last_pe_h = None
                    for c in range(4):
                        ab = c % 2
                        pav = ppo[ab]
                        p.use("t", "pA" if ab == 0 else "pC")
                        p.use("t", "pD")
                        strips = ([(0, l) for l in range(4 * c + 4)]
                                  + [(1, l) for l in range(1, 4 * c + 4)]
                                  + [(1, 0)])
                        nst = len(strips)
                        nb = (nst + 1) // 2
                        binfo = []
                        v_lacc = None
                        for k in range(nb + 1):
                            if k < nb:
                                lo = 2 * k
                                hi = min(2 * k + 1, nst - 1)
                                mg = mga if k % 2 == 0 else mgb
                                mkey = "mA" if k % 2 == 0 else "mB"
                                p.use("t", mkey)
                                vs = None
                                for j in range(lo, hi + 1):
                                    sec, l = strips[j]
                                    off = max(l - 4 * c, 0) * 128
                                    w = TG - off
                                    jj = j - lo
                                    vs = p.op("t", lambda e, sec=sec, l=l, off=off,
                                              w=w, jj=jj, mg=mg, c=c: e.matmul(
                                        mg[:, jj * TG + off:jj * TG + off + w],
                                        kh[:, sec * TOK + l * 128:
                                           sec * TOK + (l + 1) * 128],
                                        qh[:, c * TG + off:c * TG + off + w],
                                        start=True, stop=True), inc=(pe, 1))
                                cnt = hi - lo + 1
                                p.use("s", f"ptb{k % 2}")
                                p.use("s", f"ptbV{k % 2}")
                                p.wait("s", pe, vs)
                                va = p.op("s", lambda e, k=k, cnt=cnt, mg=mg: e.activation(
                                    ptb[:, (k % 2) * 2 * TG:
                                        (k % 2) * 2 * TG + cnt * TG],
                                    mg[:, 0:cnt * TG], AF.Exp, scale=SCALE),
                                    inc=(act, 1))
                                p.rel(mkey, act, va)
                                vmask = None
                                for j in range(lo, hi + 1):
                                    sec, l = strips[j]
                                    if l >= 4 * c:
                                        joff = (l - 4 * c) * 128
                                        msk = m0_sb if sec == 0 else m1_sb
                                        base = (k % 2) * 2 * TG + (j - lo) * TG
                                        p.wait("v", act, va)
                                        vmask = p.op("v", lambda e, base=base,
                                                     joff=joff, msk=msk: e.tensor_mul(
                                            ptb[:, base + joff:base + joff + 128],
                                            ptb[:, base + joff:base + joff + 128],
                                            msk[:, :]), inc=(dve, 1))
                                for j in range(lo, hi + 1):
                                    sec, l = strips[j]
                                    joff = max(l - 4 * c, 0) * 128
                                    jw = TG - joff
                                    base = (k % 2) * 2 * TG + (j - lo) * TG
                                    if vmask is not None:
                                        p.wait("v", dve, vmask)
                                    else:
                                        p.wait("v", act, va)
                                    if v_lacc is not None:
                                        p.wait("v", dve, v_lacc)
                                    if j == 0:
                                        v_lacc = p.op("v", lambda e, base=base:
                                                      e.tensor_copy(
                                            lacc[:, :], ptb[:, base:base + TG]),
                                            inc=(dve, 1))
                                    else:
                                        v_lacc = p.op("v", lambda e, base=base,
                                                      joff=joff, jw=jw: e.tensor_add(
                                            lacc[:, joff:joff + jw],
                                            lacc[:, joff:joff + jw],
                                            ptb[:, base + joff:base + joff + jw]),
                                            inc=(dve, 1))
                                p.rel(f"ptbV{k % 2}", dve, v_lacc)
                                binfo.append((k, lo, hi, va, vmask))
                            if k >= 1:
                                if k == 1:
                                    flush_pending()
                                kk, lo, hi, va, vmask = binfo[k - 1]
                                if vmask is not None:
                                    p.wait("t", dve, vmask)
                                else:
                                    p.wait("t", act, va)
                                for j in range(lo, hi + 1):
                                    sec, l = strips[j]
                                    joff = max(l - 4 * c, 0) * 128
                                    jw = TG - joff
                                    base = (kk % 2) * 2 * TG + (j - lo) * TG
                                    vav = p.op("t", lambda e, sec=sec, l=l, joff=joff,
                                               jw=jw, base=base, vhv=vhv, pav=pav,
                                               stf=(j == 0), spf=(j == nst - 1):
                                               e.matmul(
                                        pav[:, joff:joff + jw],
                                        vhv[:, (sec * 16 + l) * 128:
                                            (sec * 16 + l + 1) * 128],
                                        ptb[:, base + joff:base + joff + jw],
                                        start=stf, stop=spf),
                                        inc=(pe, 1))
                                    v_last_pe_h = vav
                                p.rel(f"ptb{kk % 2}", pe, v_last_pe_h)
                        p.wait("t", dve, v_lacc)
                        vlp = p.op("t", lambda e: e.matmul(
                            pD[0:1, 0:TG], ones_f32[:, :], lacc[:, :],
                            start=True, stop=True), inc=(pe, 1))
                        p.use("v", "rowsD")
                        p.wait("v", pe, vlp)
                        vlc = p.op("v", lambda e: e.tensor_copy(
                            rows["l"], pD[0:1, 0:TG]), inc=(dve, 1))
                        p.wait("v", dve, vlc)
                        vli = p.op("v", lambda e: e.reciprocal(rows["li"], rows["l"]),
                                   inc=(dve, 1))
                        p.rel("pD", dve, vli)

                        def mk_pending(vli=vli, v_last=v_last_pe_h, pav=pav,
                                       ab=ab, h=h, c=c):
                            def emit():
                                p.use("t", "pB")
                                p.wait("t", dve, vli)
                                vbc = p.op("t", lambda e: e.matmul(
                                    pB[:, 0:TG], ones_r[:, :], rows["li"],
                                    start=True, stop=True), inc=(pe, 1))
                                p.rel("rowsD", pe, vbc)
                                p.use("s", "linvbc")
                                p.wait("s", pe, vbc)
                                va1 = p.op("s", lambda e: e.activation(
                                    linv_bc[:, :], pB[:, 0:TG], AF.Copy),
                                    inc=(act, 1))
                                p.rel("pB", act, va1)
                                p.use("s", "avs")
                                p.wait("s", pe, v_last)
                                va2 = p.op("s", lambda e, pav=pav: e.activation(
                                    avs[:, :], pav[:, 0:TG], AF.Copy), inc=(act, 1))
                                p.rel("pA" if ab == 0 else "pC", act, va2)
                                p.wait("v", act, va2)
                                vao = p.op("v", lambda e, h=h, c=c: e.tensor_mul(
                                    ao[:, h * TOK + c * TG: h * TOK + (c + 1) * TG],
                                    avs[:, :], linv_bc[:, :]), inc=(dve, 1))
                                p.rel("linvbc", dve, vao)
                                p.rel("avs", dve, vao)
                            return emit
                        pending[0] = mk_pending()
                    p.rel("kh", pe, v_last_pe_h)
                    p.rel(f"vh{hs}", pe, v_last_pe_h)
                    p.rel("qh", pe, v_last_pe_h)
                flush_pending()
                v_ao_done = dve.n

                v_res = None
                for ob in range(NDC):
                    sl = ob % 2
                    wsem, vw = load_w(sl,
                                      wo_bf[ob].rearrange("(ac p) o -> p ac o", p=128),
                                      NDC * 128,
                                      v_cast["wo"] if ob < 2 else None)
                    p.wait("t", wsem, vw)
                    if ob == 0:
                        p.wait("t", dve, v_ao_done)
                    vmm = None
                    for qc in range(4):
                        bp = qc % 2
                        po = ppo[bp]
                        p.use("t", "pA" if bp == 0 else "pC")
                        for adc in range(NDC):
                            vmm = p.op("t", lambda e, sl=sl, adc=adc, qc=qc, po=po:
                                       e.matmul(
                                po[:, 0:TG],
                                wb[sl][:, adc * 128:(adc + 1) * 128],
                                ao[:, adc * TOK + qc * TG: adc * TOK + (qc + 1) * TG],
                                start=(adc == 0), stop=(adc == NDC - 1)),
                                inc=(pe, 1) if adc == NDC - 1 else None)
                        p.use("s", "t3")
                        p.wait("s", pe, vmm)
                        va = p.op("s", lambda e, po=po: e.activation(
                            t3b[:, :], po[:, 0:TG], AF.Copy), inc=(act, 1))
                        p.rel("pA" if bp == 0 else "pC", act, va)
                        p.wait("v", act, va)
                        if v_res is not None:
                            p.wait("v", dve, v_res)
                        v_res = p.op("v", lambda e, ob=ob, qc=qc: e.tensor_add(
                            x_f[:, ob * TOK + qc * TG: ob * TOK + (qc + 1) * TG],
                            x_f[:, ob * TOK + qc * TG: ob * TOK + (qc + 1) * TG],
                            t3b[:, :]), inc=(dve, 1))
                        p.rel("t3", dve, v_res)
                    p.rel(f"wb{sl}", pe, vmm)
                v_wo_done = v_res

        if PHASES >= 4:
            with ExitStack() as scC:
                EC = scC.enter_context
                tiles_sC = {
                    "t1b": EC(nc.sbuf_tensor("t1c", [128, 2 * TG], BF16)),
                    "t2b": EC(nc.sbuf_tensor("t2c", [128, 2 * TG], BF16)),
                    "hdn": EC(nc.sbuf_tensor("hdnc", [128, NIC * TG], BF16)),
                }
                xn3 = EC(nc.sbuf_tensor("xn3", [128, NDC * TG], BF16))
                xn4 = EC(nc.sbuf_tensor("xn4", [128, NDC * TG], BF16))
                lnb.clear()
                lnb["sqb"] = EC(nc.sbuf_tensor("sqbC", [128, TG], F16))
                lnb["rstd_bc"] = EC(nc.sbuf_tensor("rstdbcC", [128, TG], F16))
                lnb["mr_bc"] = EC(nc.sbuf_tensor("mrbcC", [128, TG], F16))
                lnb["xtmp"] = rw3[:, :]
                xbufs4 = [(xn3, "xnA"), (xn4, "xnB")]
                v_ln4 = {0: emit_ln(xf_chunk(0), xn3, xn_key="xnA",
                                    x_ready=(dve, v_wo_done)
                                    if v_wo_done is not None else None)}
                def store_ob(ob, v_res):
                    p.wait("g", dve, v_res)
                    p.op("g", lambda e, ob=ob: e.dma_start(
                        out=x_out[ob * 128:(ob + 1) * 128, :],
                        in_=x_f[:, ob * TOK:(ob + 1) * TOK]), inc=(sm["out"], 16))

                for gidx in range(4):
                    xc = xf_chunk(gidx)
                    buf, key = xbufs4[gidx % 2]
                    vh = emit_ffn_A(buf, v_ln4[gidx], tiles_sC, xn_key=key)
                    if gidx + 1 < 4:
                        nbuf, nkey = xbufs4[(gidx + 1) % 2]
                        v_ln4[gidx + 1] = emit_ln(xf_chunk(gidx + 1), nbuf,
                                                  xn_key=nkey)
                    emit_ffn_B(vh, lambda oc, xc=xc: xc(oc), tiles_sC,
                               after_ob=store_ob if gidx == 3 else None)

        if PHASES < 4:
            v_final = dve.n
            p.wait("g", dve, v_final)
            for dc in range(NDC):
                p.op("g", lambda e, dc=dc: e.dma_start(
                    out=x_out[dc * 128:(dc + 1) * 128, :],
                    in_=x_f[:, dc * TOK:(dc + 1) * TOK]), inc=(sm["out"], 16))
        p.wait("g", sm["out"], sm["out"].n)

        with nc.Block() as block:
            @block.tensor
            def _(e):
                for fn in p.cmds["t"]:
                    fn(e)

            @block.vector
            def _(e):
                for fn in p.cmds["v"]:
                    fn(e)

            @block.scalar
            def _(e):
                for fn in p.cmds["s"]:
                    fn(e)

            @block.gpsimd
            def _(e):
                for fn in p.cmds["g"]:
                    fn(e)

            @block.sync
            def _(e):
                for fn in p.cmds["y"]:
                    fn(e)

    return nc


def _geo_perm():
    perm = np.zeros(1024, np.int64)
    for pos in range(1024):
        if pos < 512:
            h, j = pos // 32, pos % 32
            perm[pos] = h * 64 + j
        else:
            h, j = (pos - 512) // 32, (pos - 512) % 32
            perm[pos] = h * 64 + 32 + j
    return perm


def make_in_maps(L0_attn_x, Wq_sem, Wk_sem, Wq_geo, Wk_geo, Wv, Wo,
                 gate_w, up_w, down_w, gate_s, up_s, down_s):
    f32 = np.float32
    x = np.asarray(L0_attn_x, dtype=f32)
    perm = _geo_perm()
    shared = {
        "wqs_t": np.ascontiguousarray(np.asarray(Wq_sem, f32).T),
        "wks_t": np.ascontiguousarray(np.asarray(Wk_sem, f32).T),
        "wqg_t": np.ascontiguousarray(np.asarray(Wq_geo, f32).T[:, perm]),
        "wkg_t": np.ascontiguousarray(np.asarray(Wk_geo, f32).T[:, perm]),
        "wv_t": np.ascontiguousarray(np.asarray(Wv, f32).T),
        "wo_t": np.ascontiguousarray(np.asarray(Wo, f32).T),
        "gate_t": np.ascontiguousarray(np.asarray(gate_w, f32).T),
        "up_t": np.ascontiguousarray(np.asarray(up_w, f32).T),
        "down_t": np.ascontiguousarray(np.asarray(down_w, f32).T),
        "gs_in": np.ascontiguousarray(np.asarray(gate_s, f32).reshape(NIC, 128).T),
        "us_in": np.ascontiguousarray(np.asarray(up_s, f32).reshape(NIC, 128).T),
        "ds_in": np.ascontiguousarray(np.asarray(down_s, f32).reshape(NDC, 128).T),
    }
    tri = (np.arange(128)[:, None] <= np.arange(128)[None, :]).astype(f32)
    in_maps = []
    for r in range(8):
        b, par = r // 2, r % 2
        own = np.arange(par, 32, 2)
        partn = np.arange(1 - par, 32, 2)
        xs = x[b].reshape(32, 128, D)
        x_core = np.concatenate([xs[own], xs[partn]], axis=0).reshape(S, D)
        x_core = np.ascontiguousarray(x_core.T)
        tiles = np.concatenate([own, partn])
        pos = (tiles[:, None] * 128 + np.arange(128)[None, :]).reshape(-1)
        inv = 1.0 / (10000.0 ** (np.arange(0, 64, 2, dtype=np.float64) / 64.0))
        fr = pos[None, :].astype(np.float64) * inv[:, None]
        cos64 = np.tile(np.cos(fr), (4, 1))
        sin64 = np.tile(np.sin(fr), (4, 1))
        cos128 = cos64.astype(f32)
        sin128 = sin64.astype(f32)
        cosl = (cos64 - cos128).astype(f32)
        sinl = (sin64 - sin128).astype(f32)
        m1 = np.zeros((128, 128), f32) if par == 0 else np.ones((128, 128), f32)
        d = dict(shared)
        d.update({
            "x_in": x_core,
            "cos_in": np.ascontiguousarray(cos128),
            "sin_in": np.ascontiguousarray(sin128),
            "cosl_in": np.ascontiguousarray(cosl),
            "sinl_in": np.ascontiguousarray(sinl),
            "m0_in": tri, "m1_in": m1,
        })
        in_maps.append(d)
    return in_maps


def assemble_output(results):
    out = np.zeros((4, S, D), np.float32)
    for r in range(8):
        b, par = r // 2, r % 2
        own = np.arange(par, 32, 2)
        xo = np.asarray(results[r]["x_out"], dtype=np.float32)
        out[b].reshape(32, 128, D)[own] = xo.T.reshape(16, 128, D)
    return out


def kernel(L0_attn_x, L0_attn_Wq_sem, L0_attn_Wk_sem, L0_attn_Wq_geo,
           L0_attn_Wk_geo, L0_attn_Wv, L0_attn_Wo, L0_ffn_gate_w,
           L0_ffn_up_w, L0_ffn_down_w, L0_ffn_gate_s, L0_ffn_up_s,
           L0_ffn_down_s):
    in_maps = make_in_maps(L0_attn_x, L0_attn_Wq_sem, L0_attn_Wk_sem,
                           L0_attn_Wq_geo, L0_attn_Wk_geo, L0_attn_Wv,
                           L0_attn_Wo, L0_ffn_gate_w, L0_ffn_up_w,
                           L0_ffn_down_w, L0_ffn_gate_s, L0_ffn_up_s,
                           L0_ffn_down_s)
    nc = build_program()
    res = run_bass_kernel_spmd(nc, in_maps, core_ids=list(range(8)))
    return assemble_output(res.results)


# version 11
# speedup vs baseline: 1.1551x; 1.0961x over previous
import math
import os

import numpy as np

import concourse.bass as bass
import concourse.mybir as mybir
from concourse.bass_utils import run_bass_kernel_spmd

F32 = mybir.dt.float32
F16 = mybir.dt.float16
BF16 = mybir.dt.bfloat16
AF = mybir.ActivationFunctionType
ALU = mybir.AluOpType

D = 2048
INTER = 5632
H = 16
S = 4096
TOK = 2048
TG = 512
NDC = D // 128
NIC = INTER // 128
IBLK = 256
NIB = INTER // IBLK
SCALE = 1.0 / math.sqrt(128.0)
EPS = 1e-5

PHASES = int(os.environ.get("MK_PHASES", "4"))


class Sem:
    __slots__ = ("h", "n")

    def __init__(self, handle):
        self.h = handle
        self.n = 0


class B:
    def __init__(self):
        self.cmds = {e: [] for e in ("t", "v", "s", "g", "y")}
        self.guard = {}

    def op(self, eng, builder, inc=None):
        if inc is None:
            self.cmds[eng].append(lambda e, b=builder: b(e))
            return None
        sem, by = inc
        sem.n += by
        self.cmds[eng].append(lambda e, b=builder, h=sem.h, k=by: b(e).then_inc(h, k))
        return sem.n

    def wait(self, eng, sem, val):
        assert val <= sem.n, f"wait on future value {val} > {sem.n}"
        self.cmds[eng].append(lambda e, h=sem.h, v=val: e.wait_ge(h, v))

    def use(self, eng, key):
        if key in self.guard:
            s, v = self.guard[key]
            self.wait(eng, s, v)

    def rel(self, key, sem, val):
        self.guard[key] = (sem, val)


def build_program():
    nc = bass.Bass("TRN2", target_bir_lowering=False)

    x_in = nc.dram_tensor("x_in", [D, S], F32, kind="ExternalInput").ap()
    gate_t = nc.dram_tensor("gate_t", [D, INTER], F32, kind="ExternalInput").ap()
    up_t = nc.dram_tensor("up_t", [D, INTER], F32, kind="ExternalInput").ap()
    down_t = nc.dram_tensor("down_t", [INTER, D], F32, kind="ExternalInput").ap()
    wqs_t = nc.dram_tensor("wqs_t", [D, 1024], F32, kind="ExternalInput").ap()
    wks_t = nc.dram_tensor("wks_t", [D, 1024], F32, kind="ExternalInput").ap()
    wqg_t = nc.dram_tensor("wqg_t", [D, 1024], F32, kind="ExternalInput").ap()
    wkg_t = nc.dram_tensor("wkg_t", [D, 1024], F32, kind="ExternalInput").ap()
    wv_t = nc.dram_tensor("wv_t", [D, D], F32, kind="ExternalInput").ap()
    wo_t = nc.dram_tensor("wo_t", [D, D], F32, kind="ExternalInput").ap()
    gs_in = nc.dram_tensor("gs_in", [128, NIC], F32, kind="ExternalInput").ap()
    us_in = nc.dram_tensor("us_in", [128, NIC], F32, kind="ExternalInput").ap()
    ds_in = nc.dram_tensor("ds_in", [128, NDC], F32, kind="ExternalInput").ap()
    cos_in = nc.dram_tensor("cos_in", [128, S], F32, kind="ExternalInput").ap()
    sin_in = nc.dram_tensor("sin_in", [128, S], F32, kind="ExternalInput").ap()
    cosl_in = nc.dram_tensor("cosl_in", [128, S], F32, kind="ExternalInput").ap()
    sinl_in = nc.dram_tensor("sinl_in", [128, S], F32, kind="ExternalInput").ap()
    m0_in = nc.dram_tensor("m0_in", [128, 128], F32, kind="ExternalInput").ap()
    m1_in = nc.dram_tensor("m1_in", [128, 128], F32, kind="ExternalInput").ap()

    x_out = nc.dram_tensor("x_out", [D, TOK], F32, kind="ExternalOutput").ap()

    gate_bf = nc.dram_tensor("gate_bf", [NIB, D, IBLK], BF16, kind="Internal").ap()
    up_bf = nc.dram_tensor("up_bf", [NIB, D, IBLK], BF16, kind="Internal").ap()
    down_bf = nc.dram_tensor("down_bf", [NDC, INTER, 128], BF16, kind="Internal").ap()
    wqs_bf = nc.dram_tensor("wqs_bf", [8, D, 128], BF16, kind="Internal").ap()
    wks_bf = nc.dram_tensor("wks_bf", [8, D, 128], BF16, kind="Internal").ap()
    wqg_bf = nc.dram_tensor("wqg_bf", [8, D, 128], BF16, kind="Internal").ap()
    wkg_bf = nc.dram_tensor("wkg_bf", [8, D, 128], BF16, kind="Internal").ap()
    wv_bf = nc.dram_tensor("wv_bf", [4, D, 512], BF16, kind="Internal").ap()
    wo_bf = nc.dram_tensor("wo_bf", [NDC, D, 128], BF16, kind="Internal").ap()
    cos_bf = nc.dram_tensor("cos_bf", [128, S], BF16, kind="Internal").ap()
    sin_bf = nc.dram_tensor("sin_bf", [128, S], BF16, kind="Internal").ap()
    k_loc = nc.dram_tensor("k_loc", [D, S], BF16, kind="Internal").ap()
    v_loc = nc.dram_tensor("v_loc", [S, D], BF16, kind="Internal").ap()
    q_loc = nc.dram_tensor("q_loc", [D, TOK], BF16, kind="Internal").ap()

    k_loc3 = k_loc.rearrange("(h r) t -> h r t", r=128)
    q_loc3 = q_loc.rearrange("(h r) t -> h r t", r=128)
    v_loc3 = v_loc.rearrange("(kt p) dv -> p kt dv", p=128)

    from contextlib import ExitStack
    with ExitStack() as top:
        E = top.enter_context
        p = B()

        sem_names = ("pe", "dve", "act", "w0", "w1", "u0", "u1", "cast", "xin",
                     "con", "rt", "tb0", "ks0", "ks1", "ks2", "ks3", "h0", "h1",
                     "out", "kq0", "kq1")
        sm = {n: Sem(E(nc.semaphore(n))) for n in sem_names}
        pe, dve, act = sm["pe"], sm["dve"], sm["act"]

        x_f = E(nc.sbuf_tensor("x_f", [128, NDC * TOK], F16))
        wbig = E(nc.sbuf_tensor("wbig", [128, 16384], BF16))
        gs_sb = E(nc.sbuf_tensor("gs_sb", [128, NIC], F32))
        us_sb = E(nc.sbuf_tensor("us_sb", [128, NIC], F32))
        ds_sb = E(nc.sbuf_tensor("ds_sb", [128, NDC], F32))
        dsh_sb = E(nc.sbuf_tensor("dsh_sb", [128, NDC], F32))
        m0_sb = E(nc.sbuf_tensor("m0_sb", [128, 128], BF16))
        m1_sb = E(nc.sbuf_tensor("m1_sb", [128, 128], BF16))
        ones_bf = E(nc.sbuf_tensor("ones_bf", [128, 1], BF16))
        ones_f16 = E(nc.sbuf_tensor("ones_f16", [128, 1], F16))
        ones_r = E(nc.sbuf_tensor("ones_r", [1, 128], F32))
        ones_f32 = E(nc.sbuf_tensor("ones_f32", [128, 1], F32))
        rw1 = E(nc.sbuf_tensor("rw1", [128, TG], F32))
        rw2 = E(nc.sbuf_tensor("rw2", [128, TG], F32))
        rw3 = E(nc.sbuf_tensor("rw3", [128, TG], F32))
        rw4 = E(nc.sbuf_tensor("rw4", [128, TG], F32))
        rw5 = E(nc.sbuf_tensor("rw5", [128, TG], F32))
        rows = {
            "mean": rw1[0:1, :], "sx": rw1[32:33, :], "sq": rw1[64:65, :],
            "e2": rw2[0:1, :], "var": rw2[32:33, :], "inv": rw2[64:65, :],
            "msq": rw3[0:1, :],
            "rstd": rw4[0:1, :],
            "mr": rw5[0:1, :], "l": rw5[32:33, :],
            "li": rw4[0:1, :],
        }
        t3b = E(nc.sbuf_tensor("t3b", [128, TG], BF16))
        lnb = {}

        wb = [wbig[:, 0:8192], wbig[:, 8192:16384]]
        pslq = [wbig[:, 0:2048], wbig[:, 2048:4096],
                wbig[:, 8192:10240], wbig[:, 10240:12288]]
        wstate = {"mode": "half"}


        pA = E(nc.psum_tensor("pA", [128, 512], F32))
        pB = E(nc.psum_tensor("pB", [128, 512], F32))
        pC = E(nc.psum_tensor("pC", [128, 512], F32))
        pD = E(nc.psum_tensor("pD", [128, 512], F32))
        mga = E(nc.psum_tensor("mga", [128, 1024], F32))
        mgb = E(nc.psum_tensor("mgb", [128, 1024], F32))
        pgu = [pA, pB, pC, pD]
        ppo = [pA, pC]

        con, xin, cast = sm["con"], sm["xin"], sm["cast"]
        for dst, src in ((gs_sb, gs_in), (us_sb, us_in), (ds_sb, ds_in)):
            p.op("y", lambda e, d=dst, s=src: e.dma_start(out=d[:, :], in_=s),
                 inc=(con, 16))
        v_con = con.n
        for dc in range(NDC):
            p.op("g", lambda e, dc=dc: e.dma_start(
                out=x_f[:, dc * TOK:(dc + 1) * TOK],
                in_=x_in[dc * 128:(dc + 1) * 128, 0:TOK]), inc=(xin, 16))
        v_xin = xin.n
        p.wait("g", xin, v_xin)

        def cast_dma(dst, src):
            v = p.op("g", lambda e, d=dst, s=src: e.dma_start(out=d, in_=s),
                     inc=(cast, 16))
            p.wait("g", cast, v)
            return v

        def cast_blocks(dst, src3, nblk):
            out = []
            for ib in range(nblk):
                out.append(cast_dma(dst[ib], src3[ib]))
            return out

        cast_dma(m0_sb[:, :], m0_in)
        cast_dma(m1_sb[:, :], m1_in)
        v_cast = {}
        p.op("v", lambda e: e.memset(mga[:, :], 0.0), inc=(dve, 1))
        p.op("v", lambda e: e.memset(mgb[:, :], 0.0), inc=(dve, 1))
        p.op("v", lambda e: e.memset(ones_bf[:, :], 1.0), inc=(dve, 1))
        p.op("v", lambda e: e.memset(ones_f16[:, :], 1.0), inc=(dve, 1))
        p.op("v", lambda e: e.memset(ones_r[:, :], 1.0), inc=(dve, 1))
        p.op("v", lambda e: e.memset(ones_f32[:, :], 1.0), inc=(dve, 1))
        p.wait("v", con, v_con)
        v_ones = p.op("v", lambda e: e.tensor_scalar_mul(
            dsh_sb[:, :], ds_sb[:, :], 0.5), inc=(dve, 1))
        with ExitStack() as sc0:
            th = sc0.enter_context(nc.sbuf_tensor("tbl_hi", [128, S], F32))
            tl = sc0.enter_context(nc.sbuf_tensor("tbl_lo", [128, S], F32))
            for dst_bf, hi_src, lo_src in ((cos_bf, cos_in, cosl_in),
                                           (sin_bf, sin_in, sinl_in)):
                v1 = p.op("y", lambda e, s=hi_src: e.dma_start(out=th[:, :], in_=s),
                          inc=(sm["tb0"], 16))
                v2 = p.op("y", lambda e, s=lo_src: e.dma_start(out=tl[:, :], in_=s),
                          inc=(sm["tb0"], 16))
                p.wait("v", sm["tb0"], v2)
                p.use("v", "tbl")
                v3 = p.op("v", lambda e: e.tensor_add(th[:, :], th[:, :], tl[:, :]),
                          inc=(dve, 1))
                p.wait("g", dve, v3)
                v_cast[("tbl", id(dst_bf))] = p.op(
                    "g", lambda e, d=dst_bf: e.dma_start(out=d, in_=th[:, :]),
                    inc=(cast, 16))
                p.wait("g", cast, cast.n)
                p.rel("tbl", cast, cast.n)
                p.wait("y", cast, cast.n)
        v_cast["cos"] = v_cast["sin"] = cast.n
        gate_src = gate_t.rearrange("d (ib i) -> ib d i", i=IBLK)
        up_src = up_t.rearrange("d (ib i) -> ib d i", i=IBLK)
        v_cast["gu_ib"] = []
        for ib in range(NIB):
            cast_dma(gate_bf[ib], gate_src[ib])
            v_cast["gu_ib"].append(cast_dma(up_bf[ib], up_src[ib]))
        v_cast["down_ob"] = cast_blocks(
            down_bf, down_t.rearrange("i (ob o) -> ob i o", o=128), NDC)
        v_cast["gate"] = v_cast["up"] = v_cast["gu_ib"][-1]
        v_cast["down"] = v_cast["down_ob"][-1]
        for nm, dst, src, nblk in (
            ("wks", wks_bf, wks_t.rearrange("d (oc o) -> oc d o", o=128), 8),
            ("wkg", wkg_bf, wkg_t.rearrange("d (oc o) -> oc d o", o=128), 8),
            ("wv", wv_bf, wv_t.rearrange("d (oc o) -> oc d o", o=512), 4),
            ("wqs", wqs_bf, wqs_t.rearrange("d (oc o) -> oc d o", o=128), 8),
            ("wqg", wqg_bf, wqg_t.rearrange("d (oc o) -> oc d o", o=128), 8),
            ("wo", wo_bf, wo_t.rearrange("d (oc o) -> oc d o", o=128), NDC),
        ):
            v_cast[nm] = cast_blocks(dst, src, nblk)[-1]
        p.wait("v", cast, v_cast["cos"])
        p.wait("s", cast, v_cast["cos"])
        p.wait("g", cast, v_cast["cos"])


        qsems = [sm["w0"], sm["w1"], sm["u0"], sm["u1"]]

        def wmode(m):
            if wstate["mode"] == m:
                return
            if m == "quarter":
                p.use("y", "wb0")
                p.use("y", "wb1")
            else:
                for j in range(4):
                    p.use("y", f"pw{j}")
            wstate["mode"] = m

        def emit_ln(xc, dst_xn, xn_key="xnrd", x_ready=None, x_chunk_ready=None):
            sqb = lnb["sqb"]
            rstd_bc = lnb["rstd_bc"]
            mr_bc = lnb["mr_bc"]
            xtmp = lnb["xtmp"]
            sxp = mga[0:1, 0:TG]
            sqp = mga[0:1, TG:2 * TG]
            bc1 = mgb[:, 0:TG]
            bc2 = mgb[:, TG:2 * TG]
            p.use("t", "mA")
            p.wait("t", dve, v_ones)
            if x_ready is not None:
                p.wait("t", x_ready[0], x_ready[1])
                p.wait("s", x_ready[0], x_ready[1])
            for dc in range(NDC):
                if x_chunk_ready is not None:
                    p.wait("t", x_chunk_ready[dc][0], x_chunk_ready[dc][1])
                p.op("t", lambda e, dc=dc, sxp=sxp: e.matmul(
                    sxp, ones_f16[:, :], xc(dc),
                    start=(dc == 0), stop=(dc == NDC - 1)))
            p.use("s", "sqb")
            prev_pe = None
            for dc in range(NDC):
                if x_chunk_ready is not None:
                    p.wait("s", x_chunk_ready[dc][0], x_chunk_ready[dc][1])
                if prev_pe is not None:
                    p.wait("s", pe, prev_pe)
                va = p.op("s", lambda e, dc=dc, sqb=sqb: e.activation(
                    sqb[:, :], xc(dc), AF.Square), inc=(act, 1))
                p.wait("t", act, va)
                prev_pe = p.op("t", lambda e, dc=dc, sqb=sqb, sqp=sqp: e.matmul(
                    sqp, ones_f16[:, :], sqb[:, :],
                    start=(dc == 0), stop=(dc == NDC - 1)), inc=(pe, 1))
            p.rel("sqb", pe, prev_pe)
            v_stats = pe.n
            p.use("v", "rowsABC")
            p.wait("v", pe, v_stats)
            p.op("v", lambda e, sxp=sxp: e.tensor_copy(rows["sx"], sxp))
            v_ev = p.op("v", lambda e, sqp=sqp: e.tensor_copy(rows["sq"], sqp),
                        inc=(dve, 1))
            p.rel("mA", dve, v_ev)
            p.wait("s", dve, v_ev)
            p.op("s", lambda e: e.activation(rows["mean"], rows["sx"], AF.Copy,
                                             scale=1.0 / D))
            v_a1 = p.op("s", lambda e: e.activation(rows["e2"], rows["sq"], AF.Copy,
                                                    scale=1.0 / D), inc=(act, 1))
            p.wait("s", act, v_a1)
            v_a2 = p.op("s", lambda e: e.activation(rows["msq"], rows["mean"],
                                                    AF.Square), inc=(act, 1))
            p.wait("v", act, v_a2)
            v_sub = p.op("v", lambda e: e.tensor_sub(
                rows["var"], rows["e2"], rows["msq"]), inc=(dve, 1))
            p.wait("v", dve, v_sub)
            v_d0 = p.op("v", lambda e: e.tensor_scalar_add(
                rows["var"], rows["var"], EPS), inc=(dve, 1))
            p.wait("v", dve, v_d0)
            v_d1 = p.op("v", lambda e: e.reciprocal(rows["inv"], rows["var"]),
                        inc=(dve, 1))
            p.wait("s", dve, v_d1)
            v_a3 = p.op("s", lambda e: e.activation(rows["rstd"], rows["inv"],
                                                    AF.Sqrt), inc=(act, 1))
            p.wait("v", act, v_a3)
            v_d2 = p.op("v", lambda e: e.tensor_mul(
                rows["mr"], rows["mean"], rows["rstd"]), inc=(dve, 1))
            p.use("t", "mB")
            p.wait("t", dve, v_d2)
            p.op("t", lambda e, bc1=bc1: e.matmul(bc1, ones_r[:, :], rows["rstd"],
                                                  start=True, stop=True))
            v_bc = p.op("t", lambda e, bc2=bc2: e.matmul(
                bc2, ones_r[:, :], rows["mr"], start=True, stop=True),
                inc=(pe, 1))
            p.rel("rowsABC", pe, v_bc)
            p.use("s", "bcb")
            p.wait("s", pe, v_bc)
            p.op("s", lambda e, rstd_bc=rstd_bc, bc1=bc1: e.activation(
                rstd_bc[:, :], bc1, AF.Copy))
            v_bce = p.op("s", lambda e, mr_bc=mr_bc, bc2=bc2: e.activation(
                mr_bc[:, :], bc2, AF.Copy), inc=(act, 1))
            p.rel("mB", act, v_bce)
            p.wait("v", act, v_bce)
            p.use("v", xn_key)
            v_last = None
            prev = v_d2
            for dc in range(NDC):
                if prev is not None:
                    p.wait("v", dve, prev)
                v_mul = p.op("v", lambda e, dc=dc, xtmp=xtmp, rstd_bc=rstd_bc:
                     e.tensor_mul(xtmp[:, :], xc(dc), rstd_bc[:, :]), inc=(dve, 1))
                p.wait("v", dve, v_mul)
                v_last = p.op("v", lambda e, dc=dc, xtmp=xtmp, mr_bc=mr_bc:
                              e.tensor_sub(
                    dst_xn[:, dc * TG:(dc + 1) * TG], xtmp[:, :], mr_bc[:, :]),
                    inc=(dve, 1))
                prev = v_last
            p.rel("bcb", dve, v_last)
            return v_last

        def emit_ffn_A(src_xn, v_xn_ready, tiles, first=False, xn_key="xnrd"):
            t1b, t2b, hdn = tiles["t1b"], tiles["t2b"], tiles["hdn"]
            wmode("half")
            last_pe = None
            for ib in range(NIB):
                sl = ib % 2
                wsem = sm["w0"] if sl == 0 else sm["w1"]
                usem = sm["u0"] if sl == 0 else sm["u1"]
                p.use("y", f"wb{sl}")
                if first:
                    p.wait("y", cast, v_cast["gu_ib"][ib])
                vw = p.op("y", lambda e, ib=ib, sl=sl: e.dma_start(
                    out=wb[sl][:, 0:NDC * IBLK],
                    in_=gate_bf[ib].rearrange("(dc p) i -> p dc i", p=128)),
                    inc=(wsem, 16))
                vu = p.op("y", lambda e, ib=ib, sl=sl: e.dma_start(
                    out=wb[sl][:, NDC * IBLK:2 * NDC * IBLK],
                    in_=up_bf[ib].rearrange("(dc p) i -> p dc i", p=128)),
                    inc=(usem, 16))
                p.wait("t", wsem, vw)
                p.wait("t", usem, vu)
                if ib == 0:
                    p.wait("t", dve, v_xn_ready)
                for hh in range(2):
                    ic = ib * 2 + hh
                    bp = ic % 2
                    pg = pgu[2 * bp]
                    pu = pgu[2 * bp + 1]
                    p.use("t", "pA" if bp == 0 else "pC")
                    p.use("t", "pB" if bp == 0 else "pD")
                    for dc in range(NDC):
                        p.op("t", lambda e, sl=sl, dc=dc, hh=hh, pg=pg: e.matmul(
                            pg[:, 0:TG],
                            wb[sl][:, dc * IBLK + hh * 128: dc * IBLK + (hh + 1) * 128],
                            src_xn[:, dc * TG:(dc + 1) * TG],
                            start=(dc == 0), stop=(dc == NDC - 1)))
                    for dc in range(NDC):
                        last_pe = p.op("t", lambda e, sl=sl, dc=dc, hh=hh, pu=pu: e.matmul(
                            pu[:, 0:TG],
                            wb[sl][:, NDC * IBLK + dc * IBLK + hh * 128:
                                    NDC * IBLK + dc * IBLK + (hh + 1) * 128],
                            src_xn[:, dc * TG:(dc + 1) * TG],
                            start=(dc == 0), stop=(dc == NDC - 1)),
                            inc=(pe, 1) if dc == NDC - 1 else None)
                    tp = ic % 2
                    p.use("s", f"t12_{tp}")
                    p.wait("s", pe, last_pe)
                    p.op("s", lambda e, ic=ic, tp=tp, pg=pg: e.activation(
                        t1b[:, tp * TG:(tp + 1) * TG], pg[:, 0:TG], AF.Silu,
                        scale=gs_sb[:, ic:ic + 1]))
                    va = p.op("s", lambda e, ic=ic, tp=tp, pu=pu: e.activation(
                        t2b[:, tp * TG:(tp + 1) * TG], pu[:, 0:TG], AF.Copy,
                        scale=us_sb[:, ic:ic + 1]), inc=(act, 1))
                    p.rel("pA" if bp == 0 else "pC", act, va)
                    p.rel("pB" if bp == 0 else "pD", act, va)
                    p.wait("v", act, va)
                    if ic < 4:
                        p.use("v", f"stg{ic}")
                    elif ic < 8:
                        p.use("v", f"gr{ic - 4}")
                    vh_ = p.op("v", lambda e, ic=ic, tp=tp: e.tensor_mul(
                        hdn[:, ic * TG:(ic + 1) * TG],
                        t1b[:, tp * TG:(tp + 1) * TG],
                        t2b[:, tp * TG:(tp + 1) * TG]), inc=(dve, 1))
                    p.rel(f"t12_{tp}", dve, vh_)
                p.rel(f"wb{sl}", pe, last_pe)
            p.rel(xn_key, pe, last_pe)
            return dve.n

        def emit_ffn_B(v_hdn_done, xdst, tiles, first=False, after_ob=None):
            hdn = tiles["hdn"]
            wmode("half")
            v_res = None
            rlist = []
            for ob in range(NDC):
                sl = ob % 2
                wsem = sm["w0"] if sl == 0 else sm["w1"]
                p.use("y", f"wb{sl}")
                if first:
                    p.wait("y", cast, v_cast["down_ob"][ob])
                vw = p.op("y", lambda e, ob=ob, sl=sl: e.dma_start(
                    out=wb[sl][:, 0:NIC * 128],
                    in_=down_bf[ob].rearrange("(ic p) o -> p ic o", p=128)),
                    inc=(wsem, 16))
                p.wait("t", wsem, vw)
                if ob == 0:
                    p.wait("t", dve, v_hdn_done)
                bp = ob % 2
                po = ppo[bp]
                p.use("t", "pA" if bp == 0 else "pC")
                vmm = None
                for ic in range(NIC):
                    vmm = p.op("t", lambda e, sl=sl, ic=ic, po=po: e.matmul(
                        po[:, 0:TG], wb[sl][:, ic * 128:(ic + 1) * 128],
                        hdn[:, ic * TG:(ic + 1) * TG],
                        start=(ic == 0), stop=(ic == NIC - 1)),
                        inc=(pe, 1) if ic == NIC - 1 else None)
                p.rel(f"wb{sl}", pe, vmm)
                p.use("s", "t3")
                p.wait("s", pe, vmm)
                va = p.op("s", lambda e, ob=ob, po=po: e.activation(
                    t3b[:, :], po[:, 0:TG], AF.Copy,
                    scale=dsh_sb[:, ob:ob + 1]), inc=(act, 1))
                p.rel("pA" if bp == 0 else "pC", act, va)
                p.wait("v", act, va)
                if v_res is not None:
                    p.wait("v", dve, v_res)
                v_res = p.op("v", lambda e, ob=ob: e.tensor_add(
                    xdst(ob), xdst(ob), t3b[:, :]), inc=(dve, 1))
                p.rel("t3", dve, v_res)
                rlist.append(v_res)
                if after_ob is not None:
                    after_ob(ob, v_res)
            return rlist

        def emit_ffn(src_xn, v_xn_ready, xdst, tiles, first=False, xn_key="xnrd"):
            vh = emit_ffn_A(src_xn, v_xn_ready, tiles, first=first, xn_key=xn_key)
            return emit_ffn_B(vh, xdst, tiles, first=first)[-1]

        def load_w(sl, src_ap, ncols, pre_wait=None):
            wsem = sm["w0"] if sl == 0 else sm["w1"]
            p.use("y", f"wb{sl}")
            if pre_wait is not None:
                p.wait("y", cast, pre_wait)
            vw = p.op("y", lambda e, s=src_ap, sl=sl, n=ncols: e.dma_start(
                out=wb[sl][:, 0:n], in_=s), inc=(wsem, 16))
            return wsem, vw

        def proj_mm4(sl, src_xn, v_xn_ready, first_mm, bp):
            pk = ppo[bp]
            p.use("t", "pA" if bp == 0 else "pC")
            if first_mm:
                p.wait("t", dve, v_xn_ready)
            vmm = None
            for dc in range(NDC):
                vmm = p.op("t", lambda e, sl=sl, dc=dc, pk=pk: e.matmul(
                    pk[:, 0:TG], pslq[sl][:, dc * 128:(dc + 1) * 128],
                    src_xn[:, dc * TG:(dc + 1) * TG],
                    start=(dc == 0), stop=(dc == NDC - 1)),
                    inc=(pe, 1) if dc == NDC - 1 else None)
            return pk, vmm

        def proj_mm(sl, src_xn, v_xn_ready, first_mm, bp):
            pk = ppo[bp]
            p.use("t", "pA" if bp == 0 else "pC")
            if first_mm:
                p.wait("t", dve, v_xn_ready)
            vmm = None
            for dc in range(NDC):
                vmm = p.op("t", lambda e, sl=sl, dc=dc, pk=pk: e.matmul(
                    pk[:, 0:TG], wb[sl][:, dc * 128:(dc + 1) * 128],
                    src_xn[:, dc * TG:(dc + 1) * TG],
                    start=(dc == 0), stop=(dc == NDC - 1)),
                    inc=(pe, 1) if dc == NDC - 1 else None)
            return pk, vmm

        def emit_sem_proj(tiles, src_xn, v_xn_ready, w_bf, dst3, tbase,
                          first_w=None):
            stg = tiles["stg"]
            wmode("quarter")
            for oc in range(8):
                sl = oc % 4
                wsem = qsems[sl]
                p.use("y", f"pw{sl}")
                if first_w is not None and oc < 4:
                    p.wait("y", cast, first_w)
                vw = p.op("y", lambda e, oc=oc, sl=sl, wb_=w_bf: e.dma_start(
                    out=pslq[sl][:, :],
                    in_=wb_[oc].rearrange("(dc p) o -> p dc o", p=128)),
                    inc=(wsem, 16))
                p.wait("t", wsem, vw)
                pk, vmm = proj_mm4(sl, src_xn, v_xn_ready, oc == 0, oc % 2)
                p.rel(f"pw{sl}", pe, vmm)
                sp = oc % 4
                ksem = sm[f"ks{sp}"]
                p.use("v", f"stg{sp}")
                p.wait("v", pe, vmm)
                vd = p.op("v", lambda e, sp=sp, pk=pk, stg=stg: e.tensor_copy(
                    stg[:, sp * TG:(sp + 1) * TG], pk[:, 0:TG]), inc=(dve, 1))
                p.rel("pA" if oc % 2 == 0 else "pC", dve, vd)
                p.wait("g", dve, vd)
                vst = p.op("g", lambda e, oc=oc, sp=sp, d3=dst3, tb=tbase, stg=stg:
                           e.dma_start(
                    out=d3[2 * oc:2 * oc + 2, 0:64, tb:tb + TG],
                    in_=stg[:, sp * TG:(sp + 1) * TG]), inc=(ksem, 16))
                p.rel(f"stg{sp}", ksem, vst)
            p.rel("xnrd", pe, pe.n)

        def emit_geo_proj(tiles, src_xn, v_xn_ready, w_bf, dst3, tbase,
                          first_w=None):
            stg, gr = tiles["stg"], tiles["gr"]
            ra = tiles["t1b"][:, 0:TG]
            rb = tiles["t1b"][:, TG:2 * TG]
            csb, snb = lnb["csb"], lnb["snb"]
            wmode("quarter")
            p.use("g", "ropetbl")
            p.use("g", "t12_0")
            p.use("g", "t12_1")
            p.op("g", lambda e, tb=tbase, csb=csb: e.dma_start(
                out=csb[:, :], in_=cos_bf[:, tb:tb + TG]), inc=(sm["rt"], 16))
            v_rt = p.op("g", lambda e, tb=tbase, snb=snb: e.dma_start(
                out=snb[:, :], in_=sin_bf[:, tb:tb + TG]), inc=(sm["rt"], 16))
            order = [0, 4, 1, 5, 2, 6, 3, 7]
            for idx, oc in enumerate(order):
                sl = idx % 4
                wsem = qsems[sl]
                p.use("y", f"pw{sl}")
                if first_w is not None and idx < 4:
                    p.wait("y", cast, first_w)
                vw = p.op("y", lambda e, oc=oc, sl=sl, wb_=w_bf: e.dma_start(
                    out=pslq[sl][:, :],
                    in_=wb_[oc].rearrange("(dc p) o -> p dc o", p=128)),
                    inc=(wsem, 16))
                p.wait("t", wsem, vw)
                pk, vmm = proj_mm4(sl, src_xn, v_xn_ready, idx == 0, idx % 2)
                p.rel(f"pw{sl}", pe, vmm)
                gslot = (idx // 2) % 2 + (2 if oc >= 4 else 0)
                p.use("v", f"gr{gslot}")
                p.wait("v", pe, vmm)
                vd = p.op("v", lambda e, g=gslot, pk=pk, gr=gr: e.tensor_copy(
                    gr[:, g * TG:(g + 1) * TG], pk[:, 0:TG]), inc=(dve, 1))
                p.rel("pA" if idx % 2 == 0 else "pC", dve, vd)
                if oc >= 4:
                    c = oc - 4
                    s1 = (idx // 2) % 2
                    s2 = s1 + 2
                    g1 = gr[:, s1 * TG:(s1 + 1) * TG]
                    g2 = gr[:, s2 * TG:(s2 + 1) * TG]
                    sp = c % 2
                    ksemA = sm[f"ks{2 * sp}"]
                    ksemB = sm[f"ks{2 * sp + 1}"]
                    p.use("v", f"stg{2 * sp}")
                    p.use("v", f"stg{2 * sp + 1}")
                    p.use("v", "t12_0")
                    p.use("v", "t12_1")
                    p.wait("v", dve, vd)
                    p.wait("v", sm["rt"], v_rt)
                    p.op("v", lambda e, g1=g1, csb=csb: e.tensor_mul(ra, g1, csb[:, :]))
                    v1 = p.op("v", lambda e, g2=g2, snb=snb: e.tensor_mul(rb, g2, snb[:, :]),
                              inc=(dve, 1))
                    p.wait("v", dve, v1)
                    v1b = p.op("v", lambda e, sp=sp, stg=stg: e.tensor_sub(
                        stg[:, (2 * sp) * TG:(2 * sp + 1) * TG], ra, rb),
                        inc=(dve, 1))
                    p.wait("v", dve, v1b)
                    p.op("v", lambda e, g1=g1, snb=snb: e.tensor_mul(ra, g1, snb[:, :]))
                    v2 = p.op("v", lambda e, g2=g2, csb=csb: e.tensor_mul(rb, g2, csb[:, :]),
                              inc=(dve, 1))
                    p.wait("v", dve, v2)
                    vro = p.op("v", lambda e, sp=sp, stg=stg: e.tensor_add(
                        stg[:, (2 * sp + 1) * TG:(2 * sp + 2) * TG], ra, rb),
                        inc=(dve, 1))
                    p.rel(f"gr{s1}", dve, vro)
                    p.rel(f"gr{s2}", dve, vro)
                    p.rel("t12_0", dve, vro)
                    p.rel("t12_1", dve, vro)
                    p.rel("ropetbl", dve, vro)
                    p.wait("g", dve, vro)
                    vs1 = p.op("g", lambda e, c=c, sp=sp, d3=dst3, tb=tbase, stg=stg:
                               e.dma_start(
                        out=d3[4 * c:4 * c + 4, 64:96, tb:tb + TG],
                        in_=stg[:, (2 * sp) * TG:(2 * sp + 1) * TG]), inc=(ksemA, 16))
                    vs2 = p.op("g", lambda e, c=c, sp=sp, d3=dst3, tb=tbase, stg=stg:
                               e.dma_start(
                        out=d3[4 * c:4 * c + 4, 96:128, tb:tb + TG],
                        in_=stg[:, (2 * sp + 1) * TG:(2 * sp + 2) * TG]),
                        inc=(ksemB, 16))
                    p.rel(f"stg{2 * sp}", ksemA, vs1)
                    p.rel(f"stg{2 * sp + 1}", ksemB, vs2)
            p.rel("xnrd", pe, pe.n)

        def emit_v_proj(tiles, src_xn, v_xn_ready, tbase, first_w=None):
            stg = tiles["stg"]
            wmode("half")
            for dvc in range(4):
                sl = dvc % 2
                wsem, vw = load_w(sl, wv_bf[dvc].rearrange("(dc p) o -> p dc o", p=128),
                                  NDC * 512, first_w if dvc < 2 else None)
                p.wait("t", wsem, vw)
                if dvc == 0:
                    p.wait("t", dve, v_xn_ready)
                vmm = None
                for tt in range(4):
                    bp = tt % 2
                    pv = ppo[bp]
                    p.use("t", "pA" if bp == 0 else "pC")
                    for dc in range(NDC):
                        vmm = p.op("t", lambda e, sl=sl, dc=dc, tt=tt, pv=pv: e.matmul(
                            pv[:, 0:TG],
                            src_xn[:, dc * TG + tt * 128: dc * TG + (tt + 1) * 128],
                            wb[sl][:, dc * 512:(dc + 1) * 512],
                            start=(dc == 0), stop=(dc == NDC - 1)),
                            inc=(pe, 1) if dc == NDC - 1 else None)
                    sp = tt
                    ksem = sm[f"ks{sp}"]
                    p.use("v", f"stg{sp}")
                    p.wait("v", pe, vmm)
                    vd = p.op("v", lambda e, sp=sp, pv=pv, stg=stg: e.tensor_copy(
                        stg[:, sp * TG:(sp + 1) * TG], pv[:, 0:TG]), inc=(dve, 1))
                    p.rel("pA" if bp == 0 else "pC", dve, vd)
                    p.wait("g", dve, vd)
                    vst = p.op("g", lambda e, dvc=dvc, tt=tt, sp=sp, tb=tbase, stg=stg:
                               e.dma_start(
                        out=v_loc[tb + tt * 128: tb + (tt + 1) * 128,
                                  dvc * 512:(dvc + 1) * 512],
                        in_=stg[:, sp * TG:(sp + 1) * TG]), inc=(ksem, 16))
                    p.rel(f"stg{sp}", ksem, vst)
                p.rel(f"wb{sl}", pe, vmm)
            p.rel("xnrd", pe, pe.n)

        def xf_chunk(gidx):
            def xc(dc):
                return x_f[:, dc * TOK + gidx * TG: dc * TOK + gidx * TG + TG]
            return xc

        v_wo_done = None
        with ExitStack() as scA:
            EA = scA.enter_context
            tiles_sA = {
                "xn": EA(nc.sbuf_tensor("xn", [128, NDC * TG], BF16)),
                "xn2": EA(nc.sbuf_tensor("xn2", [128, NDC * TG], BF16)),
                "t1b": EA(nc.sbuf_tensor("t1b", [128, 2 * TG], BF16)),
                "t2b": EA(nc.sbuf_tensor("t2b", [128, 2 * TG], BF16)),
                "hdn": EA(nc.sbuf_tensor("hdn", [128, NIC * TG], BF16)),
                "xp": EA(nc.sbuf_tensor("xp", [128, NDC * TG], F16)),
            }
            tiles_sA["stg"] = tiles_sA["hdn"][:, 0:4 * TG]
            tiles_sA["gr"] = tiles_sA["hdn"][:, 4 * TG:8 * TG]
            lnb.clear()
            lnb["sqb"] = EA(nc.sbuf_tensor("sqbA", [128, TG], F16))
            lnb["rstd_bc"] = EA(nc.sbuf_tensor("rstdbcA", [128, TG], F16))
            lnb["mr_bc"] = EA(nc.sbuf_tensor("mrbcA", [128, TG], F16))
            lnb["xtmp"] = rw3[:, :]
            lnb["csb"] = tiles_sA["t2b"][:, 0:TG]
            lnb["snb"] = tiles_sA["t2b"][:, TG:2 * TG]
            xn = tiles_sA["xn"]
            xp = tiles_sA["xp"]

            v_ffn1 = {}
            xn2 = tiles_sA["xn2"]
            xbufs = [(xn, "xnA"), (xn2, "xnB")]
            v_ln = {0: emit_ln(xf_chunk(0), xn, xn_key="xnA",
                               x_ready=(xin, v_xin))}
            for gidx in range(4):
                xc = xf_chunk(gidx)
                buf, key = xbufs[gidx % 2]
                vh = emit_ffn_A(buf, v_ln[gidx], tiles_sA,
                                first=(gidx == 0), xn_key=key)
                if gidx + 1 < 4:
                    nbuf, nkey = xbufs[(gidx + 1) % 2]
                    v_ln[gidx + 1] = emit_ln(xf_chunk(gidx + 1), nbuf, xn_key=nkey)
                v_ffn1[gidx] = emit_ffn_B(vh, lambda oc, xc=xc: xc(oc),
                                          tiles_sA, first=(gidx == 0))[-1]

            if PHASES >= 2:
                def xp_chunk(dc):
                    return xp[:, dc * TG:(dc + 1) * TG]

                xv = x_in.rearrange("(dc p) t -> p dc t", p=128)
                for gidx in range(4):
                    p.use("g", "xp")
                    vxp = p.op("g", lambda e, gidx=gidx, xv=xv: e.dma_start(
                        out=xp[:, :],
                        in_=xv[:, :, TOK + gidx * TG: TOK + (gidx + 1) * TG]),
                        inc=(xin, 16))
                    p.wait("g", xin, vxp)
                    vxn_p1 = emit_ln(xp_chunk, xn, xn_key="xnA",
                                     x_ready=(xin, vxp))
                    vh = emit_ffn_A(xn, vxn_p1, tiles_sA, xn_key="xnA")
                    vxn_o = emit_ln(xf_chunk(gidx), xn2, xn_key="xnB",
                                    x_ready=(dve, v_ffn1[gidx]) if gidx < 4 else None)
                    v_fp_l = emit_ffn_B(vh, xp_chunk, tiles_sA)
                    vxn_p = emit_ln(xp_chunk, xn, xn_key="xnA",
                                    x_chunk_ready=[(dve, v) for v in v_fp_l])
                    p.rel("xp", dve, vxn_p)
                    tb_par = TOK + gidx * TG
                    tb_own = gidx * TG
                    emit_sem_proj(tiles_sA, xn, vxn_p, wks_bf, k_loc3, tb_par,
                                  first_w=v_cast["wks"] if gidx == 0 else None)
                    emit_geo_proj(tiles_sA, xn, vxn_p, wkg_bf, k_loc3, tb_par,
                                  first_w=v_cast["wkg"] if gidx == 0 else None)
                    emit_v_proj(tiles_sA, xn, vxn_p, tb_par,
                                first_w=v_cast["wv"] if gidx == 0 else None)
                    emit_sem_proj(tiles_sA, xn2, vxn_o, wks_bf, k_loc3, tb_own)
                    emit_geo_proj(tiles_sA, xn2, vxn_o, wkg_bf, k_loc3, tb_own)
                    emit_v_proj(tiles_sA, xn2, vxn_o, tb_own)
                    emit_sem_proj(tiles_sA, xn2, vxn_o, wqs_bf, q_loc3, tb_own,
                                  first_w=v_cast["wqs"] if gidx == 0 else None)
                    emit_geo_proj(tiles_sA, xn2, vxn_o, wqg_bf, q_loc3, tb_own,
                                  first_w=v_cast["wqg"] if gidx == 0 else None)

        if PHASES >= 3:
            with ExitStack() as scB:
                EB = scB.enter_context
                kh = EB(nc.sbuf_tensor("kh", [128, S], BF16))
                vhb = EB(nc.sbuf_tensor("vhb", [128, 2 * S], BF16))
                qh = EB(nc.sbuf_tensor("qh", [128, TOK], BF16))
                ptb = EB(nc.sbuf_tensor("ptb", [128, 4 * TG], BF16))
                lacc = rw2[:, :]
                linv_bc = EB(nc.sbuf_tensor("linv_bc", [128, TG], F16))
                avs = EB(nc.sbuf_tensor("avs", [128, TG], F16))
                ao = EB(nc.sbuf_tensor("ao", [128, H * TOK], BF16))

                p.use("v", "mA")
                p.use("v", "mB")
                vz1 = p.op("v", lambda e: e.memset(mga[:, :], 0.0), inc=(dve, 1))
                vz2 = p.op("v", lambda e: e.memset(mgb[:, :], 0.0), inc=(dve, 1))
                p.rel("mA", dve, vz1)
                p.rel("mB", dve, vz2)
                v_ks_all = [sm[f"ks{i}"].n for i in range(4)]
                pending = [None]

                def flush_pending():
                    if pending[0] is not None:
                        pending[0]()
                        pending[0] = None

                for h in range(H):
                    hs = h % 2
                    hsem = sm["h0"] if hs == 0 else sm["h1"]
                    kqsem = sm["kq0"] if hs == 0 else sm["kq1"]
                    if h < 2:
                        for i in range(4):
                            p.wait("y", sm[f"ks{i}"], v_ks_all[i])
                    p.use("g", "kh")
                    p.op("g", lambda e, h=h: e.dma_start(
                        out=kh[:, :], in_=k_loc[h * 128:(h + 1) * 128, :]),
                        inc=(kqsem, 16))
                    p.use("y", f"vh{hs}")
                    p.op("y", lambda e, h=h, hs=hs: e.dma_start(
                        out=vhb[:, hs * S:(hs + 1) * S],
                        in_=v_loc3[:, :, h * 128:(h + 1) * 128]), inc=(hsem, 16))
                    p.use("g", "qh")
                    vldq = p.op("g", lambda e, h=h: e.dma_start(
                        out=qh[:, :], in_=q_loc[h * 128:(h + 1) * 128, :]),
                        inc=(kqsem, 16))
                    p.wait("t", kqsem, vldq)
                    p.wait("t", hsem, hsem.n)
                    vhv = vhb[:, hs * S:(hs + 1) * S]
                    v_last_pe_h = None
                    for c in range(4):
                        ab = c % 2
                        pav = ppo[ab]
                        p.use("t", "pA" if ab == 0 else "pC")
                        p.use("t", "pD")
                        strips = ([(0, l) for l in range(4 * c + 4)]
                                  + [(1, l) for l in range(1, 4 * c + 4)]
                                  + [(1, 0)])
                        nst = len(strips)
                        nb = (nst + 1) // 2
                        binfo = []
                        v_lacc = None
                        for k in range(nb + 1):
                            if k < nb:
                                lo = 2 * k
                                hi = min(2 * k + 1, nst - 1)
                                mg = mga if k % 2 == 0 else mgb
                                mkey = "mA" if k % 2 == 0 else "mB"
                                p.use("t", mkey)
                                vs = None
                                for j in range(lo, hi + 1):
                                    sec, l = strips[j]
                                    off = max(l - 4 * c, 0) * 128
                                    w = TG - off
                                    jj = j - lo
                                    vs = p.op("t", lambda e, sec=sec, l=l, off=off,
                                              w=w, jj=jj, mg=mg, c=c: e.matmul(
                                        mg[:, jj * TG + off:jj * TG + off + w],
                                        kh[:, sec * TOK + l * 128:
                                           sec * TOK + (l + 1) * 128],
                                        qh[:, c * TG + off:c * TG + off + w],
                                        start=True, stop=True), inc=(pe, 1))
                                cnt = hi - lo + 1
                                p.use("s", f"ptb{k % 2}")
                                p.use("s", f"ptbV{k % 2}")
                                p.wait("s", pe, vs)
                                va = p.op("s", lambda e, k=k, cnt=cnt, mg=mg: e.activation(
                                    ptb[:, (k % 2) * 2 * TG:
                                        (k % 2) * 2 * TG + cnt * TG],
                                    mg[:, 0:cnt * TG], AF.Exp, scale=SCALE),
                                    inc=(act, 1))
                                p.rel(mkey, act, va)
                                vmask = None
                                for j in range(lo, hi + 1):
                                    sec, l = strips[j]
                                    if l >= 4 * c:
                                        joff = (l - 4 * c) * 128
                                        msk = m0_sb if sec == 0 else m1_sb
                                        base = (k % 2) * 2 * TG + (j - lo) * TG
                                        p.wait("v", act, va)
                                        vmask = p.op("v", lambda e, base=base,
                                                     joff=joff, msk=msk: e.tensor_mul(
                                            ptb[:, base + joff:base + joff + 128],
                                            ptb[:, base + joff:base + joff + 128],
                                            msk[:, :]), inc=(dve, 1))
                                for j in range(lo, hi + 1):
                                    sec, l = strips[j]
                                    joff = max(l - 4 * c, 0) * 128
                                    jw = TG - joff
                                    base = (k % 2) * 2 * TG + (j - lo) * TG
                                    if vmask is not None:
                                        p.wait("v", dve, vmask)
                                    else:
                                        p.wait("v", act, va)
                                    if v_lacc is not None:
                                        p.wait("v", dve, v_lacc)
                                    if j == 0:
                                        v_lacc = p.op("v", lambda e, base=base:
                                                      e.tensor_copy(
                                            lacc[:, :], ptb[:, base:base + TG]),
                                            inc=(dve, 1))
                                    else:
                                        v_lacc = p.op("v", lambda e, base=base,
                                                      joff=joff, jw=jw: e.tensor_add(
                                            lacc[:, joff:joff + jw],
                                            lacc[:, joff:joff + jw],
                                            ptb[:, base + joff:base + joff + jw]),
                                            inc=(dve, 1))
                                p.rel(f"ptbV{k % 2}", dve, v_lacc)
                                binfo.append((k, lo, hi, va, vmask))
                            if k >= 1:
                                if k == 1:
                                    flush_pending()
                                kk, lo, hi, va, vmask = binfo[k - 1]
                                if vmask is not None:
                                    p.wait("t", dve, vmask)
                                else:
                                    p.wait("t", act, va)
                                for j in range(lo, hi + 1):
                                    sec, l = strips[j]
                                    joff = max(l - 4 * c, 0) * 128
                                    jw = TG - joff
                                    base = (kk % 2) * 2 * TG + (j - lo) * TG
                                    vav = p.op("t", lambda e, sec=sec, l=l, joff=joff,
                                               jw=jw, base=base, vhv=vhv, pav=pav,
                                               stf=(j == 0), spf=(j == nst - 1):
                                               e.matmul(
                                        pav[:, joff:joff + jw],
                                        vhv[:, (sec * 16 + l) * 128:
                                            (sec * 16 + l + 1) * 128],
                                        ptb[:, base + joff:base + joff + jw],
                                        start=stf, stop=spf),
                                        inc=(pe, 1))
                                    v_last_pe_h = vav
                                p.rel(f"ptb{kk % 2}", pe, v_last_pe_h)
                        p.wait("t", dve, v_lacc)
                        vlp = p.op("t", lambda e: e.matmul(
                            pD[0:1, 0:TG], ones_f32[:, :], lacc[:, :],
                            start=True, stop=True), inc=(pe, 1))
                        p.use("v", "rowsD")
                        p.wait("v", pe, vlp)
                        vlc = p.op("v", lambda e: e.tensor_copy(
                            rows["l"], pD[0:1, 0:TG]), inc=(dve, 1))
                        p.wait("v", dve, vlc)
                        vli = p.op("v", lambda e: e.reciprocal(rows["li"], rows["l"]),
                                   inc=(dve, 1))
                        p.rel("pD", dve, vli)

                        def mk_pending(vli=vli, v_last=v_last_pe_h, pav=pav,
                                       ab=ab, h=h, c=c):
                            def emit():
                                p.use("t", "pB")
                                p.wait("t", dve, vli)
                                vbc = p.op("t", lambda e: e.matmul(
                                    pB[:, 0:TG], ones_r[:, :], rows["li"],
                                    start=True, stop=True), inc=(pe, 1))
                                p.rel("rowsD", pe, vbc)
                                p.use("s", "linvbc")
                                p.wait("s", pe, vbc)
                                va1 = p.op("s", lambda e: e.activation(
                                    linv_bc[:, :], pB[:, 0:TG], AF.Copy),
                                    inc=(act, 1))
                                p.rel("pB", act, va1)
                                p.use("s", "avs")
                                p.wait("s", pe, v_last)
                                va2 = p.op("s", lambda e, pav=pav: e.activation(
                                    avs[:, :], pav[:, 0:TG], AF.Copy), inc=(act, 1))
                                p.rel("pA" if ab == 0 else "pC", act, va2)
                                p.wait("v", act, va2)
                                vao = p.op("v", lambda e, h=h, c=c: e.tensor_mul(
                                    ao[:, h * TOK + c * TG: h * TOK + (c + 1) * TG],
                                    avs[:, :], linv_bc[:, :]), inc=(dve, 1))
                                p.rel("linvbc", dve, vao)
                                p.rel("avs", dve, vao)
                            return emit
                        pending[0] = mk_pending()
                    p.rel("kh", pe, v_last_pe_h)
                    p.rel(f"vh{hs}", pe, v_last_pe_h)
                    p.rel("qh", pe, v_last_pe_h)
                flush_pending()
                v_ao_done = dve.n

                v_res = None
                for ob in range(NDC):
                    sl = ob % 2
                    wsem, vw = load_w(sl,
                                      wo_bf[ob].rearrange("(ac p) o -> p ac o", p=128),
                                      NDC * 128,
                                      v_cast["wo"] if ob < 2 else None)
                    p.wait("t", wsem, vw)
                    if ob == 0:
                        p.wait("t", dve, v_ao_done)
                    vmm = None
                    for qc in range(4):
                        bp = qc % 2
                        po = ppo[bp]
                        p.use("t", "pA" if bp == 0 else "pC")
                        for adc in range(NDC):
                            vmm = p.op("t", lambda e, sl=sl, adc=adc, qc=qc, po=po:
                                       e.matmul(
                                po[:, 0:TG],
                                wb[sl][:, adc * 128:(adc + 1) * 128],
                                ao[:, adc * TOK + qc * TG: adc * TOK + (qc + 1) * TG],
                                start=(adc == 0), stop=(adc == NDC - 1)),
                                inc=(pe, 1) if adc == NDC - 1 else None)
                        p.use("s", "t3")
                        p.wait("s", pe, vmm)
                        va = p.op("s", lambda e, po=po: e.activation(
                            t3b[:, :], po[:, 0:TG], AF.Copy), inc=(act, 1))
                        p.rel("pA" if bp == 0 else "pC", act, va)
                        p.wait("v", act, va)
                        if v_res is not None:
                            p.wait("v", dve, v_res)
                        v_res = p.op("v", lambda e, ob=ob, qc=qc: e.tensor_add(
                            x_f[:, ob * TOK + qc * TG: ob * TOK + (qc + 1) * TG],
                            x_f[:, ob * TOK + qc * TG: ob * TOK + (qc + 1) * TG],
                            t3b[:, :]), inc=(dve, 1))
                        p.rel("t3", dve, v_res)
                    p.rel(f"wb{sl}", pe, vmm)
                v_wo_done = v_res

        if PHASES >= 4:
            with ExitStack() as scC:
                EC = scC.enter_context
                tiles_sC = {
                    "t1b": EC(nc.sbuf_tensor("t1c", [128, 2 * TG], BF16)),
                    "t2b": EC(nc.sbuf_tensor("t2c", [128, 2 * TG], BF16)),
                    "hdn": EC(nc.sbuf_tensor("hdnc", [128, NIC * TG], BF16)),
                }
                xn3 = EC(nc.sbuf_tensor("xn3", [128, NDC * TG], BF16))
                xn4 = EC(nc.sbuf_tensor("xn4", [128, NDC * TG], BF16))
                lnb.clear()
                lnb["sqb"] = EC(nc.sbuf_tensor("sqbC", [128, TG], F16))
                lnb["rstd_bc"] = EC(nc.sbuf_tensor("rstdbcC", [128, TG], F16))
                lnb["mr_bc"] = EC(nc.sbuf_tensor("mrbcC", [128, TG], F16))
                lnb["xtmp"] = rw3[:, :]
                xbufs4 = [(xn3, "xnA"), (xn4, "xnB")]
                v_ln4 = {0: emit_ln(xf_chunk(0), xn3, xn_key="xnA",
                                    x_ready=(dve, v_wo_done)
                                    if v_wo_done is not None else None)}
                def store_ob(ob, v_res):
                    p.wait("g", dve, v_res)
                    p.op("g", lambda e, ob=ob: e.dma_start(
                        out=x_out[ob * 128:(ob + 1) * 128, :],
                        in_=x_f[:, ob * TOK:(ob + 1) * TOK]), inc=(sm["out"], 16))

                for gidx in range(4):
                    xc = xf_chunk(gidx)
                    buf, key = xbufs4[gidx % 2]
                    vh = emit_ffn_A(buf, v_ln4[gidx], tiles_sC, xn_key=key)
                    if gidx + 1 < 4:
                        nbuf, nkey = xbufs4[(gidx + 1) % 2]
                        v_ln4[gidx + 1] = emit_ln(xf_chunk(gidx + 1), nbuf,
                                                  xn_key=nkey)
                    emit_ffn_B(vh, lambda oc, xc=xc: xc(oc), tiles_sC,
                               after_ob=store_ob if gidx == 3 else None)

        if PHASES < 4:
            v_final = dve.n
            p.wait("g", dve, v_final)
            for dc in range(NDC):
                p.op("g", lambda e, dc=dc: e.dma_start(
                    out=x_out[dc * 128:(dc + 1) * 128, :],
                    in_=x_f[:, dc * TOK:(dc + 1) * TOK]), inc=(sm["out"], 16))
        p.wait("g", sm["out"], sm["out"].n)

        with nc.Block() as block:
            @block.tensor
            def _(e):
                for fn in p.cmds["t"]:
                    fn(e)

            @block.vector
            def _(e):
                for fn in p.cmds["v"]:
                    fn(e)

            @block.scalar
            def _(e):
                for fn in p.cmds["s"]:
                    fn(e)

            @block.gpsimd
            def _(e):
                for fn in p.cmds["g"]:
                    fn(e)

            @block.sync
            def _(e):
                for fn in p.cmds["y"]:
                    fn(e)

    return nc


def _geo_perm():
    perm = np.zeros(1024, np.int64)
    for pos in range(1024):
        if pos < 512:
            h, j = pos // 32, pos % 32
            perm[pos] = h * 64 + j
        else:
            h, j = (pos - 512) // 32, (pos - 512) % 32
            perm[pos] = h * 64 + 32 + j
    return perm


def make_in_maps(L0_attn_x, Wq_sem, Wk_sem, Wq_geo, Wk_geo, Wv, Wo,
                 gate_w, up_w, down_w, gate_s, up_s, down_s):
    f32 = np.float32
    x = np.asarray(L0_attn_x, dtype=f32)
    perm = _geo_perm()
    shared = {
        "wqs_t": np.ascontiguousarray(np.asarray(Wq_sem, f32).T),
        "wks_t": np.ascontiguousarray(np.asarray(Wk_sem, f32).T),
        "wqg_t": np.ascontiguousarray(np.asarray(Wq_geo, f32).T[:, perm]),
        "wkg_t": np.ascontiguousarray(np.asarray(Wk_geo, f32).T[:, perm]),
        "wv_t": np.ascontiguousarray(np.asarray(Wv, f32).T),
        "wo_t": np.ascontiguousarray(np.asarray(Wo, f32).T),
        "gate_t": np.ascontiguousarray(np.asarray(gate_w, f32).T),
        "up_t": np.ascontiguousarray(np.asarray(up_w, f32).T),
        "down_t": np.ascontiguousarray(np.asarray(down_w, f32).T),
        "gs_in": np.ascontiguousarray(np.asarray(gate_s, f32).reshape(NIC, 128).T),
        "us_in": np.ascontiguousarray(np.asarray(up_s, f32).reshape(NIC, 128).T),
        "ds_in": np.ascontiguousarray(np.asarray(down_s, f32).reshape(NDC, 128).T),
    }
    tri = (np.arange(128)[:, None] <= np.arange(128)[None, :]).astype(f32)
    in_maps = []
    for r in range(8):
        b, par = r // 2, r % 2
        own = np.arange(par, 32, 2)
        partn = np.arange(1 - par, 32, 2)
        xs = x[b].reshape(32, 128, D)
        x_core = np.concatenate([xs[own], xs[partn]], axis=0).reshape(S, D)
        x_core = np.ascontiguousarray(x_core.T)
        tiles = np.concatenate([own, partn])
        pos = (tiles[:, None] * 128 + np.arange(128)[None, :]).reshape(-1)
        inv = 1.0 / (10000.0 ** (np.arange(0, 64, 2, dtype=np.float64) / 64.0))
        fr = pos[None, :].astype(np.float64) * inv[:, None]
        cos64 = np.tile(np.cos(fr), (4, 1))
        sin64 = np.tile(np.sin(fr), (4, 1))
        cos128 = cos64.astype(f32)
        sin128 = sin64.astype(f32)
        cosl = (cos64 - cos128).astype(f32)
        sinl = (sin64 - sin128).astype(f32)
        m1 = np.zeros((128, 128), f32) if par == 0 else np.ones((128, 128), f32)
        d = dict(shared)
        d.update({
            "x_in": x_core,
            "cos_in": np.ascontiguousarray(cos128),
            "sin_in": np.ascontiguousarray(sin128),
            "cosl_in": np.ascontiguousarray(cosl),
            "sinl_in": np.ascontiguousarray(sinl),
            "m0_in": tri, "m1_in": m1,
        })
        in_maps.append(d)
    return in_maps


def assemble_output(results):
    out = np.zeros((4, S, D), np.float32)
    for r in range(8):
        b, par = r // 2, r % 2
        own = np.arange(par, 32, 2)
        xo = np.asarray(results[r]["x_out"], dtype=np.float32)
        out[b].reshape(32, 128, D)[own] = xo.T.reshape(16, 128, D)
    return out


def kernel(L0_attn_x, L0_attn_Wq_sem, L0_attn_Wk_sem, L0_attn_Wq_geo,
           L0_attn_Wk_geo, L0_attn_Wv, L0_attn_Wo, L0_ffn_gate_w,
           L0_ffn_up_w, L0_ffn_down_w, L0_ffn_gate_s, L0_ffn_up_s,
           L0_ffn_down_s):
    in_maps = make_in_maps(L0_attn_x, L0_attn_Wq_sem, L0_attn_Wk_sem,
                           L0_attn_Wq_geo, L0_attn_Wk_geo, L0_attn_Wv,
                           L0_attn_Wo, L0_ffn_gate_w, L0_ffn_up_w,
                           L0_ffn_down_w, L0_ffn_gate_s, L0_ffn_up_s,
                           L0_ffn_down_s)
    nc = build_program()
    res = run_bass_kernel_spmd(nc, in_maps, core_ids=list(range(8)))
    return assemble_output(res.results)
